# version 03
# baseline (speedup 1.0000x reference)
import os
import contextlib
import numpy as np
import ml_dtypes
import concourse.bass as bass
import concourse.mybir as mybir
from concourse.bass_utils import run_bass_kernel_spmd
from concourse.aot_env import allow_ici_all_to_all

F32 = mybir.dt.float32
BF16 = mybir.dt.bfloat16
I32 = mybir.dt.int32
AF = mybir.ActivationFunctionType
ALU = mybir.AluOpType
AX = mybir.AxisListType
Pool = mybir.EngineType.Pool

bnp16 = ml_dtypes.bfloat16

B, S, E = 4, 4096, 1024
TOK = 2048
NST = 128
HID = 3072
EXT = 2176
NS_ITERS = 25

N_PHASES = int(os.environ.get("KPHASES", "99"))
DBG = [d for d in os.environ.get("KDBG", "").split(",") if d]
SCAN_T = int(os.environ.get("KSCANT", str(S)))
AR_KB = int(os.environ.get("KARENA", "180"))

PHASE_NAMES = ["prep", "ns", "qkv", "kquant", "vquant", "mhat", "attn",
               "l0tail", "l1head", "scan", "ffn1", "tail"]


class Ctr:
    def __init__(self):
        self.v = 0

    def inc(self, k=1):
        self.v += k
        return self.v


class Arena:
    def __init__(self, tensor, nbytes):
        self.t = tensor
        self.regions = {}
        self.free = [(0, nbytes)]

    def alloc(self, name, nbytes):
        nbytes = (nbytes + 3) & ~3
        for i, (o, sz) in enumerate(self.free):
            if sz >= nbytes:
                self.regions[name] = (o, nbytes)
                if sz == nbytes:
                    self.free.pop(i)
                else:
                    self.free[i] = (o + nbytes, sz - nbytes)
                return
        raise RuntimeError(f"arena OOM for {name} ({nbytes}B); free={self.free}")

    def release(self, *names):
        for name in names:
            o, sz = self.regions.pop(name)
            self.free.append((o, sz))
        self.free.sort()
        out = []
        for seg in self.free:
            if out and out[-1][0] + out[-1][1] == seg[0]:
                out[-1] = (out[-1][0], out[-1][1] + seg[1])
            else:
                out.append(list(seg))
        self.free = [tuple(s) for s in out]

    def bf(self, name, off_el=0, cols=None):
        o, sz = self.regions[name]
        n = (sz // 2 - off_el) if cols is None else cols
        return self.t[:, o // 2 + off_el: o // 2 + off_el + n]

    def f32(self, name, off_el=0, cols=None):
        o, sz = self.regions[name]
        ap = self.t[:, o // 2: o // 2 + sz // 2].bitcast(F32)
        n = (sz // 4 - off_el) if cols is None else cols
        return ap[:, off_el: off_el + n]

    def i32(self, name, off_el=0, cols=None):
        return self.f32(name, off_el, cols).bitcast(I32)


def build_nc():
    nc = bass.Bass("TRN2", target_bir_lowering=False)
    stack = contextlib.ExitStack()
    ctx = {"nc": nc}

    def inp(name, shape, dt=F32):
        t = nc.dram_tensor(name, shape, dt, kind="ExternalInput")
        ctx[name] = t.ap()
        ctx[name + "_h"] = t

    inp("x_b", [S, E]); inp("x_bh", [S, E], BF16)
    inp("x_own", [TOK, E])
    inp("wattn", [E, 1536]); inp("battn_row", [1, 1536], BF16)
    inp("wproj", [E, E]); inp("bproj_row", [1, E], BF16)
    inp("wrT", [E, E]); inp("wkT", [E, E]); inp("wvT", [E, E])
    inp("mur", [128, 8]); inp("muk", [128, 8])
    inp("wiT", [E, 256]); inp("bi_row", [1, 256], BF16)
    inp("woT", [256, E]); inp("bo_row", [1, E], BF16)
    inp("gam_row", [1, E], BF16); inp("bet_row", [1, E], BF16)
    inp("wgT", [E, HID], I32); inp("wuT", [E, HID], I32); inp("wdT", [HID, E], I32)
    inp("a_hip", [NST, NST]); inp("bp_g", [NST, 1]); inp("cv_g", [NST, 1])
    inp("ldt_g", [1, 1]); inp("sc_g", [1, 1])
    inp("id32", [128, 128]); inp("eye2", [128, 128])
    inp("ones_col", [128, 1]); inp("ones_colb", [128, 1], BF16)
    inp("ones_rowb", [1, 512], BF16)
    inp("masks", [128, 4 * 512], BF16)
    inp("hsel", [2, 128])
    inp("ones_rowf", [1, 128])
    inp("off_xel", [1, 1], I32)
    inp("off_y0", [1, 1], I32)
    inp("off_uel", [1, 1], I32)

    ctx["out_x2"] = nc.dram_tensor("out_x2", [TOK, E], F32, kind="ExternalOutput").ap()
    ctx["dbg"] = {}
    dbg_specs = {
        "xn0": ([4096, E], BF16), "qf": ([128, 4 * S], BF16), "kf": ([128, 4 * S], BF16),
        "vaug": ([128, 32 * 520], BF16), "negm": ([128, 1], F32),
        "y0": ([512, S], BF16), "x1": ([128, 8 * TOK], F32), "xn1": ([128, 8 * TOK], BF16),
        "adT": ([128, 128], BF16), "brow": ([1, 128], BF16), "uown": ([128, 16 * 256], BF16),
        "yfm": ([128, S], BF16), "res": ([128, 8 * TOK], F32), "xn0e": ([128, 8 * EXT], BF16),
        "st0dbg": ([128, 128], BF16), "xt32dbg": ([128, 2048], F32), "tmpAdbg": ([128, 2048], F32),
        "svdbg": ([128, 16], F32), "srowdbg": ([1, 37], F32),
    }
    for nm in DBG:
        shp, dt = dbg_specs[nm]
        ctx["dbg"][nm] = nc.dram_tensor("dbg_" + nm, shp, dt, kind="ExternalOutput").ap()

    for nm, shp, dt in [("xn0_tm", [4352, E], BF16), ("xn0_own_tm", [EXT, E], BF16),
                        ("u_g", [S, 128], BF16),
                        ("y0_in", [512, S], BF16), ("y0_out", [1024, S], BF16),
                        ("u_in", [TOK, 256], BF16), ("u_out", [S, 256], BF16),
                        ("ys_in", [NST, S], BF16), ("ys_out", [2 * NST, S], BF16)]:
        ctx[nm] = nc.dram_tensor(nm, shp, dt)

    ctx["sems"] = {}
    ctx["ctrs"] = {}
    for name in ["d0", "d1", "d2", "d3", "d4", "d5", "p0", "v0", "s0", "g0", "cc"]:
        ctx["sems"][name] = stack.enter_context(nc.semaphore(name))
        ctx["ctrs"][name] = Ctr()

    def sbuf(name, shape, dt=F32):
        t = stack.enter_context(nc.sbuf_tensor(name, shape, dt))
        ctx[name] = t
        return t

    sbuf("cst", [128, 256], F32)
    sbuf("onec", [128, 1], F32)
    sbuf("onecb", [128, 1], BF16)
    sbuf("onerb", [1, 512], BF16)
    sbuf("sv", [128, 32], F32)
    sbuf("srow", [1, 64], F32)
    sbuf("biasrows", [1, 4096], BF16)
    sbuf("gbrow", [1, 2048], BF16)
    sbuf("adT", [128, 128], BF16)
    sbuf("brow", [1, 128], BF16)
    sbuf("ccol", [128, 1], BF16)
    sbuf("ss0", [128, 128], BF16)
    sbuf("ss1", [128, 128], BF16)
    sbuf("wsa", [128, 1024], BF16)
    sbuf("wsb", [128, 1024], BF16)
    sbuf("eringb", [128, 2 * 512], BF16)
    sbuf("offs", [1, 4], I32)
    sbuf("hselt", [2, 128])
    sbuf("onerf", [1, 128])
    sbuf("mixmu", [128, 16])

    arena_t = stack.enter_context(nc.sbuf_tensor("arena", [128, AR_KB * 512], BF16))
    ctx["ar"] = Arena(arena_t, AR_KB * 1024)
    ctx["arena_t"] = arena_t

    ctx["pb"] = [stack.enter_context(nc.psum_tensor(f"pb{i}", [128, 512], F32))
                 for i in range(8)]

    with stack:
        phases = [ph_prep, ph_ns, ph_qkv, ph_kquant, ph_vquant, ph_mhat, ph_attn,
                  ph_l0tail, ph_l1head, ph_scan, ph_ffn1, ph_tail]
        for i, ph in enumerate(phases):
            if i < N_PHASES:
                ph(ctx)
        ph_debug(ctx)
    return nc


def _sem(ctx, n):
    return ctx["sems"][n], ctx["ctrs"][n]


def ph_prep(ctx):
    KPREP = int(os.environ.get("KPREP", "4"))
    KNT = int(os.environ.get("KNT", "32"))
    nc = ctx["nc"]
    ar = ctx["ar"]
    d0, cd0 = _sem(ctx, "d0")
    d1, cd1 = _sem(ctx, "d1")
    d2, cd2 = _sem(ctx, "d2")
    v0, cv0 = _sem(ctx, "v0")
    s0, cs0 = _sem(ctx, "s0")
    g0, cg0 = _sem(ctx, "g0")
    NT = KNT
    ar.alloc("maskt", 4 * 512 * 2)
    ar.alloc("xn0full", 8 * S * 2)
    ar.alloc("tmpA", 8 * 1024)
    ar.alloc("preploc", (2 * E) * 4 + (2 * E) * 2 + (2 * E) * 2 + E * 2)

    d3, cd3 = _sem(ctx, "d3")
    d4, cd4 = _sem(ctx, "d4")
    d5, cd5 = _sem(ctx, "d5")
    v0b, s0b = cv0.v, cs0.v
    d4b, d5b = cd4.v, cd5.v
    const_list = [
        ("id32", ctx["cst"][:, 0:128]), ("eye2", ctx["cst"][:, 128:256]),
        ("ones_col", ctx["onec"][:]), ("ones_colb", ctx["onecb"][:]),
        ("ones_rowb", ctx["onerb"][:]), ("masks", ar.bf("maskt", 0, 4 * 512)),
        ("bp_g", ctx["sv"][:, 0:1]), ("cv_g", ctx["sv"][:, 1:2]),
        ("ldt_g", ctx["sv"][0:1, 2:3]), ("sc_g", ctx["sv"][0:1, 3:4]),
        ("battn_row", ctx["biasrows"][0:1, 0:1536]),
        ("bproj_row", ctx["biasrows"][0:1, 1536:2560]),
        ("bo_row", ctx["biasrows"][0:1, 2560:3584]),
        ("bi_row", ctx["biasrows"][0:1, 3584:3840]),
        ("gam_row", ctx["gbrow"][0:1, 0:1024]),
        ("bet_row", ctx["gbrow"][0:1, 1024:2048]),
        ("hsel", ctx["hselt"][:]),
        ("ones_rowf", ctx["onerf"][:]),
        ("mur", ctx["mixmu"][:, 0:8]),
        ("muk", ctx["mixmu"][:, 8:16]),
        ("off_xel", ctx["offs"][0:1, 0:1]),
        ("off_y0", ctx["offs"][0:1, 1:2]),
        ("off_uel", ctx["offs"][0:1, 2:3]),
    ]
    d0b = cd0.v + len(const_list) * 16
    d1b = cd1.v
    xt32 = ar.f32("preploc", 0, 2 * E)
    xtb = ar.bf("preploc", 4 * E, 2 * E)
    xnt = ar.bf("preploc", 6 * E, 2 * E)
    zrow = ar.bf("preploc", 8 * E, E)[0:1, :]
    sqd = ar.f32("tmpA", 0, 2 * E)
    with (
        nc.sbuf_tensor("st0", [128, 16], F32) as st0,
        nc.sbuf_tensor("st1", [128, 2], F32) as st1,
        nc.Block() as blk,
    ):
        @blk.sync
        def _(sy):
            for src, dst in const_list:
                sy.dma_start(out=dst, in_=ctx[src]).then_inc(d0, 16); cd0.inc(16)
            for i in range(NT):
                if i >= 2:
                    sy.wait_ge(v0, v0b + 2 * (i - 1))
                    sy.wait_ge(s0, s0b + 2 * (i - 2) + 1)
                dsem = d0 if i % 2 == 0 else d1
                sy.dma_start(out=xt32[:, (i % 2) * E:(i % 2 + 1) * E],
                             in_=ctx["x_b"][i * 128:(i + 1) * 128, :]).then_inc(dsem, 16)
                sy.dma_start(out=xtb[:, (i % 2) * E:(i % 2 + 1) * E],
                             in_=ctx["x_bh"][i * 128:(i + 1) * 128, :]).then_inc(dsem, 16)
                (cd0 if i % 2 == 0 else cd1).inc(32)

        @blk.scalar
        def _(sc):
            if KPREP < 2:
                return
            for i in range(NT):
                sl = slice((i % 2) * E, (i % 2 + 1) * E)
                o = (i % 2) * 8
                sc.wait_ge(d0 if i % 2 == 0 else d1,
                           (d0b if i % 2 == 0 else d1b) + 32 * (i // 2 + 1))
                if i >= 2:
                    sc.wait_ge(v0, v0b + 2 * (i - 1) - 1)
                sc.activation(out=sqd[:, (i % 2) * E:(i % 2 + 1) * E], in_=xt32[:, sl], func=AF.Square).then_inc(s0, 1)
                sc.wait_ge(v0, v0b + 2 * i + 1)
                sc.activation(out=st1[:, (i % 2):(i % 2) + 1], in_=st0[:, 3 + o:4 + o], func=AF.Sqrt).then_inc(s0, 1)
            cs0.inc(2 * NT)

        @blk.vector
        def _(v):
            if KPREP < 2:
                v.drain(fusable=False).then_inc(g0, 1)
                return
            v.memset(zrow[:], 0.0)
            v.memset(st0[:], 0.0)
            v.memset(ctx["ss1"][:], 0.0)
            v.memset(ctx["ss0"][:], 0.0).then_inc(g0, 1)
            v.drain()
            for i in range(NT):
                sl = slice((i % 2) * E, (i % 2 + 1) * E)
                o = (i % 2) * 8
                v.wait_ge(s0, s0b + 2 * i + 1)
                v.wait_ge(d0 if i % 2 == 0 else d1,
                          (d0b if i % 2 == 0 else d1b) + 32 * (i // 2 + 1))
                v.tensor_reduce(out=st0[:, 1 + o:2 + o], in_=sqd[:, (i % 2) * E:(i % 2 + 1) * E],
                                axis=AX.X, op=ALU.add)
                v.tensor_reduce(out=st0[:, 0 + o:1 + o], in_=xt32[:, sl], axis=AX.X, op=ALU.add)
                v.drain()
                v.tensor_scalar(out=st0[:, 4 + o:5 + o], in0=st0[:, 0 + o:1 + o],
                                scalar1=1.0 / E, scalar2=None, op0=ALU.mult)
                v.tensor_scalar(out=st0[:, 2 + o:3 + o], in0=st0[:, 1 + o:2 + o],
                                scalar1=1.0 / E, scalar2=None, op0=ALU.mult)
                v.drain()
                v.tensor_copy(out=st0[:, 0 + o:1 + o], in_=st0[:, 4 + o:5 + o])
                v.tensor_tensor(out=st0[:, 7 + o:8 + o], in0=st0[:, 4 + o:5 + o],
                                in1=st0[:, 4 + o:5 + o], op=ALU.mult)
                v.drain()
                v.tensor_tensor(out=st0[:, 3 + o:4 + o], in0=st0[:, 2 + o:3 + o],
                                in1=st0[:, 7 + o:8 + o], op=ALU.subtract)
                v.drain()
                v.tensor_scalar(out=st0[:, 3 + o:4 + o], in0=st0[:, 3 + o:4 + o],
                                scalar1=1e-5, scalar2=None, op0=ALU.add).then_inc(v0, 1)
                v.wait_ge(s0, s0b + 2 * i + 2)
                v.reciprocal(out=st0[:, 5 + o:6 + o], in_=st1[:, (i % 2):(i % 2) + 1])
                v.drain()
                v.tensor_tensor(out=st0[:, 6 + o:7 + o], in0=st0[:, 0 + o:1 + o],
                                in1=st0[:, 5 + o:6 + o], op=ALU.mult)
                v.drain()
                v.tensor_scalar(out=st0[:, 6 + o:7 + o], in0=st0[:, 6 + o:7 + o],
                                scalar1=-1.0, scalar2=None, op0=ALU.mult)
                v.drain()
                if i >= 2:
                    j = i - 2
                    if j % 2 == 0:
                        v.wait_ge(d4, d4b + 32 + 16 * (j // 2 + 1))
                    else:
                        v.wait_ge(d5, d5b + 16 * (j // 2 + 1))
                v.tensor_tensor(out=xnt[:, sl], in0=xt32[:, sl],
                                in1=st0[:, 5 + o:6 + o].broadcast_to([128, E]), op=ALU.mult)
                v.drain()
                if i == 0:
                    v.tensor_copy(out=ctx["ss0"][:, 0:16], in_=st0[:, 0:16])
                    v.drain()
                if i == NT - 1 and os.environ.get("KRERED"):
                    v.tensor_reduce(out=st0[:, 8:9], in_=xt32[:, 0:E], axis=AX.X, op=ALU.add)
                    v.tensor_reduce(out=st0[:, 9:10], in_=sqd[:, 0:E], axis=AX.X, op=ALU.add)
                    v.drain()
                    v.tensor_copy(out=ctx["ss0"][:, 16:32], in_=st0[:, 0:16])
                    v.tensor_copy(out=ctx["ss0"][:, 32:34], in_=st0[:, 8:10])
                    v.drain()
                v.tensor_tensor(out=xnt[:, sl], in0=xnt[:, sl],
                                in1=st0[:, 6 + o:7 + o].broadcast_to([128, E]),
                                op=ALU.add).then_inc(v0, 1)
            cv0.inc(2 * NT)
        cg0.inc()

        @blk.gpsimd
        def _(gp):
            if KPREP < 3:
                return
            gp.wait_ge(g0, cg0.v)
            gp.dma_start(out=ctx["xn0_tm"].ap()[0:1, :], in_=zrow[:]).then_inc(d4, 16); cd4.inc(16)
            gp.dma_start(out=bass.AP(ctx["xn0_tm"], 4097 * E, [[0, 1], [E, 255], [1, E]]),
                         in_=zrow.unsqueeze(1).broadcast_to([1, 255, E])
                         ).then_inc(d4, 16); cd4.inc(16)
            for i in range(NT):
                sl = slice((i % 2) * E, (i % 2 + 1) * E)
                gp.wait_ge(v0, v0b + 2 * i + 2)
                gp.dma_start(out=ctx["xn0_tm"].ap()[1 + i * 128:1 + (i + 1) * 128, :],
                             in_=xnt[:, sl]).then_inc(d4 if i % 2 == 0 else d5, 16)
                (cd4 if i % 2 == 0 else cd5).inc(16)

    if KPREP < 4:
        return
    with nc.Block() as blk:
        @blk.gpsimd
        def _(gp):
            gp.wait_ge(d0, cd0.v)
            gp.wait_ge(d4, cd4.v)
            gp.wait_ge(d5, cd5.v)
            rgA = gp.alloc_register()
            gp.reg_load(rgA, ctx["offs"][0:1, 0:1])
            gp.dma_start(out=ctx["xn0_own_tm"].ap(),
                         in_=bass.AP(ctx["xn0_tm"], rgA, [[E, EXT], [1, E]])
                         ).then_inc(d2, 16)
            cd2.inc(16)

        @blk.sync
        def _(sy):
            sy.wait_ge(d2, cd2.v)
            for f in range(8):
                sy.dma_start_transpose(ar.bf("xn0full", f * S, S),
                                       ctx["xn0_tm"].ap()[1:4097, f * 128:(f + 1) * 128]
                                       ).then_inc(d0, 16)
                cd0.inc(16)

        @blk.scalar
        def _(sc):
            sc.wait_ge(d0, cd0.v)

    ar.release("preploc")


def ph_ns(ctx):
    nc = ctx["nc"]
    pb = ctx["pb"]
    p0, cp0 = _sem(ctx, "p0")
    v0, cv0 = _sem(ctx, "v0")
    s0, cs0 = _sem(ctx, "s0")
    g0, cg0 = _sem(ctx, "g0")
    d0, cd0 = _sem(ctx, "d0")
    ID = ctx["cst"][:, 0:128]
    EY2 = ctx["cst"][:, 128:256]
    sv = ctx["sv"]
    srow = ctx["srow"]

    with (
        nc.sbuf_tensor("ns", [128, 8 * 128], F32) as ns,
        nc.Block() as blk,
    ):
        C = lambda k: ns[:, k * 128:(k + 1) * 128]

        @blk.sync
        def _(sy):
            sy.dma_start(out=C(7), in_=ctx["a_hip"]).then_inc(d0, 16); cd0.inc(16)

        @blk.scalar
        def _(sc):
            sc.wait_ge(d0, cd0.v)
            sc.activation(out=sv[0:1, 4:5], in_=sv[0:1, 2:3], func=AF.Exp).then_inc(s0, 1)
            cs0.inc()

        def pe(fn):
            @blk.tensor
            def _(te):
                fn(te)

        def ve(fn):
            @blk.vector
            def _(v):
                fn(v)

        def do_v(fn):
            def g(v):
                fn(v)
                v.drain(fusable=False).then_inc(v0, 1)
            ve(g); cv0.inc()

        def do_p(fn):
            def g(te):
                fn(te)
                te.drain(fusable=False).then_inc(p0, 1)
            pe(g); cp0.inc()

        do_p(lambda te: (te.wait_ge(s0, cs0.v),
                         te.matmul(pb[1][:, 0:2], ctx["onerf"][:], sv[0:1, 3:5],
                                   start=True, stop=True)))
        do_v(lambda v: (v.wait_ge(p0, cp0.v),
                        v.tensor_copy(out=sv[:, 5:6], in_=pb[1][:, 1:2]),
                        v.tensor_copy(out=sv[:, 6:7], in_=pb[1][:, 0:1])))
        do_v(lambda v: (v.tensor_tensor(out=C(0), in0=C(7), in1=sv[:, 6:7].broadcast_to([128, 128]), op=ALU.mult),
                        v.tensor_scalar(out=sv[:, 7:8], in0=sv[:, 5:6], scalar1=0.5, scalar2=None, op0=ALU.mult),
                        v.drain(),
                        v.tensor_tensor(out=C(0), in0=C(0), in1=sv[:, 7:8].broadcast_to([128, 128]), op=ALU.mult)))
        do_p(lambda te: (te.wait_ge(v0, cv0.v), te.transpose(pb[0][:, 0:128], C(0), ID)))
        do_v(lambda v: (v.wait_ge(p0, cp0.v),
                        v.tensor_tensor(out=C(1), in0=ID, in1=C(0), op=ALU.add),
                        v.tensor_tensor(out=C(2), in0=ID, in1=pb[0][:, 0:128], op=ALU.add),
                        v.tensor_tensor(out=C(3), in0=ID, in1=C(0), op=ALU.subtract)))

        @blk.scalar
        def _(sc):
            sc.wait_ge(v0, cv0.v)
            sc.activation(out=C(7), in_=C(1), func=AF.Abs).then_inc(s0, 1)
            cs0.inc()

        do_p(lambda te: (te.wait_ge(s0, cs0.v),
                         te.matmul(pb[1][0:1, 0:128], ctx["onec"][:], C(7), start=True, stop=True),
                         te.transpose(pb[0][:, 128:256], C(7), ID)))
        do_v(lambda v: (v.wait_ge(p0, cp0.v), v.tensor_copy(out=C(7), in_=pb[0][:, 128:256])))
        do_p(lambda te: (te.wait_ge(v0, cv0.v),
                         te.matmul(pb[1][0:1, 128:256], ctx["onec"][:], C(7), start=True, stop=True)))
        do_v(lambda v: (v.wait_ge(p0, cp0.v),
                        v.tensor_reduce(out=srow[0:1, 0:1], in_=pb[1][0:1, 0:128], axis=AX.X, op=ALU.max),
                        v.tensor_reduce(out=srow[0:1, 1:2], in_=pb[1][0:1, 128:256], axis=AX.X, op=ALU.max),
                        v.drain(),
                        v.tensor_tensor(out=srow[0:1, 2:3], in0=srow[0:1, 0:1], in1=srow[0:1, 1:2], op=ALU.mult),
                        v.drain(),
                        v.reciprocal(out=srow[0:1, 3:4], in_=srow[0:1, 2:3])))

        do_p(lambda te: (te.wait_ge(v0, cv0.v),
                         te.matmul(pb[1][:, 2:3], ctx["onerf"][:], srow[0:1, 3:4],
                                   start=True, stop=True)))
        do_v(lambda v: (v.wait_ge(p0, cp0.v),
                        v.tensor_copy(out=sv[:, 8:9], in_=pb[1][:, 2:3])))
        do_v(lambda v: (v.tensor_tensor(out=C(4), in0=C(2), in1=sv[:, 8:9].broadcast_to([128, 128]), op=ALU.mult),
                        v.tensor_tensor(out=C(5), in0=C(1), in1=sv[:, 8:9].broadcast_to([128, 128]), op=ALU.mult)))

        for it in range(NS_ITERS):
            do_p(lambda te: (te.wait_ge(v0, cv0.v),
                             te.matmul(pb[2][:, 0:128], C(2), C(4), start=True, stop=True)))
            do_v(lambda v: (v.wait_ge(p0, cp0.v),
                            v.tensor_tensor(out=C(6), in0=EY2, in1=pb[2][:, 0:128], op=ALU.subtract)))
            do_p(lambda te: (te.wait_ge(v0, cv0.v),
                             te.matmul(pb[2][:, 128:256], C(6), C(5), start=True, stop=True)))
            do_v(lambda v: (v.wait_ge(p0, cp0.v), v.tensor_copy(out=C(5), in_=pb[2][:, 128:256])))
            do_p(lambda te: (te.wait_ge(v0, cv0.v),
                             te.transpose(pb[2][:, 256:384], C(5), ID)))
            do_v(lambda v: (v.wait_ge(p0, cp0.v), v.tensor_copy(out=C(4), in_=pb[2][:, 256:384])))

        do_v(lambda v: v.tensor_tensor(out=sv[:, 9:10], in0=sv[:, 0:1], in1=sv[:, 5:6], op=ALU.mult))
        do_p(lambda te: (te.wait_ge(v0, cv0.v),
                         te.matmul(pb[3][:, 0:128], C(3), C(5), start=True, stop=True),
                         te.matmul(pb[3][:, 128:129], C(5), sv[:, 9:10], start=True, stop=True)))
        do_v(lambda v: (v.wait_ge(p0, cp0.v),
                        v.tensor_copy(out=ctx["adT"][:], in_=pb[3][:, 0:128]),
                        v.tensor_copy(out=sv[:, 10:11], in_=pb[3][:, 128:129]),
                        v.tensor_copy(out=ctx["ccol"][:], in_=sv[:, 1:2])))
        do_p(lambda te: (te.wait_ge(v0, cv0.v),
                         te.transpose(pb[3][0:1, 256:384], sv[:, 10:11], ID)))
        do_v(lambda v: (v.wait_ge(p0, cp0.v),
                        v.tensor_copy(out=ctx["brow"][:], in_=pb[3][0:1, 256:384])))


def ph_qkv(ctx):
    nc = ctx["nc"]
    ar = ctx["ar"]
    ar.alloc("qf", 4 * S * 2)
    ar.alloc("kf", 4 * S * 2)
    ar.alloc("vaug", 32 * 520 * 2)
    pb = ctx["pb"]
    d0, cd0 = _sem(ctx, "d0")
    d2, cd2 = _sem(ctx, "d2")
    d3, cd3 = _sem(ctx, "d3")
    p0, cp0 = _sem(ctx, "p0")
    v0, cv0 = _sem(ctx, "v0")
    s0, cs0 = _sem(ctx, "s0")
    xn0 = lambda kc, a, n: ar.bf("xn0full", kc * S + a, n)
    qfv = lambda c, a, n: ar.bf("qf", c * S + a, n)
    kfv = lambda c, a, n: ar.bf("kf", c * S + a, n)
    vaug = lambda a, n: ar.bf("vaug", a, n)
    cum = lambda ncg: 8 * min(ncg, 8) + 32 * max(0, ncg - 8)

    p0b, v0b, s0b = cp0.v, cv0.v, cs0.v
    d2b, d3b = cd2.v, cd3.v
    with nc.Block() as blk:
        @blk.vector
        def _(v):
            v.memset(vaug(0, 32 * 520), 1.0).then_inc(v0, 1)
        cv0.inc()
        v0b = cv0.v

        @blk.gpsimd
        def _(gp):
            for cg in range(12):
                buf = ctx["wsa"] if cg % 2 == 0 else ctx["wsb"]
                if cg >= 2:
                    gp.wait_ge(p0, p0b + cum(cg - 1))
                gp.dma_start(out=buf[:, 0:1024],
                             in_=bass.AP(ctx["wattn_h"], cg * 128,
                                         [[1536, 128], [128 * 1536, 8], [1, 128]])
                             ).then_inc(d2 if cg % 2 == 0 else d3, 16)
                (cd2 if cg % 2 == 0 else cd3).inc(16)

        @blk.tensor
        def _(te):
            te.wait_ge(v0, v0b)
            te.wait_ge(d0, cd0.v)
            n = 0
            for cg in range(12):
                buf = ctx["wsa"] if cg % 2 == 0 else ctx["wsb"]
                te.wait_ge(d2 if cg % 2 == 0 else d3,
                           (d2b if cg % 2 == 0 else d3b) + 16 * (cg // 2 + 1))
                if cg < 8:
                    for tc in range(8):
                        j = cg * 8 + tc
                        if j >= 4:
                            te.wait_ge(s0, s0b + (j - 3))
                        bank = pb[tc % 4]
                        for kc in range(8):
                            te.matmul(bank[:, 0:512], buf[:, kc * 128:(kc + 1) * 128],
                                      xn0(kc, tc * 512, 512), start=(kc == 0), stop=False)
                        te.matmul(bank[:, 0:512], ctx["biasrows"][0:1, cg * 128:(cg + 1) * 128],
                                  ctx["onerb"][0:1, 0:512], start=False, stop=True).then_inc(p0, 1)
                        n += 1
                else:
                    for tc in range(8):
                        for tt in range(4):
                            idx = (cg - 8) * 32 + tc * 4 + tt
                            if idx >= 2:
                                te.wait_ge(v0, v0b + (idx - 1))
                            t0 = tc * 512 + tt * 128
                            bank = pb[4 + idx % 2]
                            for kc in range(8):
                                te.matmul(bank[:, 0:128], xn0(kc, t0, 128),
                                          buf[:, kc * 128:(kc + 1) * 128],
                                          start=(kc == 0), stop=False)
                            te.matmul(bank[:, 0:128], ctx["onerb"][0:1, 0:128],
                                      ctx["biasrows"][0:1, cg * 128:(cg + 1) * 128],
                                      start=False, stop=True).then_inc(p0, 1)
                            n += 1
            cp0.inc(n)

        @blk.scalar
        def _(sc):
            for cg in range(8):
                for tc in range(8):
                    sc.wait_ge(p0, p0b + cg * 8 + tc + 1)
                    bank = pb[tc % 4]
                    if cg < 4:
                        sc.activation(out=qfv(cg, tc * 512, 512), in_=bank[:, 0:512],
                                      func=AF.Copy, scale=0.125).then_inc(s0, 1)
                    else:
                        sc.activation(out=kfv(cg - 4, tc * 512, 512), in_=bank[:, 0:512],
                                      func=AF.Copy).then_inc(s0, 1)
            cs0.inc(64)

        @blk.vector
        def _(v):
            for cg in range(8, 12):
                for tc in range(8):
                    for tt in range(4):
                        idx = (cg - 8) * 32 + tc * 4 + tt
                        v.wait_ge(p0, p0b + 64 + idx + 1)
                        bank = pb[4 + idx % 2]
                        tile = tc * 4 + tt
                        hh = (cg - 8) * 2
                        v.tensor_copy(out=vaug(tile * 520 + hh * 65, 64), in_=bank[:, 0:64])
                        v.tensor_copy(out=vaug(tile * 520 + (hh + 1) * 65, 64), in_=bank[:, 64:128]).then_inc(v0, 1)
            cv0.inc(128)


def ph_kquant(ctx):
    nc = ctx["nc"]
    ar = ctx["ar"]
    pb = ctx["pb"]
    p0, cp0 = _sem(ctx, "p0")
    v0, cv0 = _sem(ctx, "v0")
    s0, cs0 = _sem(ctx, "s0")
    d2, cd2 = _sem(ctx, "d2")
    sv, srow = ctx["sv"], ctx["srow"]
    QL = S - 32
    kfv = lambda c, a, n: ar.bf("kf", c * S + a, n)

    with (
        nc.sbuf_tensor("kst", [128, 8], F32) as kst,
        nc.sbuf_tensor("kstt", [8, 128], F32) as kstt,
        nc.sbuf_tensor("st8", [8, 4], F32) as st8,
        nc.sbuf_tensor("svt", [2, 12], F32) as svt,
        nc.Block() as blk,
    ):
        @blk.vector
        def _(v):
            v.wait_ge(s0, cs0.v)
            for c in range(4):
                v.tensor_reduce(out=kst[:, c:c + 1], in_=kfv(c, 0, QL), axis=AX.X, op=ALU.min)
                mmv = v.tensor_reduce(out=kst[:, 4 + c:5 + c], in_=kfv(c, 0, QL),
                                      axis=AX.X, op=ALU.max)
            mmv.then_inc(v0, 1)
            cv0.inc()

        @blk.tensor
        def _(te):
            te.wait_ge(v0, cv0.v)
            te.transpose(pb[0][0:8, 0:128], kst[:, 0:8], ctx["cst"][:, 0:128]).then_inc(p0, 1)
            cp0.inc()

        @blk.vector
        def _(v):
            v.wait_ge(p0, cp0.v)
            v.tensor_copy(out=kstt[:], in_=pb[0][0:8, 0:128])
            v.drain()
            k3 = kstt[0:8, :].rearrange("p (h c) -> p h c", c=64)
            v.tensor_reduce(out=st8[0:8, 0:2], in_=k3, axis=AX.X, op=ALU.min)
            v.tensor_reduce(out=st8[0:8, 2:4], in_=k3, axis=AX.X, op=ALU.max).then_inc(v0, 1)
            cv0.inc()

        @blk.gpsimd
        def _(gp):
            gp.wait_ge(v0, cv0.v)
            gp.dma_start(out=srow[0:1, 8:16], in_=st8[0:4, 0:2]).then_inc(d2, 16)
            cd2.inc(16)
            gp.dma_start(out=srow[0:1, 16:24], in_=st8[4:8, 2:4]).then_inc(d2, 16)
            cd2.inc(16)

        @blk.vector
        def _(v):
            v.wait_ge(d2, cd2.v)
            v.tensor_scalar(out=srow[0:1, 8:24], in0=srow[0:1, 8:24],
                            scalar1=-1e4, scalar2=1e4, op0=ALU.max, op1=ALU.min)
            v.drain()
            v.tensor_tensor(out=srow[0:1, 24:32], in0=srow[0:1, 16:24], in1=srow[0:1, 8:16],
                            op=ALU.subtract)
            v.drain()
            v.tensor_scalar(out=srow[0:1, 24:32], in0=srow[0:1, 24:32],
                            scalar1=1.0 / 3.0, scalar2=1e-8, op0=ALU.mult, op1=ALU.max)
            v.drain()
            v.reciprocal(out=srow[0:1, 32:40], in_=srow[0:1, 24:32])
            v.drain()
            v.tensor_copy(out=srow[0:1, 16:24], in_=srow[0:1, 8:16]).then_inc(v0, 1)
            cv0.inc()

        @blk.gpsimd
        def _(gp):
            gp.wait_ge(v0, cv0.v)
            srci = srow[0:1, 16:40].rearrange("p (a h) -> p a h", h=2)
            with nc.allow_non_contiguous_dma(reason="24-element stat shuffle"):
                gp.dma_start(out=svt[0:1, :].unsqueeze(2),
                             in_=srci[:, :, 0:1]).then_inc(d2, 16)
                cd2.inc(16)
                gp.dma_start(out=svt[1:2, :].unsqueeze(2),
                             in_=srci[:, :, 1:2]).then_inc(d2, 16)
                cd2.inc(16)

        @blk.tensor
        def _(te):
            te.wait_ge(d2, cd2.v)
            te.matmul(pb[1][:, 0:12], ctx["hselt"][:], svt[:], start=True, stop=True).then_inc(p0, 1)
            cp0.inc()

        @blk.vector
        def _(v):
            v.wait_ge(p0, cp0.v)
            v.tensor_copy(out=sv[:, 11:23], in_=pb[1][:, 0:12]).then_inc(v0, 1)
            cv0.inc()

        @blk.vector
        def _(v):
            kqt = ar.f32("tmpA", 0, 512)
            kqr = ar.f32("tmpA", 512, 512)
            kqi = ar.i32("tmpA", 1024, 512)
            for c in range(4):
                mn, scl, rsc = sv[:, 11 + c:12 + c], sv[:, 15 + c:16 + c], sv[:, 19 + c:20 + c]
                for bi in range(8):
                    lo = bi * 512
                    w = min(512, QL - lo)
                    v.tensor_scalar(out=kqt[:, 0:w], in0=kfv(c, lo, w),
                                    scalar1=-1e4, scalar2=1e4, op0=ALU.max, op1=ALU.min)
                    v.drain()
                    v.tensor_tensor(out=kqt[:, 0:w], in0=kqt[:, 0:w],
                                    in1=mn.broadcast_to([128, w]), op=ALU.subtract)
                    v.drain()
                    v.tensor_tensor(out=kqt[:, 0:w], in0=kqt[:, 0:w],
                                    in1=rsc.broadcast_to([128, w]), op=ALU.mult)
                    v.drain()
                    v.tensor_copy(out=kqi[:, 0:w], in_=kqt[:, 0:w])
                    v.drain()
                    v.tensor_copy(out=kqr[:, 0:w], in_=kqi[:, 0:w])
                    v.drain()
                    v.tensor_tensor(out=kqt[:, 0:w], in0=kqt[:, 0:w],
                                    in1=kqr[:, 0:w], op=ALU.subtract)
                    v.drain()
                    v.tensor_scalar(out=kqt[:, 0:w], in0=kqt[:, 0:w],
                                    scalar1=0.5, scalar2=None, op0=ALU.is_ge)
                    v.drain()
                    v.tensor_tensor(out=kqr[:, 0:w], in0=kqr[:, 0:w],
                                    in1=kqt[:, 0:w], op=ALU.add)
                    v.drain()
                    v.tensor_tensor(out=kqr[:, 0:w], in0=kqr[:, 0:w],
                                    in1=scl.broadcast_to([128, w]), op=ALU.mult)
                    v.drain()
                    v.tensor_tensor(out=kfv(c, lo, w), in0=kqr[:, 0:w],
                                    in1=mn.broadcast_to([128, w]), op=ALU.add)
                    v.drain()
            v.drain(fusable=False).then_inc(v0, 1); cv0.inc()


def ph_vquant(ctx):
    nc = ctx["nc"]
    ar = ctx["ar"]
    v0, cv0 = _sem(ctx, "v0")
    vaug = lambda a, n: ar.bf("vaug", a, n)

    ar.alloc("vqtmp", (32 + 512 * 3) * 4)
    vst = ar.f32("vqtmp", 0, 32)
    vtmp = ar.f32("vqtmp", 32, 512)
    vtr = ar.f32("vqtmp", 544, 512)
    vti = ar.i32("vqtmp", 1056, 512)
    with (
        nc.Block() as blk,
    ):
        @blk.vector
        def _(v):
            for tile in range(32):
                nr = 128 if tile < 31 else 96
                sl3 = vaug(tile * 520, 520)[0:nr, :].rearrange(
                    "p (h c) -> p h c", c=65)[:, :, 0:64]
                vt3 = vtmp[0:nr, :].rearrange("p (h c) -> p h c", c=64)
                vi3 = vti[0:nr, :].rearrange("p (h c) -> p h c", c=64)
                mn = vst[0:nr, 0:8]
                scl = vst[0:nr, 16:24]
                rsc = vst[0:nr, 24:32]
                bc = lambda t: t.unsqueeze(2).broadcast_to([nr, 8, 64])
                v.drain()
                v.tensor_scalar(out=vt3, in0=sl3, scalar1=-1e4, scalar2=1e4,
                                op0=ALU.max, op1=ALU.min)
                v.drain()
                v.tensor_reduce(out=mn, in_=vt3, axis=AX.X, op=ALU.min)
                v.tensor_reduce(out=vst[0:nr, 8:16], in_=vt3, axis=AX.X, op=ALU.max)
                v.drain()
                v.tensor_tensor(out=scl, in0=vst[0:nr, 8:16], in1=mn, op=ALU.subtract)
                v.drain()
                v.tensor_scalar(out=scl, in0=scl, scalar1=1.0 / 3.0, scalar2=1e-8,
                                op0=ALU.mult, op1=ALU.max)
                v.drain()
                v.reciprocal(out=rsc, in_=scl)
                v.drain()
                v.tensor_tensor(out=vt3, in0=vt3, in1=bc(mn), op=ALU.subtract)
                v.drain()
                v.tensor_tensor(out=vt3, in0=vt3, in1=bc(rsc), op=ALU.mult)
                v.drain()
                v.tensor_copy(out=vti[0:nr, :], in_=vtmp[0:nr, :])
                v.drain()
                v.tensor_copy(out=vtr[0:nr, :], in_=vti[0:nr, :])
                v.drain()
                v.tensor_tensor(out=vtmp[0:nr, :], in0=vtmp[0:nr, :],
                                in1=vtr[0:nr, :], op=ALU.subtract)
                v.drain()
                v.tensor_scalar(out=vtmp[0:nr, :], in0=vtmp[0:nr, :],
                                scalar1=0.5, scalar2=None, op0=ALU.is_ge)
                v.drain()
                v.tensor_tensor(out=vtr[0:nr, :], in0=vtr[0:nr, :],
                                in1=vtmp[0:nr, :], op=ALU.add)
                v.drain()
                vr3 = vtr[0:nr, :].rearrange("p (h c) -> p h c", c=64)
                v.tensor_tensor(out=vr3, in0=vr3, in1=bc(scl), op=ALU.mult)
                v.drain()
                mmv = v.tensor_tensor(out=sl3, in0=vr3, in1=bc(mn), op=ALU.add)
            mmv.then_inc(v0, 1)
            cv0.inc()
    ar.release("vqtmp")


def ph_mhat(ctx):
    nc = ctx["nc"]
    ar = ctx["ar"]
    pb = ctx["pb"]
    p0, cp0 = _sem(ctx, "p0")
    v0, cv0 = _sem(ctx, "v0")
    s0, cs0 = _sem(ctx, "s0")
    sv, srow = ctx["sv"], ctx["srow"]
    sq = lambda a, n: ar.bf("tmpA", a, n)

    p0b, v0b = cp0.v, cv0.v
    with nc.Block() as blk:
        @blk.vector
        def _(v):
            v.wait_ge(s0, cs0.v)
            for c in range(4):
                if c >= 1:
                    v.wait_ge(p0, p0b + c)
                buf = ar.bf("qf", c * S, S)
                v.tensor_tensor(out=sq(0, S), in0=buf, in1=buf, op=ALU.mult).then_inc(v0, 1)
            v.wait_ge(p0, p0b + 4)
            for tc in range(8):
                v.tensor_reduce(out=srow[0:1, 48 + tc:49 + tc], in_=pb[tc][0:1, 0:512],
                                axis=AX.X, op=ALU.max)
            v.drain()
            v.tensor_reduce(out=srow[0:1, 40:41], in_=srow[0:1, 48:56], axis=AX.X, op=ALU.max).then_inc(v0, 1)
            for c in range(4):
                v.wait_ge(p0, p0b + 4 + c)
                buf = ar.bf("kf", c * S, S)
                v.tensor_tensor(out=sq(0, S), in0=buf, in1=buf, op=ALU.mult).then_inc(v0, 1)
            v.wait_ge(p0, p0b + 8)
            for tc in range(8):
                v.tensor_reduce(out=srow[0:1, 48 + tc:49 + tc], in_=pb[tc][0:1, 0:512],
                                axis=AX.X, op=ALU.max)
            v.drain()
            v.tensor_reduce(out=srow[0:1, 41:42], in_=srow[0:1, 48:56], axis=AX.X, op=ALU.max)
            v.drain()
            v.tensor_tensor(out=srow[0:1, 42:43], in0=srow[0:1, 40:41], in1=srow[0:1, 41:42],
                            op=ALU.mult).then_inc(v0, 1)
            cv0.inc(10)

        @blk.tensor
        def _(te):
            for half in range(2):
                for c in range(4):
                    te.wait_ge(v0, v0b + half * 5 + c + 1)
                    for tc in range(8):
                        mm = te.matmul(pb[tc][0:1, 0:512], ctx["onecb"][:], sq(tc * 512, 512),
                                       start=(c == 0), stop=(c == 3))
                        if tc == 7:
                            mm.then_inc(p0, 1)
            cp0.inc(8)

        @blk.scalar
        def _(sc):
            sc.wait_ge(v0, cv0.v)
            sc.activation(out=srow[0:1, 43:44], in_=srow[0:1, 42:43], func=AF.Sqrt).then_inc(s0, 1)
            cs0.inc()

        @blk.vector
        def _(v):
            v.wait_ge(s0, cs0.v)
            v.tensor_scalar(out=srow[0:1, 44:45], in0=srow[0:1, 43:44], scalar1=-1.0,
                            scalar2=None, op0=ALU.mult).then_inc(v0, 1)
            cv0.inc()

        @blk.tensor
        def _(te):
            te.wait_ge(v0, cv0.v)
            te.matmul(pb[1][:, 16:17], ctx["onerf"][:], srow[0:1, 44:45],
                      start=True, stop=True).then_inc(p0, 1)
            cp0.inc()

        @blk.vector
        def _(v):
            v.wait_ge(p0, cp0.v)
            v.tensor_copy(out=sv[:, 23:24], in_=pb[1][:, 16:17]).then_inc(v0, 1)
            cv0.inc()


def _attn_seq():
    seq = []
    for h in range(8):
        for j in range(8):
            seq.append((h, j, 4 * (j + 1)))
    return seq


def ph_attn(ctx):
    nc = ctx["nc"]
    ar = ctx["ar"]
    pb = ctx["pb"]
    p0, cp0 = _sem(ctx, "p0")
    v0, cv0 = _sem(ctx, "v0")
    s0, cs0 = _sem(ctx, "s0")
    d3, cd3 = _sem(ctx, "d3")
    sv = ctx["sv"]
    ering = ctx["eringb"]
    qfb = lambda c: ar.bf("qf", c * S, S)
    kfb = lambda c: ar.bf("kf", c * S, S)
    vaug = lambda a, n: ar.bf("vaug", a, n)
    seq = _attn_seq()

    p0b, v0b, s0b, d3b = cp0.v, cv0.v, cs0.v, cd3.v

    def replay():
        pe, act, dve = 0, 0, 0
        ev = []
        for (h, j, nk) in seq:
            for i in range(nk):
                pe += 1; ev.append(("qk", h, j, i, pe, act, dve))
                act += 1; ev.append(("exp", h, j, i, pe, act, dve))
                if i >= 4 * j:
                    dve += 1; ev.append(("mask", h, j, i, pe, act, dve))
                pe += 1; ev.append(("av", h, j, i, pe, act, dve))
            dve += 1; ev.append(("recip", h, j, -1, pe, act, dve))
            pe += 1; ev.append(("bcast", h, j, -1, pe, act, dve))
            dve += 1; ev.append(("ymul", h, j, -1, pe, act, dve))
            ev.append(("ydma", h, j, -1, pe, act, dve))
        return ev

    ev = replay()
    exp_after_count = {}
    av_after = {}
    bcast_after = {}
    ymul_after = {}
    recip_after = {}
    qk_after = {}
    mask_after = {}
    for (kind, h, j, i, pe, act, dve) in ev:
        if kind == "exp":
            exp_after_count[(h, j, i)] = act
        elif kind == "av":
            av_after[(h, j, i)] = pe
        elif kind == "qk":
            qk_after[(h, j, i)] = pe
        elif kind == "mask":
            mask_after[(h, j, i)] = dve
        elif kind == "bcast":
            bcast_after[(h, j)] = pe
        elif kind == "ymul":
            ymul_after[(h, j)] = dve
        elif kind == "recip":
            recip_after[(h, j)] = dve

    with (
        nc.sbuf_tensor("zrec", [1, 512], F32) as zrec,
        nc.sbuf_tensor("zbc", [64, 512], F32) as zbc,
        nc.sbuf_tensor("ytile", [64, 512], BF16) as ytile,
        nc.Block() as blk,
    ):
        @blk.tensor
        def _(te):
            te.wait_ge(v0, v0b)
            te.wait_ge(s0, s0b)
            for si, (h, j, nk) in enumerate(seq):
                ch, half = h // 2, h % 2
                rows = slice(half * 64, (half + 1) * 64)
                av = pb[4 + si % 2]
                for i in range(nk):
                    if i >= 2:
                        te.wait_ge(s0, s0b + exp_after_count[(h, j, i - 2)])
                    elif si > 0:
                        ph_, pj_ = seq[si - 1][0], seq[si - 1][1]
                        te.wait_ge(s0, s0b + exp_after_count[(ph_, pj_, seq[si - 1][2] - 1)])
                    te.matmul(pb[i % 2][:, 0:512],
                              kfb(ch)[rows, i * 128:(i + 1) * 128],
                              qfb(ch)[rows, j * 512:(j + 1) * 512], start=True, stop=True).then_inc(p0, 1)
                    if i == 0 and si >= 2:
                        h2, j2 = seq[si - 2][0], seq[si - 2][1]
                        te.wait_ge(v0, v0b + ymul_after[(h2, j2)])
                    if i >= 4 * j:
                        te.wait_ge(v0, v0b + mask_after[(h, j, i)])
                    else:
                        te.wait_ge(s0, s0b + exp_after_count[(h, j, i)])
                    te.matmul(av[0:65, 0:512], vaug(i * 520 + h * 65, 65),
                              ering[:, (i % 2) * 512:(i % 2 + 1) * 512],
                              start=(i == 0), stop=(i == nk - 1)).then_inc(p0, 1)
                te.wait_ge(v0, v0b + recip_after[(h, j)])
                if si >= 1:
                    h1, j1 = seq[si - 1][0], seq[si - 1][1]
                    te.wait_ge(v0, v0b + ymul_after[(h1, j1)])
                te.matmul(pb[6][0:64, 0:512], ctx["onerf"][0:1, 0:64], zrec[0:1, 0:512],
                          start=True, stop=True).then_inc(p0, 1)
            cp0.inc(sum(2 * nk + 1 for (_, _, nk) in seq))

        @blk.scalar
        def _(sc):
            sc.wait_ge(v0, v0b)
            for (h, j, nk) in seq:
                for i in range(nk):
                    sc.wait_ge(p0, p0b + qk_after[(h, j, i)])
                    sc.activation(out=ering[:, (i % 2) * 512:(i % 2 + 1) * 512],
                                  in_=pb[i % 2][:, 0:512], func=AF.Exp,
                                  bias=sv[:, 23:24], scale=1.0).then_inc(s0, 1)
            cs0.inc(sum(nk for (_, _, nk) in seq))

        @blk.vector
        def _(v):
            ndma = 0
            for si, (h, j, nk) in enumerate(seq):
                av = pb[4 + si % 2]
                for i in range(nk):
                    if i >= 4 * j:
                        v.wait_ge(s0, s0b + exp_after_count[(h, j, i)])
                        r = i - 4 * j
                        v.tensor_tensor(out=ering[:, (i % 2) * 512:(i % 2 + 1) * 512],
                                        in0=ering[:, (i % 2) * 512:(i % 2 + 1) * 512],
                                        in1=ar.bf("maskt", r * 512, 512), op=ALU.mult).then_inc(v0, 1)
                v.wait_ge(p0, p0b + av_after[(h, j, nk - 1)])
                v.reciprocal(out=zrec[0:1, 0:512], in_=av[64:65, 0:512]).then_inc(v0, 1)
                v.wait_ge(p0, p0b + bcast_after[(h, j)])
                if si >= 1:
                    v.wait_ge(d3, d3b + 16 * si)
                v.tensor_copy(out=zbc[:], in_=pb[6][0:64, 0:512])
                v.drain()
                v.tensor_tensor(out=ytile[:, 0:512], in0=av[0:64, 0:512],
                                in1=zbc[:], op=ALU.mult).then_inc(v0, 1)
            cv0.inc(max(d[1] for d in [(0, ymul_after[seq[-1][0], seq[-1][1]])]))

        @blk.gpsimd
        def _(gp):
            for si, (h, j, nk) in enumerate(seq):
                gp.wait_ge(v0, v0b + ymul_after[(h, j)])
                gp.dma_start(out=ctx["y0_in"].ap()[h * 64:(h + 1) * 64, j * 512:(j + 1) * 512],
                             in_=ytile[:, 0:512]).then_inc(d3, 16)
                cd3.inc(16)
            gp.wait_ge(d3, cd3.v)


def ph_l0tail(ctx):
    nc = ctx["nc"]
    ar = ctx["ar"]
    pb = ctx["pb"]
    d1, cd1 = _sem(ctx, "d1")
    d2, cd2 = _sem(ctx, "d2")
    d3, cd3 = _sem(ctx, "d3")
    d4, cd4 = _sem(ctx, "d4")
    d5, cd5 = _sem(ctx, "d5")
    p0, cp0 = _sem(ctx, "p0")
    v0, cv0 = _sem(ctx, "v0")
    s0, cs0 = _sem(ctx, "s0")
    g0, cg0 = _sem(ctx, "g0")
    cc, ccc = _sem(ctx, "cc")

    with nc.Block() as blk:
        @blk.gpsimd
        def _(gp):
            with allow_ici_all_to_all():
                gp.collective_compute(
                    "AllGather", ALU.bypass,
                    replica_groups=[[0, 1], [2, 3], [4, 5], [6, 7]],
                    ins=[ctx["y0_in"].ap().opt()],
                    outs=[ctx["y0_out"].ap().opt()],
                ).then_inc(cc, 1)
            ccc.inc()

    ar.release("xn0full", "qf", "kf", "vaug", "tmpA", "maskt")
    ar.alloc("x1", 8 * TOK * 4)
    ar.alloc("xn0e", 8 * EXT * 2)
    ar.alloc("mixr", 8 * TOK * 2)
    ar.alloc("mixk", 8 * TOK * 2)
    x1 = lambda f, a, n: ar.f32("x1", f * TOK + a, n)
    xn0e = lambda kc, a, n: ar.bf("xn0e", kc * EXT + a, n)
    xrv = lambda kc, a, n: ar.bf("mixr", kc * TOK + a, n)
    xkv = lambda kc, a, n: ar.bf("mixk", kc * TOK + a, n)

    ar.alloc("xtt", 2 * E * 4)
    xtt = ar.f32("xtt", 0, 2 * E)
    with (
        nc.Block() as blk,
    ):
        d1b, p0b, v0b = cd1.v, cp0.v, cv0.v
        d4b, d5b = cd4.v, cd5.v

        @blk.sync
        def _(sy):
            for f in range(8):
                sy.dma_start_transpose(xn0e(f, 0, EXT),
                                       ctx["xn0_own_tm"].ap()[:, f * 128:(f + 1) * 128]
                                       ).then_inc(d1, 16)
                cd1.inc(16)

        @blk.gpsimd
        def _(gp):
            for i in range(16):
                if i >= 2:
                    gp.wait_ge(p0, p0b + 8 * (i - 1))
                gp.dma_start(out=xtt[:, (i % 2) * E:(i % 2 + 1) * E],
                             in_=ctx["x_own"][i * 128:(i + 1) * 128, :]
                             ).then_inc(d4 if i % 2 == 0 else d5, 16)
                (cd4 if i % 2 == 0 else cd5).inc(16)

        @blk.tensor
        def _(te):
            for i in range(16):
                te.wait_ge(d4 if i % 2 == 0 else d5,
                           (d4b if i % 2 == 0 else d5b) + 16 * (i // 2 + 1))
                for f in range(8):
                    idx = i * 8 + f
                    if idx >= 4:
                        te.wait_ge(v0, v0b + idx - 3)
                    te.transpose(pb[idx % 4][:, 0:128],
                                 xtt[:, (i % 2) * E + f * 128:(i % 2) * E + (f + 1) * 128],
                                 ctx["cst"][:, 0:128]).then_inc(p0, 1)
            cp0.inc(128)

        @blk.vector
        def _(v):
            for i in range(16):
                for f in range(8):
                    idx = i * 8 + f
                    v.wait_ge(p0, p0b + idx + 1)
                    v.tensor_copy(out=x1(f, i * 128, 128), in_=pb[idx % 4][:, 0:128]).then_inc(v0, 1)
            cv0.inc(128)

        @blk.scalar
        def _(sc):
            sc.wait_ge(d1, cd1.v)

    ar.release("xtt")
    with nc.Block() as blk:
        @blk.vector
        def _(v):
            v.wait_ge(d1, cd1.v)
            for kc in range(8):
                for half in range(2):
                    a = half * 1024
                    xn = xn0e(kc, 1 + a, 1024)
                    xp = xn0e(kc, a, 1024)
                    d_r = xrv(kc, a, 1024)
                    d_k = xkv(kc, a, 1024)
                    v.tensor_tensor(out=d_k, in0=xn, in1=xp, op=ALU.subtract)
                    v.drain()
                    v.tensor_tensor(out=d_r, in0=d_k,
                                    in1=ctx["mixmu"][:, kc:kc + 1].broadcast_to([128, 1024]),
                                    op=ALU.mult)
                    v.drain()
                    v.tensor_tensor(out=d_k, in0=d_k,
                                    in1=ctx["mixmu"][:, 8 + kc:9 + kc].broadcast_to([128, 1024]),
                                    op=ALU.mult)
                    v.drain()
                    v.tensor_tensor(out=d_r, in0=d_r, in1=xp, op=ALU.add)
                    mmv = v.tensor_tensor(out=d_k, in0=d_k, in1=xp, op=ALU.add)
            mmv.then_inc(v0, 1)
            cv0.inc()

    ar.release("xn0e")
    ar.alloc("kact", 8 * TOK * 2)
    ar.alloc("wfull", 8 * 1024 * 2)
    wfull = lambda kc, dc: ar.bf("wfull", kc * 1024 + dc * 128, 128)
    kact = lambda kc, a, n: ar.bf("kact", kc * TOK + a, n)

    def stream_wfull(gp, wname):
        for kc in range(8):
            gp.dma_start(out=ar.bf("wfull", kc * 1024, 1024),
                         in_=bass.AP(ctx[wname + "_h"], kc * 128 * E, [[E, 128], [1, E]])
                         ).then_inc(d2, 16)
            cd2.inc(16)

    with nc.Block() as blk:
        d2b, p0b, v0b = cd2.v, cp0.v, cv0.v

        @blk.gpsimd
        def _(gp):
            stream_wfull(gp, "wkT")

        @blk.tensor
        def _(te):
            te.wait_ge(d2, d2b + 128)
            n = 0
            for dc in range(8):
                for tc in range(4):
                    idx = dc * 4 + tc
                    if idx >= 4:
                        te.wait_ge(v0, v0b + 2 * (idx - 3))
                    bank = pb[idx % 4]
                    for kc in range(8):
                        mm = te.matmul(bank[:, 0:512], wfull(kc, dc), xkv(kc, tc * 512, 512),
                                       start=(kc == 0), stop=(kc == 7))
                    mm.then_inc(p0, 1)
                    n += 1
            cp0.inc(n)

        @blk.vector
        def _(v):
            for dc in range(8):
                for tc in range(4):
                    idx = dc * 4 + tc
                    v.wait_ge(p0, p0b + idx + 1)
                    bank = pb[idx % 4]
                    v.tensor_scalar(out=kact(dc, tc * 512, 512), in0=bank[:, 0:512],
                                    scalar1=0.0, scalar2=None, op0=ALU.max)
                    v.drain()
                    v.tensor_tensor(out=kact(dc, tc * 512, 512), in0=kact(dc, tc * 512, 512),
                                    in1=kact(dc, tc * 512, 512), op=ALU.mult).then_inc(v0, 2)
            cv0.inc(64)

    ar.release("mixk")
    ar.alloc("wfull2", 8 * 1024 * 2)
    ar.alloc("kvslab", 8 * 512 * 2)
    ar.alloc("sgslab", 8 * 512 * 2)
    wfull2 = lambda kc, dc: ar.bf("wfull2", kc * 1024 + dc * 128, 128)
    kvs = lambda dc: ar.bf("kvslab", dc * 512, 512)
    sgs = lambda dc: ar.bf("sgslab", dc * 512, 512)

    with nc.Block() as blk:
        d2b, p0b, v0b, s0b = cd2.v, cp0.v, cv0.v, cs0.v

        @blk.gpsimd
        def _(gp):
            stream_wfull(gp, "wvT")
            for kc in range(8):
                gp.dma_start(out=ar.bf("wfull2", kc * 1024, 1024),
                             in_=bass.AP(ctx["wrT_h"], kc * 128 * E, [[E, 128], [1, E]])
                             ).then_inc(d2, 16)
                cd2.inc(16)

        @blk.tensor
        def _(te):
            te.wait_ge(d2, d2b + 256)
            n = 0
            for tc in range(4):
                for dc in range(8):
                    bank = pb[dc % 4]
                    if dc >= 4 or tc > 0:
                        te.wait_ge(v0, v0b + _l0_vr_vwait(tc, dc))
                    for kc in range(8):
                        mm = te.matmul(bank[:, 0:512], wfull(kc, dc), kact(kc, tc * 512, 512),
                                       start=(kc == 0), stop=(kc == 7))
                    mm.then_inc(p0, 1)
                    n += 1
                for dc in range(8):
                    bank = pb[4 + dc % 3]
                    if tc > 0 or dc >= 3:
                        te.wait_ge(s0, s0b + _l0_r_swait(tc, dc))
                    for kc in range(8):
                        mm = te.matmul(bank[:, 0:512], wfull2(kc, dc), xrv(kc, tc * 512, 512),
                                       start=(kc == 0), stop=(kc == 7))
                    mm.then_inc(p0, 1)
                    n += 1
            cp0.inc(n)

        @blk.vector
        def _(v):
            for tc in range(4):
                for dc in range(8):
                    v.wait_ge(p0, p0b + tc * 16 + dc + 1)
                    v.tensor_copy(out=kvs(dc), in_=pb[dc % 4][:, 0:512]).then_inc(v0, 1)
                for dc in range(8):
                    v.wait_ge(s0, s0b + tc * 8 + dc + 1)
                    v.tensor_tensor(out=sgs(dc), in0=sgs(dc), in1=kvs(dc), op=ALU.mult)
                    v.drain()
                    v.tensor_tensor(out=x1(dc, tc * 512, 512), in0=x1(dc, tc * 512, 512),
                                    in1=sgs(dc), op=ALU.add).then_inc(v0, 1)
            cv0.inc(64)

        @blk.scalar
        def _(sc):
            for tc in range(4):
                for dc in range(8):
                    sc.wait_ge(p0, p0b + tc * 16 + 8 + dc + 1)
                    if tc > 0:
                        sc.wait_ge(v0, v0b + 16 * (tc - 1) + 8 + dc + 1)
                    sc.activation(out=sgs(dc), in_=pb[4 + dc % 3][:, 0:512], func=AF.Sigmoid).then_inc(s0, 1)
            cs0.inc(32)

    ar.release("mixr", "kact", "kvslab", "sgslab", "wfull2")
    ar.alloc("yall", 8 * TOK * 2)
    yall = lambda kc, a, n: ar.bf("yall", kc * TOK + a, n)

    with nc.Block() as blk:
        d2b, d3b, p0b, v0b = cd2.v, cd3.v, cp0.v, cv0.v

        @blk.gpsimd
        def _(gp):
            stream_wfull(gp, "wproj")
            gp.wait_ge(cc, ccc.v)
            rgC = gp.alloc_register()
            gp.reg_load(rgC, ctx["offs"][0:1, 1:2])
            gp.dma_start(out=yall(0, 0, 8 * TOK).rearrange("p (kc t) -> p kc t", t=TOK),
                         in_=bass.AP(ctx["y0_out"], rgC,
                                     [[S, 128], [128 * S, 8], [1, TOK]])
                         ).then_inc(d3, 16)
            cd3.inc(16)

        @blk.tensor
        def _(te):
            te.wait_ge(d2, d2b + 128)
            te.wait_ge(d3, d3b + 16)
            n = 0
            for dc in range(8):
                for tc in range(4):
                    idx = dc * 4 + tc
                    if idx >= 4:
                        te.wait_ge(v0, v0b + idx - 3)
                    bank = pb[idx % 4]
                    for kc in range(8):
                        te.matmul(bank[:, 0:512], wfull(kc, dc), yall(kc, tc * 512, 512),
                                  start=(kc == 0), stop=False)
                    te.matmul(bank[:, 0:512], ctx["biasrows"][0:1, 1536 + dc * 128:1536 + (dc + 1) * 128],
                              ctx["onerb"][0:1, 0:512], start=False, stop=True).then_inc(p0, 1)
                    n += 1
            cp0.inc(n)

        @blk.vector
        def _(v):
            for dc in range(8):
                for tc in range(4):
                    idx = dc * 4 + tc
                    v.wait_ge(p0, p0b + idx + 1)
                    v.tensor_tensor(out=x1(dc, tc * 512, 512), in0=x1(dc, tc * 512, 512),
                                    in1=pb[idx % 4][:, 0:512], op=ALU.add).then_inc(v0, 1)
            cv0.inc(32)

    ar.release("yall", "wfull")
    ar.alloc("xn1", 8 * TOK * 2)
    _emit_ln_fm(ctx, src_f32=lambda f, a, n: ar.f32("x1", f * TOK + a, n),
                dst_bf=lambda f, a, n: ar.bf("xn1", f * TOK + a, n),
                n_tok=TOK)


def _l0_vr_vwait(tc, dc):
    if dc >= 4:
        return 16 * tc + (dc - 4) + 1
    return 16 * (tc - 1) + (dc + 4) + 1 if tc > 0 else 0


def _l0_r_swait(tc, dc):
    if dc >= 3:
        return tc * 8 + dc - 2
    return tc * 8


def _emit_ln_fm(ctx, src_f32, dst_bf, n_tok, gamma_fn=None, beta_fn=None,
                 res_dst=None, x1_acc=None):
    nc = ctx["nc"]
    pb = ctx["pb"]
    p0, cp0 = _sem(ctx, "p0")
    v0, cv0 = _sem(ctx, "v0")
    s0, cs0 = _sem(ctx, "s0")
    ntc = n_tok // 512

    def pe_sx(tc, kc):
        return tc * 16 + kc + 1

    def pe_bc(tc, kc):
        return tc * 16 + 8 + kc + 1

    ar = ctx["ar"]
    ar.alloc("lntmp", 4 * 512 * 4 + 2 * 512 * 2 + 2 * 512 * 4)
    lnr = ar.f32("lntmp", 0, 4 * 512)[0:1, :]
    lnrb = ar.bf("lntmp", 4 * 512 * 2, 2 * 512)[0:1, :]
    sqf = ar.f32("lntmp", 4 * 512 + 512, 2 * 512)
    with (
        nc.Block() as blk,
    ):
        p0b, v0b, s0b = cp0.v, cv0.v, cs0.v

        @blk.scalar
        def _(sc):
            sc.wait_ge(v0, v0b)
            for tc in range(ntc):
                for kc in range(8):
                    idx = tc * 8 + kc
                    if idx >= 2:
                        sc.wait_ge(p0, p0b + pe_sx((idx - 2) // 8, (idx - 2) % 8))
                    sc.activation(out=sqf[:, (kc % 2) * 512:(kc % 2 + 1) * 512],
                                  in_=src_f32(kc, tc * 512, 512), func=AF.Square).then_inc(s0, 1)
                sc.wait_ge(v0, v0b + tc * 10 + 1)
                sc.activation(out=lnr[0:1, 512:1024], in_=lnr[0:1, 1024:1536], func=AF.Sqrt).then_inc(s0, 1)
            cs0.inc(ntc * 9)

        @blk.tensor
        def _(te):
            te.wait_ge(v0, v0b)
            for tc in range(ntc):
                if tc >= 1:
                    te.wait_ge(v0, v0b + tc * 10)
                for kc in range(8):
                    te.matmul(pb[6][0:1, 0:512], ctx["onec"][:], src_f32(kc, tc * 512, 512),
                              start=(kc == 0), stop=(kc == 7))
                    te.wait_ge(s0, s0b + tc * 9 + kc + 1)
                    te.matmul(pb[7][0:1, 0:512], ctx["onec"][:],
                              sqf[:, (kc % 2) * 512:(kc % 2 + 1) * 512],
                              start=(kc == 0), stop=(kc == 7)).then_inc(p0, 1)
                for kc in range(8):
                    te.wait_ge(v0, v0b + tc * 10 + 2 + kc)
                    ga = ctx["onerb"][0:1, 0:128] if gamma_fn is None else gamma_fn(kc)
                    te.matmul(pb[6][:, 0:512], ga, lnrb[0:1, 0:512], start=True, stop=True)
                    if beta_fn is None:
                        te.matmul(pb[7][:, 0:512], ga, lnrb[0:1, 512:1024],
                                  start=True, stop=True).then_inc(p0, 1)
                    else:
                        te.matmul(pb[7][:, 0:512], ga, lnrb[0:1, 512:1024], start=True, stop=False)
                        te.matmul(pb[7][:, 0:512], beta_fn(kc), ctx["onerb"][0:1, 0:512],
                                  start=False, stop=True).then_inc(p0, 1)
            cp0.inc(ntc * 16)

        @blk.vector
        def _(v):
            for tc in range(ntc):
                v.wait_ge(p0, p0b + pe_sx(tc, 7))
                v.tensor_scalar(out=lnr[0:1, 0:512], in0=pb[6][0:1, 0:512],
                                scalar1=1.0 / E, scalar2=None, op0=ALU.mult)
                v.tensor_scalar(out=lnr[0:1, 1024:1536], in0=pb[7][0:1, 0:512],
                                scalar1=1.0 / E, scalar2=None, op0=ALU.mult)
                v.drain()
                v.tensor_tensor(out=lnr[0:1, 1536:2048], in0=lnr[0:1, 0:512],
                                in1=lnr[0:1, 0:512], op=ALU.mult)
                v.drain()
                v.tensor_tensor(out=lnr[0:1, 1024:1536], in0=lnr[0:1, 1024:1536],
                                in1=lnr[0:1, 1536:2048], op=ALU.subtract)
                v.drain()
                v.tensor_scalar(out=lnr[0:1, 1024:1536], in0=lnr[0:1, 1024:1536],
                                scalar1=1e-5, scalar2=None, op0=ALU.add).then_inc(v0, 1)
                v.wait_ge(s0, s0b + tc * 9 + 9)
                v.reciprocal(out=lnr[0:1, 1024:1536], in_=lnr[0:1, 512:1024])
                v.drain()
                v.tensor_tensor(out=lnr[0:1, 1536:2048], in0=lnr[0:1, 0:512],
                                in1=lnr[0:1, 1024:1536], op=ALU.mult)
                v.drain()
                v.tensor_scalar(out=lnr[0:1, 1536:2048], in0=lnr[0:1, 1536:2048],
                                scalar1=-1.0, scalar2=None, op0=ALU.mult)
                v.drain()
                v.tensor_copy(out=lnrb[0:1, 0:512], in_=lnr[0:1, 1024:1536])
                v.tensor_copy(out=lnrb[0:1, 512:1024], in_=lnr[0:1, 1536:2048]).then_inc(v0, 1)
                for kc in range(8):
                    v.wait_ge(p0, p0b + pe_bc(tc, kc))
                    if res_dst is not None:
                        v.tensor_tensor(out=res_dst(kc, tc * 512, 512),
                                        in0=src_f32(kc, tc * 512, 512),
                                        in1=pb[6][:, 0:512], op=ALU.mult)
                        v.drain()
                        if x1_acc is None:
                            v.tensor_tensor(out=res_dst(kc, tc * 512, 512),
                                            in0=res_dst(kc, tc * 512, 512),
                                            in1=pb[7][:, 0:512], op=ALU.add).then_inc(v0, 1)
                        else:
                            v.tensor_tensor(out=res_dst(kc, tc * 512, 512),
                                            in0=res_dst(kc, tc * 512, 512),
                                            in1=pb[7][:, 0:512], op=ALU.add)
                            v.drain()
                            v.tensor_tensor(out=x1_acc(kc, tc * 512, 512),
                                            in0=x1_acc(kc, tc * 512, 512),
                                            in1=res_dst(kc, tc * 512, 512),
                                            op=ALU.add).then_inc(v0, 1)
                    else:
                        v.tensor_tensor(out=dst_bf(kc, tc * 512, 512),
                                        in0=src_f32(kc, tc * 512, 512),
                                        in1=pb[6][:, 0:512], op=ALU.mult)
                        v.drain()
                        v.tensor_tensor(out=dst_bf(kc, tc * 512, 512),
                                        in0=dst_bf(kc, tc * 512, 512),
                                        in1=pb[7][:, 0:512], op=ALU.add).then_inc(v0, 1)
            cv0.inc(ntc * 10)
    ar.release("lntmp")


def ph_l1head(ctx):
    nc = ctx["nc"]
    ar = ctx["ar"]
    pb = ctx["pb"]
    d2, cd2 = _sem(ctx, "d2")
    d3, cd3 = _sem(ctx, "d3")
    p0, cp0 = _sem(ctx, "p0")
    v0, cv0 = _sem(ctx, "v0")
    cc, ccc = _sem(ctx, "cc")
    xn1 = lambda kc, a, n: ar.bf("xn1", kc * TOK + a, n)
    ar.alloc("uown", 16 * 256 * 2)
    uown = lambda tt: ar.bf("uown", tt * 256, 256)

    with nc.Block() as blk:
        d2b, p0b, v0b = cd2.v, cp0.v, cv0.v

        @blk.gpsimd
        def _(gp):
            gp.dma_start(out=ctx["wsa"][:, 0:1024],
                         in_=bass.AP(ctx["wiT_h"], 0, [[256, 128], [128 * 256, 4], [1, 256]])
                         ).then_inc(d2, 16)
            cd2.inc(16)
            gp.dma_start(out=ctx["wsb"][:, 0:1024],
                         in_=bass.AP(ctx["wiT_h"], 4 * 128 * 256, [[256, 128], [128 * 256, 4], [1, 256]])
                         ).then_inc(d2, 16)
            cd2.inc(16)

        @blk.tensor
        def _(te):
            te.wait_ge(d2, d2b + 32)
            n = 0
            for tt in range(16):
                bank = pb[tt % 4]
                if tt >= 4:
                    te.wait_ge(v0, v0b + tt - 3)
                for kc in range(8):
                    wbuf = ctx["wsa"] if kc < 4 else ctx["wsb"]
                    te.matmul(bank[:, 0:256], xn1(kc, tt * 128, 128),
                              wbuf[:, (kc % 4) * 256:(kc % 4 + 1) * 256],
                              start=(kc == 0), stop=False)
                te.matmul(bank[:, 0:256], ctx["onerb"][0:1, 0:128],
                          ctx["biasrows"][0:1, 3584:3840], start=False, stop=True).then_inc(p0, 1)
                n += 1
            cp0.inc(n)

        @blk.vector
        def _(v):
            for tt in range(16):
                v.wait_ge(p0, p0b + tt + 1)
                v.tensor_copy(out=uown(tt), in_=pb[tt % 4][:, 0:256]).then_inc(v0, 1)
            cv0.inc(16)

        @blk.gpsimd
        def _(gp):
            for tt in range(16):
                gp.wait_ge(v0, v0b + tt + 1)
                gp.dma_start(out=ctx["u_in"].ap()[tt * 128:(tt + 1) * 128, :],
                             in_=uown(tt)).then_inc(d3, 16)
                cd3.inc(16)
            gp.wait_ge(d3, cd3.v)
            with allow_ici_all_to_all():
                gp.collective_compute(
                    "AllGather", ALU.bypass,
                    replica_groups=[[0, 1], [2, 3], [4, 5], [6, 7]],
                    ins=[ctx["u_in"].ap().opt()],
                    outs=[ctx["u_out"].ap().opt()],
                ).then_inc(cc, 1)
            ccc.inc()

    ar.release("uown")


def ph_scan(ctx):
    nc = ctx["nc"]
    ar = ctx["ar"]
    pb = ctx["pb"]
    d2, cd2 = _sem(ctx, "d2")
    d3, cd3 = _sem(ctx, "d3")
    p0, cp0 = _sem(ctx, "p0")
    v0, cv0 = _sem(ctx, "v0")
    cc, ccc = _sem(ctx, "cc")
    T = SCAN_T
    ar.alloc("ustrip", 2 * 32 * 128 * 2)
    ar.alloc("yfm", S * 2)
    ustrip = lambda k: ar.bf("ustrip", (k % 2) * 32 * 128, 32 * 128)
    yfm = lambda a, n: ar.bf("yfm", a, n)
    ss = [ctx["ss0"], ctx["ss1"]]
    pv = lambda t: pb[t % 3][:, 0:128]
    py = lambda t: pb[3 + (t // 512) % 2][:, (t % 512):(t % 512) + 1]

    p0b, v0b, d2b = cp0.v, cv0.v, cd2.v
    NSTRIP = (T + 31) // 32
    d3b = cd3.v + 16

    with nc.Block() as blk:
        @blk.gpsimd
        def _(gp):
            gp.wait_ge(cc, ccc.v)
            rgD = gp.alloc_register()
            gp.reg_load(rgD, ctx["offs"][0:1, 2:3])
            gp.dma_start(out=ctx["u_g"].ap(),
                         in_=bass.AP(ctx["u_out"], rgD, [[256, S], [1, 128]])
                         ).then_inc(d3, 16)
            cd3.inc(16)
            gp.wait_ge(d3, cd3.v)
            for k in range(NSTRIP):
                if k >= 2:
                    gp.wait_ge(v0, v0b + 32 * (k - 1))
                gp.dma_start(out=ustrip(k)[0:1, :],
                             in_=ctx["u_g"].ap()[k * 32:(k + 1) * 32, :]
                             ).then_inc(d2 if k % 2 == 0 else d3, 16)
                (cd2 if k % 2 == 0 else cd3).inc(16)

        @blk.tensor
        def _(te):
            te.wait_ge(d2, d2b + 16)
            te.matmul(pv(0), ctx["brow"][:], ustrip(0)[0:1, 0:128], start=True, stop=False)
            te.matmul(pv(1), ctx["brow"][:], ustrip(0)[0:1, 128:256], start=True, stop=False)
            n = 0
            for t in range(T):
                if (t + 2) % 32 == 0 and (t + 2) < T:
                    k = (t + 2) // 32
                    te.wait_ge(d2 if k % 2 == 0 else d3,
                               (d2b if k % 2 == 0 else d3b) + 16 * (k // 2 + 1))
                te.wait_ge(v0, v0b + t)
                te.matmul(pv(t), ctx["adT"][:], ss[(t + 1) % 2][:], start=False, stop=True).then_inc(p0, 1)
                n += 1
                if t >= 1:
                    te.matmul(py(t - 1), ss[(t - 1) % 2][:], ctx["ccol"][:], start=True, stop=True)
                if t + 2 < T:
                    k = (t + 2) // 32
                    te.matmul(pv(t + 2), ctx["brow"][:],
                              ustrip(k)[0:1, ((t + 2) % 32) * 128:((t + 2) % 32 + 1) * 128],
                              start=True, stop=False)
            te.wait_ge(v0, v0b + T)
            te.matmul(py(T - 1), ss[(T - 1) % 2][:], ctx["ccol"][:], start=True, stop=True).then_inc(p0, 1)
            n += 1
            cp0.inc(n)

        @blk.vector
        def _(v):
            for t in range(T):
                v.wait_ge(p0, p0b + t + 1)
                v.tensor_scalar(out=ss[t % 2][:], in0=pv(t), scalar1=-10.0, scalar2=10.0,
                                op0=ALU.max, op1=ALU.min).then_inc(v0, 1)
                if t % 512 == 2 and t > 512:
                    kk = t // 512 - 1
                    v.tensor_copy(out=yfm(kk * 512, 512), in_=pb[3 + kk % 2][:, 0:512])
            v.wait_ge(p0, p0b + T + 1)
            kk = (T - 1) // 512
            w = (T - 1) % 512 + 1
            v.tensor_copy(out=yfm(kk * 512, w), in_=pb[3 + kk % 2][:, 0:w]).then_inc(v0, 1)
            cv0.inc(T + 1)

        @blk.gpsimd
        def _(gp):
            gp.wait_ge(v0, v0b + T + 1)
            gp.dma_start(out=ctx["ys_in"].ap(), in_=yfm(0, S)[:, 0:S]).then_inc(d2, 16)
            cd2.inc(16)
            gp.wait_ge(d2, cd2.v)
            with allow_ici_all_to_all():
                gp.collective_compute(
                    "AllGather", ALU.bypass,
                    replica_groups=[[0, 1], [2, 3], [4, 5], [6, 7]],
                    ins=[ctx["ys_in"].ap().opt()],
                    outs=[ctx["ys_out"].ap().opt()],
                ).then_inc(cc, 1)
            ccc.inc()

    ar.release("ustrip")


def ph_ffn1(ctx):
    nc = ctx["nc"]
    ar = ctx["ar"]
    pb = ctx["pb"]
    d2, cd2 = _sem(ctx, "d2")
    d3, cd3 = _sem(ctx, "d3")
    d4, cd4 = _sem(ctx, "d4")
    d5, cd5 = _sem(ctx, "d5")
    p0, cp0 = _sem(ctx, "p0")
    v0, cv0 = _sem(ctx, "v0")
    s0, cs0 = _sem(ctx, "s0")
    xn1 = lambda kc, a, n: ar.bf("xn1", kc * TOK + a, n)
    x1 = lambda f, a, n: ar.f32("x1", f * TOK + a, n)
    ar.alloc("hslab", 24 * 512 * 2)
    ar.alloc("ftmp", 2 * 512 * 4)
    ar.alloc("wsc", 24 * 128 * 2)
    hs = lambda hc: ar.bf("hslab", hc * 512, 512)
    SC2 = 1.0 / (127.0 * 127.0)

    for tc in range(4):
        with nc.Block() as blk:
            p0b, v0b, s0b = cp0.v, cv0.v, cs0.v
            d2b, d3b, d4b, d5b = cd2.v, cd3.v, cd4.v, cd5.v

            @blk.gpsimd
            def _(gp, tc=tc):
                for cg in range(24):
                    bufg = ctx["wsa"]
                    bufu = ctx["wsb"]
                    if cg >= 1:
                        gp.wait_ge(p0, p0b + 2 * cg - 1)
                    elif tc > 0:
                        gp.wait_ge(p0, p0b)
                    dsem = d2 if cg % 2 == 0 else d3
                    gp.dma_start(out=bufg[:, 0:1024],
                                 in_=bass.AP(ctx["wgT_h"], cg * 128,
                                             [[HID, 128], [128 * HID, 8], [1, 128]])
                                 ).then_inc(dsem, 16)
                    if cg >= 1:
                        gp.wait_ge(p0, p0b + 2 * cg)
                    gp.dma_start(out=bufu[:, 0:1024],
                                 in_=bass.AP(ctx["wuT_h"], cg * 128,
                                             [[HID, 128], [128 * HID, 8], [1, 128]])
                                 ).then_inc(dsem, 16)
                    (cd2 if cg % 2 == 0 else cd3).inc(32)
                for dc in range(8):
                    gp.wait_ge(p0, p0b + 48 + dc) if dc > 0 else None
                    gp.dma_start(out=ar.bf("wsc", 0, 3072),
                                 in_=bass.AP(ctx["wdT_h"], dc * 128,
                                             [[E, 128], [128 * E, 24], [1, 128]])
                                 ).then_inc(d4 if dc % 2 == 0 else d5, 16)
                    (cd4 if dc % 2 == 0 else cd5).inc(16)

            @blk.tensor
            def _(te, tc=tc):
                n = 0
                for cg in range(24):
                    te.wait_ge(d2 if cg % 2 == 0 else d3,
                               (d2b if cg % 2 == 0 else d3b) + 32 * (cg // 2 + 1))
                    bg = pb[cg % 2]
                    bu = pb[2 + cg % 2]
                    if cg >= 2:
                        te.wait_ge(v0, v0b + cg - 1)
                    for kc in range(8):
                        mm = te.matmul(bg[:, 0:512], ctx["wsa"][:, kc * 128:(kc + 1) * 128],
                                       xn1(kc, tc * 512, 512), start=(kc == 0), stop=(kc == 7))
                    mm.then_inc(p0, 1)
                    for kc in range(8):
                        mm = te.matmul(bu[:, 0:512], ctx["wsb"][:, kc * 128:(kc + 1) * 128],
                                       xn1(kc, tc * 512, 512), start=(kc == 0), stop=(kc == 7))
                    mm.then_inc(p0, 1)
                    n += 2
                for dc in range(8):
                    te.wait_ge(d4 if dc % 2 == 0 else d5,
                               (d4b if dc % 2 == 0 else d5b) + 16 * (dc // 2 + 1))
                    bank = pb[4 + dc % 2]
                    if dc == 0:
                        te.wait_ge(v0, v0b + 24)
                    if dc >= 2:
                        te.wait_ge(s0, s0b + 23 + dc)
                    for hc in range(24):
                        mm = te.matmul(bank[:, 0:512], ar.bf("wsc", hc * 128, 128),
                                       hs(hc), start=(hc == 0), stop=(hc == 23))
                    mm.then_inc(p0, 1)
                    n += 1
                cp0.inc(n)

            @blk.scalar
            def _(sc, tc=tc):
                for cg in range(24):
                    sc.wait_ge(p0, p0b + 2 * cg + 2)
                    sc.activation(out=hs(cg), in_=pb[cg % 2][:, 0:512], func=AF.Silu,
                                  scale=1.0 / 127.0).then_inc(s0, 1)
                for dc in range(8):
                    sc.wait_ge(p0, p0b + 48 + dc + 1)
                    if dc >= 2:
                        sc.wait_ge(v0, v0b + 23 + dc)
                    sc.activation(out=ar.f32("ftmp", (dc % 2) * 512, 512),
                                  in_=pb[4 + dc % 2][:, 0:512], func=AF.Copy, scale=SC2).then_inc(s0, 1)
                cs0.inc(32)

            @blk.vector
            def _(v, tc=tc):
                for cg in range(24):
                    v.wait_ge(s0, s0b + cg + 1)
                    v.tensor_tensor(out=hs(cg), in0=hs(cg), in1=pb[2 + cg % 2][:, 0:512],
                                    op=ALU.mult).then_inc(v0, 1)
                for dc in range(8):
                    v.wait_ge(s0, s0b + 24 + dc + 1)
                    v.tensor_tensor(out=x1(dc, tc * 512, 512), in0=x1(dc, tc * 512, 512),
                                    in1=ar.f32("ftmp", (dc % 2) * 512, 512), op=ALU.add).then_inc(v0, 1)
                cv0.inc(32)

    ar.release("hslab", "ftmp", "wsc")


def ph_tail(ctx):
    nc = ctx["nc"]
    ar = ctx["ar"]
    pb = ctx["pb"]
    d2, cd2 = _sem(ctx, "d2")
    d3, cd3 = _sem(ctx, "d3")
    d4, cd4 = _sem(ctx, "d4")
    p0, cp0 = _sem(ctx, "p0")
    v0, cv0 = _sem(ctx, "v0")
    cc, ccc = _sem(ctx, "cc")
    xn1 = lambda kc, a, n: ar.bf("xn1", kc * TOK + a, n)
    x1 = lambda f, a, n: ar.f32("x1", f * TOK + a, n)
    ar.release("yfm")
    ar.alloc("res", 8 * TOK * 4)
    ar.alloc("ysl", 2 * TOK * 2)
    res = lambda f, a, n: ar.f32("res", f * TOK + a, n)
    ysl = lambda kc: ar.bf("ysl", kc * TOK, TOK)

    with nc.Block() as blk:
        d2b, p0b, v0b = cd2.v, cp0.v, cv0.v

        @blk.gpsimd
        def _(gp):
            gp.wait_ge(cc, ccc.v)
            rgE = gp.alloc_register()
            gp.reg_load(rgE, ctx["offs"][0:1, 1:2])
            gp.dma_start(out=ysl(0).rearrange("p t -> p 1 t").broadcast_to([128, 2, TOK])
                         if False else
                         ar.bf("ysl", 0, 2 * TOK).rearrange("p (kc t) -> p kc t", t=TOK),
                         in_=bass.AP(ctx["ys_out"], rgE,
                                     [[S, 128], [128 * S, 2], [1, TOK]])
                         ).then_inc(d2, 16)
            cd2.inc(16)
            gp.dma_start(out=ctx["wsa"][:, 0:1024],
                         in_=bass.AP(ctx["woT_h"], 0, [[E, 128], [1, E]])).then_inc(d2, 16)
            cd2.inc(16)
            gp.dma_start(out=ctx["wsb"][:, 0:1024],
                         in_=bass.AP(ctx["woT_h"], 128 * E, [[E, 128], [1, E]])).then_inc(d2, 16)
            cd2.inc(16)

        @blk.tensor
        def _(te):
            te.wait_ge(d2, d2b + 48)
            n = 0
            for dc in range(8):
                for tcc in range(4):
                    idx = dc * 4 + tcc
                    if idx >= 4:
                        te.wait_ge(v0, v0b + idx - 3)
                    bank = pb[idx % 4]
                    te.matmul(bank[:, 0:512], ctx["wsa"][:, dc * 128:(dc + 1) * 128],
                              ysl(0)[:, tcc * 512:(tcc + 1) * 512], start=True, stop=False)
                    te.matmul(bank[:, 0:512], ctx["wsb"][:, dc * 128:(dc + 1) * 128],
                              ysl(1)[:, tcc * 512:(tcc + 1) * 512], start=False, stop=False)
                    te.matmul(bank[:, 0:512],
                              ctx["biasrows"][0:1, 2560 + dc * 128:2560 + (dc + 1) * 128],
                              ctx["onerb"][0:1, 0:512], start=False, stop=True).then_inc(p0, 1)
                    n += 1
            cp0.inc(n)

        @blk.vector
        def _(v):
            for dc in range(8):
                for tcc in range(4):
                    idx = dc * 4 + tcc
                    v.wait_ge(p0, p0b + idx + 1)
                    v.tensor_tensor(out=res(dc, tcc * 512, 512), in0=pb[idx % 4][:, 0:512],
                                    in1=xn1(dc, tcc * 512, 512), op=ALU.add).then_inc(v0, 1)
            cv0.inc(32)

    ar.release("xn1", "ysl")
    gam = ctx["gbrow"]
    _emit_ln_fm(ctx, src_f32=res, dst_bf=None, n_tok=TOK,
                gamma_fn=lambda kc: gam[0:1, kc * 128:(kc + 1) * 128],
                beta_fn=lambda kc: gam[0:1, 1024 + kc * 128:1024 + (kc + 1) * 128],
                res_dst=lambda f, a, n: ar.f32("res", f * TOK + a, n),
                x1_acc=x1)

    ar.release("res")
    ar.alloc("otile", 2 * E * 4)
    otile = ar.f32("otile", 0, 2 * E)
    with (
        nc.Block() as blk,
    ):
        p0b, v0b, d3b, d4b = cp0.v, cv0.v, cd3.v, cd4.v

        @blk.tensor
        def _(te):
            for tt in range(16):
                for f in range(8):
                    idx = tt * 8 + f
                    if idx >= 4:
                        te.wait_ge(v0, v0b + idx - 3)
                    te.transpose(pb[idx % 4][:, 0:128], x1(f, tt * 128, 128),
                                 ctx["cst"][:, 0:128]).then_inc(p0, 1)
            cp0.inc(128)

        @blk.vector
        def _(v):
            for tt in range(16):
                if tt >= 2:
                    v.wait_ge(d3 if tt % 2 == 0 else d4,
                              (d3b if tt % 2 == 0 else d4b) + 16 * (tt // 2))
                for f in range(8):
                    idx = tt * 8 + f
                    v.wait_ge(p0, p0b + idx + 1)
                    v.tensor_copy(out=otile[:, (tt % 2) * E + f * 128:(tt % 2) * E + (f + 1) * 128],
                                  in_=pb[idx % 4][:, 0:128]).then_inc(v0, 1)
            cv0.inc(128)

        @blk.gpsimd
        def _(gp):
            for tt in range(16):
                gp.wait_ge(v0, v0b + (tt + 1) * 8)
                gp.dma_start(out=ctx["out_x2"][tt * 128:(tt + 1) * 128, :],
                             in_=otile[:, (tt % 2) * E:(tt % 2 + 1) * E]
                             ).then_inc(d3 if tt % 2 == 0 else d4, 16)
                (cd3 if tt % 2 == 0 else cd4).inc(16)
            gp.wait_ge(d3, cd3.v)
            gp.wait_ge(d4, cd4.v)


def ph_debug(ctx):
    nc = ctx["nc"]
    ar = ctx["ar"]
    d2, cd2 = _sem(ctx, "d2")
    with nc.Block() as blk:
        @blk.gpsimd
        def _(gp):
            srcs = {
                "xn0": lambda: ctx["xn0_tm"].ap()[1:4097, :],
                "qf": lambda: ar.bf("qf", 0, 4 * S),
                "kf": lambda: ar.bf("kf", 0, 4 * S),
                "vaug": lambda: ar.bf("vaug", 0, 32 * 520),
                "negm": lambda: ctx["sv"][:, 23:24],
                "st0dbg": lambda: ctx["ss0"][:],
                "svdbg": lambda: ctx["sv"][:, 8:24],
                "srowdbg": lambda: ctx["srow"][0:1, 8:45],
                "xt32dbg": lambda: ctx["arena_t"][:, 38912:43008].bitcast(F32),
                "tmpAdbg": lambda: ctx["arena_t"][:, 34816:38912].bitcast(F32),
                "adT": lambda: ctx["adT"][:],
                "brow": lambda: ctx["brow"][:],
                "y0": lambda: ctx["y0_in"].ap(),
                "x1": lambda: ar.f32("x1", 0, 8 * TOK),
                "xn1": lambda: ar.bf("xn1", 0, 8 * TOK),
                "uown": lambda: ctx["u_in"].ap().rearrange("a b -> (a b)").rearrange(
                    "(p f) -> p f", p=128),
                "yfm": lambda: ar.bf("yfm", 0, S),
                "res": lambda: ar.f32("res", 0, 8 * TOK),
                "xn0e": lambda: ar.bf("xn0e", 0, 8 * EXT),
            }
            for nm, ap in ctx["dbg"].items():
                try:
                    src = srcs[nm]()
                except (KeyError, RuntimeError):
                    continue
                gp.dma_start(out=ap, in_=src).then_inc(d2, 16)
                cd2.inc(16)
            gp.wait_ge(d2, cd2.v)
    if N_PHASES < 12:
        with nc.Block() as blk:
            @blk.gpsimd
            def _(gp):
                gp.dma_start(out=ctx["out_x2"].rearrange("a b -> (a b)").rearrange(
                    "(p f) -> p f", p=128),
                    in_=ctx["arena_t"][:, 0:2 * E * TOK // 128].bitcast(F32)
                    ).then_inc(d2, 16)
                cd2.inc(16)
                gp.wait_ge(d2, cd2.v)


_NC_CACHE = {}


def _host_prepare(inputs):
    x = np.ascontiguousarray(inputs["L0_attn_x"])
    Wa = inputs["L0_attn_W_attn"]; ba = inputs["L0_attn_b_attn"]
    id32 = np.eye(128, dtype=np.float32)
    eye2 = (2 * np.eye(128)).astype(np.float32)
    masks = np.zeros((128, 4 * 512), bnp16)
    for r in range(4):
        k_idx = np.arange(128)[:, None] + r * 128
        q_idx = np.arange(512)[None, :]
        masks[:, r * 512:(r + 1) * 512] = (k_idx <= q_idx).astype(bnp16)
    shared = dict(
        wproj=np.ascontiguousarray(inputs["L0_attn_W_proj"]),
        bproj_row=np.ascontiguousarray(inputs["L0_attn_b_proj"][None, :].astype(bnp16)),
        wrT=np.ascontiguousarray(inputs["L0_ffn_W_r"].T),
        wkT=np.ascontiguousarray(inputs["L0_ffn_W_k"].T),
        wvT=np.ascontiguousarray(inputs["L0_ffn_W_v"].T),
        mur=np.ascontiguousarray(inputs["L0_ffn_mu_r"].reshape(8, 128).T.astype(np.float32)),
        muk=np.ascontiguousarray(inputs["L0_ffn_mu_k"].reshape(8, 128).T.astype(np.float32)),
        wiT=np.ascontiguousarray(inputs["L1_attn_Wi"].T),
        bi_row=np.ascontiguousarray(inputs["L1_attn_bi"][None, :].astype(bnp16)),
        woT=np.ascontiguousarray(inputs["L1_attn_Wo"].T),
        bo_row=np.ascontiguousarray(inputs["L1_attn_bo"][None, :].astype(bnp16)),
        gam_row=np.ascontiguousarray(inputs["L1_attn_gamma"][None, :].astype(bnp16)),
        bet_row=np.ascontiguousarray(inputs["L1_attn_beta"][None, :].astype(bnp16)),
        wgT=np.ascontiguousarray(inputs["L1_ffn_w_gate"].T),
        wuT=np.ascontiguousarray(inputs["L1_ffn_w_up"].T),
        wdT=np.ascontiguousarray(inputs["L1_ffn_w_down"].T),
        a_hip=np.ascontiguousarray(inputs["L1_attn_A_hippo"]),
        id32=id32, eye2=eye2,
        ones_col=np.ones((128, 1), np.float32),
        ones_colb=np.ones((128, 1), bnp16),
        ones_rowb=np.ones((1, 512), bnp16),
        masks=masks,
        hsel=np.concatenate([np.repeat([[1, 0]], 64, 1).astype(np.float32).reshape(1, 128),
                             np.repeat([[0, 1]], 64, 1).astype(np.float32).reshape(1, 128)]),
        ones_rowf=np.ones((1, 128), np.float32),
    )
    maps = []
    for c in range(8):
        b, gg = c // 2, c % 2
        hsl = slice(gg * 512, (gg + 1) * 512)
        m = dict(shared)
        m.update(
            x_b=np.ascontiguousarray(x[b]),
            x_own=np.ascontiguousarray(x[b, gg * TOK:(gg + 1) * TOK]),
            x_bh=np.ascontiguousarray(x[b].astype(bnp16)),
            wattn=np.ascontiguousarray(np.concatenate(
                [Wa[:, hsl], Wa[:, 1024 + gg * 512:1024 + (gg + 1) * 512],
                 Wa[:, 2048 + gg * 512:2048 + (gg + 1) * 512]], axis=1)),
            battn_row=np.ascontiguousarray(np.concatenate(
                [ba[hsl], ba[1024 + gg * 512:1024 + (gg + 1) * 512],
                 ba[2048 + gg * 512:2048 + (gg + 1) * 512]])[None, :].astype(bnp16)),
            bp_g=np.ascontiguousarray(inputs["L1_attn_Bp"][gg]),
            cv_g=np.ascontiguousarray(inputs["L1_attn_Cp"][gg, 0][:, None]),
            ldt_g=np.array([[inputs["L1_attn_log_dt"][gg]]], np.float32),
            sc_g=np.array([[inputs["L1_attn_scale"][gg]]], np.float32),
            off_xel=np.array([[gg * TOK * E]], np.int32),
            off_y0=np.array([[gg * TOK]], np.int32),
            off_uel=np.array([[gg * 128]], np.int32),
        )
        maps.append(m)
    return maps


def kernel(**inputs):
    maps = _host_prepare(inputs)
    key = (N_PHASES, tuple(DBG), SCAN_T, AR_KB)
    if key not in _NC_CACHE:
        _NC_CACHE[key] = build_nc()
    nc = _NC_CACHE[key]
    res = run_bass_kernel_spmd(nc, maps, core_ids=list(range(8)))
    out = np.empty((B, S, E), np.float32)
    for c in range(8):
        b, gg = c // 2, c % 2
        out[b, gg * TOK:(gg + 1) * TOK, :] = res.results[c]["out_x2"]
    globals()["_LAST_RES"] = res
    return out


# version 04
# speedup vs baseline: 1.1205x; 1.1205x over previous
import os
import contextlib
import numpy as np
import ml_dtypes
import concourse.bass as bass
import concourse.mybir as mybir
from concourse.bass_utils import run_bass_kernel_spmd
from concourse.aot_env import allow_ici_all_to_all

F32 = mybir.dt.float32
BF16 = mybir.dt.bfloat16
I32 = mybir.dt.int32
AF = mybir.ActivationFunctionType
ALU = mybir.AluOpType
AX = mybir.AxisListType
Pool = mybir.EngineType.Pool

bnp16 = ml_dtypes.bfloat16

B, S, E = 4, 4096, 1024
TOK = 2048
NST = 128
HID = 3072
EXT = 2176
NS_ITERS = 25

N_PHASES = int(os.environ.get("KPHASES", "99"))
DBG = [d for d in os.environ.get("KDBG", "").split(",") if d]
SCAN_T = int(os.environ.get("KSCANT", str(S)))
AR_KB = int(os.environ.get("KARENA", "180"))

PHASE_NAMES = ["prep", "ns", "qkv", "kquant", "vquant", "mhat", "attn",
               "l0tail", "l1head", "scan", "ffn1", "tail"]


class Ctr:
    def __init__(self):
        self.v = 0

    def inc(self, k=1):
        self.v += k
        return self.v


class Arena:
    def __init__(self, tensor, nbytes):
        self.t = tensor
        self.regions = {}
        self.free = [(0, nbytes)]

    def alloc(self, name, nbytes):
        nbytes = (nbytes + 3) & ~3
        for i, (o, sz) in enumerate(self.free):
            if sz >= nbytes:
                self.regions[name] = (o, nbytes)
                if sz == nbytes:
                    self.free.pop(i)
                else:
                    self.free[i] = (o + nbytes, sz - nbytes)
                return
        raise RuntimeError(f"arena OOM for {name} ({nbytes}B); free={self.free}")

    def release(self, *names):
        for name in names:
            o, sz = self.regions.pop(name)
            self.free.append((o, sz))
        self.free.sort()
        out = []
        for seg in self.free:
            if out and out[-1][0] + out[-1][1] == seg[0]:
                out[-1] = (out[-1][0], out[-1][1] + seg[1])
            else:
                out.append(list(seg))
        self.free = [tuple(s) for s in out]

    def bf(self, name, off_el=0, cols=None):
        o, sz = self.regions[name]
        n = (sz // 2 - off_el) if cols is None else cols
        return self.t[:, o // 2 + off_el: o // 2 + off_el + n]

    def f32(self, name, off_el=0, cols=None):
        o, sz = self.regions[name]
        ap = self.t[:, o // 2: o // 2 + sz // 2].bitcast(F32)
        n = (sz // 4 - off_el) if cols is None else cols
        return ap[:, off_el: off_el + n]

    def i32(self, name, off_el=0, cols=None):
        return self.f32(name, off_el, cols).bitcast(I32)


def build_nc():
    nc = bass.Bass("TRN2", target_bir_lowering=False)
    stack = contextlib.ExitStack()
    ctx = {"nc": nc}

    def inp(name, shape, dt=F32):
        t = nc.dram_tensor(name, shape, dt, kind="ExternalInput")
        ctx[name] = t.ap()
        ctx[name + "_h"] = t

    inp("x_b", [S, E]); inp("x_bh", [S, E], BF16)
    inp("x_own", [TOK, E])
    inp("wattn", [E, 1536]); inp("battn_row", [1, 1536], BF16)
    inp("wproj", [E, E]); inp("bproj_row", [1, E], BF16)
    inp("wrT", [E, E]); inp("wkT", [E, E]); inp("wvT", [E, E])
    inp("mur", [128, 8]); inp("muk", [128, 8])
    inp("wiT", [E, 256]); inp("bi_row", [1, 256], BF16)
    inp("woT", [256, E]); inp("bo_row", [1, E], BF16)
    inp("gam_row", [1, E], BF16); inp("bet_row", [1, E], BF16)
    inp("wgT", [E, HID], I32); inp("wuT", [E, HID], I32); inp("wdT", [HID, E], I32)
    inp("a_hip", [NST, NST]); inp("bp_g", [NST, 1]); inp("cv_g", [NST, 1])
    inp("ldt_g", [1, 1]); inp("sc_g", [1, 1])
    inp("id32", [128, 128]); inp("eye2", [128, 128])
    inp("ones_col", [128, 1]); inp("ones_colb", [128, 1], BF16)
    inp("ones_rowb", [1, 512], BF16)
    inp("masks", [128, 4 * 512], BF16)
    inp("hsel", [2, 128])
    inp("ones_rowf", [1, 128])
    inp("off_xel", [1, 1], I32)
    inp("off_y0", [1, 1], I32)
    inp("off_uel", [1, 1], I32)

    ctx["out_x2"] = nc.dram_tensor("out_x2", [TOK, E], F32, kind="ExternalOutput").ap()
    ctx["dbg"] = {}
    dbg_specs = {
        "xn0": ([4096, E], BF16), "qf": ([128, 4 * S], BF16), "kf": ([128, 4 * S], BF16),
        "vaug": ([128, 32 * 520], BF16), "negm": ([128, 1], F32),
        "y0": ([512, S], BF16), "x1": ([128, 8 * TOK], F32), "xn1": ([128, 8 * TOK], BF16),
        "adT": ([128, 128], BF16), "brow": ([1, 128], BF16), "uown": ([128, 16 * 256], BF16),
        "yfm": ([128, S], BF16), "res": ([128, 8 * TOK], F32), "xn0e": ([128, 8 * EXT], BF16),
        "st0dbg": ([128, 128], BF16), "xt32dbg": ([128, 2048], F32), "tmpAdbg": ([128, 2048], F32),
        "svdbg": ([128, 16], F32), "srowdbg": ([1, 37], F32),
    }
    for nm in DBG:
        shp, dt = dbg_specs[nm]
        ctx["dbg"][nm] = nc.dram_tensor("dbg_" + nm, shp, dt, kind="ExternalOutput").ap()

    for nm, shp, dt in [("xn0_tm", [4352, E], BF16), ("xn0_own_tm", [EXT, E], BF16),
                        ("u_g", [S, 128], BF16),
                        ("y0_in", [512, S], BF16), ("y0_out", [1024, S], BF16),
                        ("u_in", [TOK, 256], BF16), ("u_out", [S, 256], BF16),
                        ("ys_in", [NST, S], BF16), ("ys_out", [2 * NST, S], BF16)]:
        ctx[nm] = nc.dram_tensor(nm, shp, dt)

    ctx["sems"] = {}
    ctx["ctrs"] = {}
    for name in ["d0", "d1", "d2", "d3", "d4", "d5", "p0", "v0", "s0", "g0", "cc"]:
        ctx["sems"][name] = stack.enter_context(nc.semaphore(name))
        ctx["ctrs"][name] = Ctr()

    def sbuf(name, shape, dt=F32):
        t = stack.enter_context(nc.sbuf_tensor(name, shape, dt))
        ctx[name] = t
        return t

    sbuf("cst", [128, 256], F32)
    sbuf("onec", [128, 1], F32)
    sbuf("onecb", [128, 1], BF16)
    sbuf("onerb", [1, 512], BF16)
    sbuf("sv", [128, 32], F32)
    sbuf("srow", [1, 64], F32)
    sbuf("biasrows", [1, 4096], BF16)
    sbuf("gbrow", [1, 2048], BF16)
    sbuf("adT", [128, 128], BF16)
    sbuf("brow", [1, 128], BF16)
    sbuf("ccol", [128, 1], BF16)
    sbuf("ss0", [128, 128], BF16)
    sbuf("ss1", [128, 128], BF16)
    sbuf("wsa", [128, 1024], BF16)
    sbuf("wsb", [128, 1024], BF16)
    sbuf("eringb", [128, 2 * 512], BF16)
    sbuf("offs", [1, 4], I32)
    sbuf("hselt", [2, 128])
    sbuf("onerf", [1, 128])
    sbuf("mixmu", [128, 16])

    arena_t = stack.enter_context(nc.sbuf_tensor("arena", [128, AR_KB * 512], BF16))
    ctx["ar"] = Arena(arena_t, AR_KB * 1024)
    ctx["arena_t"] = arena_t

    ctx["pb"] = [stack.enter_context(nc.psum_tensor(f"pb{i}", [128, 512], F32))
                 for i in range(8)]

    with stack:
        phases = [ph_prep, ph_ns, ph_qkv, ph_kquant, ph_vquant, ph_mhat, ph_attn,
                  ph_l0tail, ph_l1head, ph_scan, ph_ffn1, ph_tail]
        for i, ph in enumerate(phases):
            if i < N_PHASES:
                ph(ctx)
        ph_debug(ctx)
    return nc


def _sem(ctx, n):
    return ctx["sems"][n], ctx["ctrs"][n]


def ph_prep(ctx):
    KPREP = int(os.environ.get("KPREP", "4"))
    KNT = int(os.environ.get("KNT", "32"))
    nc = ctx["nc"]
    ar = ctx["ar"]
    d0, cd0 = _sem(ctx, "d0")
    d1, cd1 = _sem(ctx, "d1")
    d2, cd2 = _sem(ctx, "d2")
    v0, cv0 = _sem(ctx, "v0")
    s0, cs0 = _sem(ctx, "s0")
    g0, cg0 = _sem(ctx, "g0")
    NT = KNT
    ar.alloc("maskt", 4 * 512 * 2)
    ar.alloc("xn0full", 8 * S * 2)
    ar.alloc("tmpA", 8 * 1024)
    ar.alloc("preploc", (2 * E) * 4 + (2 * E) * 2 + (2 * E) * 2 + E * 2)

    d3, cd3 = _sem(ctx, "d3")
    d4, cd4 = _sem(ctx, "d4")
    d5, cd5 = _sem(ctx, "d5")
    v0b, s0b = cv0.v, cs0.v
    d4b, d5b = cd4.v, cd5.v
    const_list = [
        ("id32", ctx["cst"][:, 0:128]), ("eye2", ctx["cst"][:, 128:256]),
        ("ones_col", ctx["onec"][:]), ("ones_colb", ctx["onecb"][:]),
        ("ones_rowb", ctx["onerb"][:]), ("masks", ar.bf("maskt", 0, 4 * 512)),
        ("bp_g", ctx["sv"][:, 0:1]), ("cv_g", ctx["sv"][:, 1:2]),
        ("ldt_g", ctx["sv"][0:1, 2:3]), ("sc_g", ctx["sv"][0:1, 3:4]),
        ("battn_row", ctx["biasrows"][0:1, 0:1536]),
        ("bproj_row", ctx["biasrows"][0:1, 1536:2560]),
        ("bo_row", ctx["biasrows"][0:1, 2560:3584]),
        ("bi_row", ctx["biasrows"][0:1, 3584:3840]),
        ("gam_row", ctx["gbrow"][0:1, 0:1024]),
        ("bet_row", ctx["gbrow"][0:1, 1024:2048]),
        ("hsel", ctx["hselt"][:]),
        ("ones_rowf", ctx["onerf"][:]),
        ("mur", ctx["mixmu"][:, 0:8]),
        ("muk", ctx["mixmu"][:, 8:16]),
        ("off_xel", ctx["offs"][0:1, 0:1]),
        ("off_y0", ctx["offs"][0:1, 1:2]),
        ("off_uel", ctx["offs"][0:1, 2:3]),
    ]
    d0b = cd0.v + len(const_list) * 16
    d1b = cd1.v
    xt32 = ar.f32("preploc", 0, 2 * E)
    xtb = ar.bf("preploc", 4 * E, 2 * E)
    xnt = ar.bf("preploc", 6 * E, 2 * E)
    zrow = ar.bf("preploc", 8 * E, E)[0:1, :]
    sqd = ar.f32("tmpA", 0, 2 * E)
    with (
        nc.sbuf_tensor("st0", [128, 16], F32) as st0,
        nc.sbuf_tensor("st1", [128, 2], F32) as st1,
        nc.Block() as blk,
    ):
        @blk.sync
        def _(sy):
            for src, dst in const_list:
                sy.dma_start(out=dst, in_=ctx[src]).then_inc(d0, 16); cd0.inc(16)
            for i in range(NT):
                if i >= 2:
                    sy.wait_ge(v0, v0b + 2 * (i - 1))
                    sy.wait_ge(s0, s0b + 2 * (i - 2) + 1)
                dsem = d0 if i % 2 == 0 else d1
                sy.dma_start(out=xt32[:, (i % 2) * E:(i % 2 + 1) * E],
                             in_=ctx["x_b"][i * 128:(i + 1) * 128, :]).then_inc(dsem, 16)
                sy.dma_start(out=xtb[:, (i % 2) * E:(i % 2 + 1) * E],
                             in_=ctx["x_bh"][i * 128:(i + 1) * 128, :]).then_inc(dsem, 16)
                (cd0 if i % 2 == 0 else cd1).inc(32)

        @blk.scalar
        def _(sc):
            if KPREP < 2:
                return
            for i in range(NT):
                sl = slice((i % 2) * E, (i % 2 + 1) * E)
                o = (i % 2) * 8
                sc.wait_ge(d0 if i % 2 == 0 else d1,
                           (d0b if i % 2 == 0 else d1b) + 32 * (i // 2 + 1))
                if i >= 2:
                    sc.wait_ge(v0, v0b + 2 * (i - 1) - 1)
                sc.activation(out=sqd[:, (i % 2) * E:(i % 2 + 1) * E], in_=xt32[:, sl], func=AF.Square).then_inc(s0, 1)
                sc.wait_ge(v0, v0b + 2 * i + 1)
                sc.activation(out=st1[:, (i % 2):(i % 2) + 1], in_=st0[:, 3 + o:4 + o], func=AF.Sqrt).then_inc(s0, 1)
            cs0.inc(2 * NT)

        @blk.vector
        def _(v):
            if KPREP < 2:
                v.drain(fusable=False).then_inc(g0, 1)
                return
            v.memset(zrow[:], 0.0)
            v.memset(st0[:], 0.0)
            v.memset(ctx["ss1"][:], 0.0)
            v.memset(ctx["ss0"][:], 0.0).then_inc(g0, 1)
            v.drain()
            for i in range(NT):
                sl = slice((i % 2) * E, (i % 2 + 1) * E)
                o = (i % 2) * 8
                v.wait_ge(s0, s0b + 2 * i + 1)
                v.wait_ge(d0 if i % 2 == 0 else d1,
                          (d0b if i % 2 == 0 else d1b) + 32 * (i // 2 + 1))
                v.tensor_reduce(out=st0[:, 1 + o:2 + o], in_=sqd[:, (i % 2) * E:(i % 2 + 1) * E],
                                axis=AX.X, op=ALU.add)
                v.tensor_reduce(out=st0[:, 0 + o:1 + o], in_=xt32[:, sl], axis=AX.X, op=ALU.add)
                v.drain()
                v.tensor_scalar(out=st0[:, 4 + o:5 + o], in0=st0[:, 0 + o:1 + o],
                                scalar1=1.0 / E, scalar2=None, op0=ALU.mult)
                v.tensor_scalar(out=st0[:, 2 + o:3 + o], in0=st0[:, 1 + o:2 + o],
                                scalar1=1.0 / E, scalar2=None, op0=ALU.mult)
                v.drain()
                v.tensor_copy(out=st0[:, 0 + o:1 + o], in_=st0[:, 4 + o:5 + o])
                v.tensor_tensor(out=st0[:, 7 + o:8 + o], in0=st0[:, 4 + o:5 + o],
                                in1=st0[:, 4 + o:5 + o], op=ALU.mult)
                v.drain()
                v.tensor_tensor(out=st0[:, 3 + o:4 + o], in0=st0[:, 2 + o:3 + o],
                                in1=st0[:, 7 + o:8 + o], op=ALU.subtract)
                v.drain()
                v.tensor_scalar(out=st0[:, 3 + o:4 + o], in0=st0[:, 3 + o:4 + o],
                                scalar1=1e-5, scalar2=None, op0=ALU.add).then_inc(v0, 1)
                v.wait_ge(s0, s0b + 2 * i + 2)
                v.reciprocal(out=st0[:, 5 + o:6 + o], in_=st1[:, (i % 2):(i % 2) + 1])
                v.drain()
                v.tensor_tensor(out=st0[:, 6 + o:7 + o], in0=st0[:, 0 + o:1 + o],
                                in1=st0[:, 5 + o:6 + o], op=ALU.mult)
                v.drain()
                v.tensor_scalar(out=st0[:, 6 + o:7 + o], in0=st0[:, 6 + o:7 + o],
                                scalar1=-1.0, scalar2=None, op0=ALU.mult)
                v.drain()
                if i >= 2:
                    j = i - 2
                    if j % 2 == 0:
                        v.wait_ge(d4, d4b + 32 + 16 * (j // 2 + 1))
                    else:
                        v.wait_ge(d5, d5b + 16 * (j // 2 + 1))
                v.tensor_tensor(out=xnt[:, sl], in0=xt32[:, sl],
                                in1=st0[:, 5 + o:6 + o].broadcast_to([128, E]), op=ALU.mult)
                v.drain()
                if i == 0:
                    v.tensor_copy(out=ctx["ss0"][:, 0:16], in_=st0[:, 0:16])
                    v.drain()
                if i == NT - 1 and os.environ.get("KRERED"):
                    v.tensor_reduce(out=st0[:, 8:9], in_=xt32[:, 0:E], axis=AX.X, op=ALU.add)
                    v.tensor_reduce(out=st0[:, 9:10], in_=sqd[:, 0:E], axis=AX.X, op=ALU.add)
                    v.drain()
                    v.tensor_copy(out=ctx["ss0"][:, 16:32], in_=st0[:, 0:16])
                    v.tensor_copy(out=ctx["ss0"][:, 32:34], in_=st0[:, 8:10])
                    v.drain()
                v.tensor_tensor(out=xnt[:, sl], in0=xnt[:, sl],
                                in1=st0[:, 6 + o:7 + o].broadcast_to([128, E]),
                                op=ALU.add).then_inc(v0, 1)
            cv0.inc(2 * NT)
        cg0.inc()

        @blk.gpsimd
        def _(gp):
            if KPREP < 3:
                return
            gp.wait_ge(g0, cg0.v)
            gp.dma_start(out=ctx["xn0_tm"].ap()[0:1, :], in_=zrow[:]).then_inc(d4, 16); cd4.inc(16)
            gp.dma_start(out=bass.AP(ctx["xn0_tm"], 4097 * E, [[0, 1], [E, 255], [1, E]]),
                         in_=zrow.unsqueeze(1).broadcast_to([1, 255, E])
                         ).then_inc(d4, 16); cd4.inc(16)
            for i in range(NT):
                sl = slice((i % 2) * E, (i % 2 + 1) * E)
                gp.wait_ge(v0, v0b + 2 * i + 2)
                gp.dma_start(out=ctx["xn0_tm"].ap()[1 + i * 128:1 + (i + 1) * 128, :],
                             in_=xnt[:, sl]).then_inc(d4 if i % 2 == 0 else d5, 16)
                (cd4 if i % 2 == 0 else cd5).inc(16)

    if KPREP < 4:
        return
    with nc.Block() as blk:
        @blk.gpsimd
        def _(gp):
            gp.wait_ge(d0, cd0.v)
            gp.wait_ge(d4, cd4.v)
            gp.wait_ge(d5, cd5.v)
            rgA = gp.alloc_register()
            gp.reg_load(rgA, ctx["offs"][0:1, 0:1])
            gp.dma_start(out=ctx["xn0_own_tm"].ap(),
                         in_=bass.AP(ctx["xn0_tm"], rgA, [[E, EXT], [1, E]])
                         ).then_inc(d2, 16)
            cd2.inc(16)

        @blk.sync
        def _(sy):
            sy.wait_ge(d2, cd2.v)
            for f in range(8):
                sy.dma_start_transpose(ar.bf("xn0full", f * S, S),
                                       ctx["xn0_tm"].ap()[1:4097, f * 128:(f + 1) * 128]
                                       ).then_inc(d0, 16)
                cd0.inc(16)

        @blk.scalar
        def _(sc):
            sc.wait_ge(d0, cd0.v)

    ar.release("preploc")


def ph_ns(ctx):
    nc = ctx["nc"]
    pb = ctx["pb"]
    p0, cp0 = _sem(ctx, "p0")
    v0, cv0 = _sem(ctx, "v0")
    s0, cs0 = _sem(ctx, "s0")
    g0, cg0 = _sem(ctx, "g0")
    d0, cd0 = _sem(ctx, "d0")
    ID = ctx["cst"][:, 0:128]
    EY2 = ctx["cst"][:, 128:256]
    sv = ctx["sv"]
    srow = ctx["srow"]

    with (
        nc.sbuf_tensor("ns", [128, 8 * 128], F32) as ns,
        nc.Block() as blk,
    ):
        C = lambda k: ns[:, k * 128:(k + 1) * 128]

        @blk.sync
        def _(sy):
            sy.dma_start(out=C(7), in_=ctx["a_hip"]).then_inc(d0, 16); cd0.inc(16)

        @blk.scalar
        def _(sc):
            sc.wait_ge(d0, cd0.v)
            sc.activation(out=sv[0:1, 4:5], in_=sv[0:1, 2:3], func=AF.Exp).then_inc(s0, 1)
            cs0.inc()

        def pe(fn):
            @blk.tensor
            def _(te):
                fn(te)

        def ve(fn):
            @blk.vector
            def _(v):
                fn(v)

        def do_v(fn):
            def g(v):
                fn(v)
                v.drain(fusable=False).then_inc(v0, 1)
            ve(g); cv0.inc()

        def do_p(fn):
            def g(te):
                fn(te)
                te.drain(fusable=False).then_inc(p0, 1)
            pe(g); cp0.inc()

        do_p(lambda te: (te.wait_ge(s0, cs0.v),
                         te.matmul(pb[1][:, 0:2], ctx["onerf"][:], sv[0:1, 3:5],
                                   start=True, stop=True)))
        do_v(lambda v: (v.wait_ge(p0, cp0.v),
                        v.tensor_copy(out=sv[:, 5:6], in_=pb[1][:, 1:2]),
                        v.tensor_copy(out=sv[:, 6:7], in_=pb[1][:, 0:1])))
        do_v(lambda v: (v.tensor_tensor(out=C(0), in0=C(7), in1=sv[:, 6:7].broadcast_to([128, 128]), op=ALU.mult),
                        v.tensor_scalar(out=sv[:, 7:8], in0=sv[:, 5:6], scalar1=0.5, scalar2=None, op0=ALU.mult),
                        v.drain(),
                        v.tensor_tensor(out=C(0), in0=C(0), in1=sv[:, 7:8].broadcast_to([128, 128]), op=ALU.mult)))
        do_p(lambda te: (te.wait_ge(v0, cv0.v), te.transpose(pb[0][:, 0:128], C(0), ID)))
        do_v(lambda v: (v.wait_ge(p0, cp0.v),
                        v.tensor_tensor(out=C(1), in0=ID, in1=C(0), op=ALU.add),
                        v.tensor_tensor(out=C(2), in0=ID, in1=pb[0][:, 0:128], op=ALU.add),
                        v.tensor_tensor(out=C(3), in0=ID, in1=C(0), op=ALU.subtract)))

        @blk.scalar
        def _(sc):
            sc.wait_ge(v0, cv0.v)
            sc.activation(out=C(7), in_=C(1), func=AF.Abs).then_inc(s0, 1)
            cs0.inc()

        do_p(lambda te: (te.wait_ge(s0, cs0.v),
                         te.matmul(pb[1][0:1, 0:128], ctx["onec"][:], C(7), start=True, stop=True),
                         te.transpose(pb[0][:, 128:256], C(7), ID)))
        do_v(lambda v: (v.wait_ge(p0, cp0.v), v.tensor_copy(out=C(7), in_=pb[0][:, 128:256])))
        do_p(lambda te: (te.wait_ge(v0, cv0.v),
                         te.matmul(pb[1][0:1, 128:256], ctx["onec"][:], C(7), start=True, stop=True)))
        do_v(lambda v: (v.wait_ge(p0, cp0.v),
                        v.tensor_reduce(out=srow[0:1, 0:1], in_=pb[1][0:1, 0:128], axis=AX.X, op=ALU.max),
                        v.tensor_reduce(out=srow[0:1, 1:2], in_=pb[1][0:1, 128:256], axis=AX.X, op=ALU.max),
                        v.drain(),
                        v.tensor_tensor(out=srow[0:1, 2:3], in0=srow[0:1, 0:1], in1=srow[0:1, 1:2], op=ALU.mult),
                        v.drain(),
                        v.reciprocal(out=srow[0:1, 3:4], in_=srow[0:1, 2:3])))

        do_p(lambda te: (te.wait_ge(v0, cv0.v),
                         te.matmul(pb[1][:, 2:3], ctx["onerf"][:], srow[0:1, 3:4],
                                   start=True, stop=True)))
        do_v(lambda v: (v.wait_ge(p0, cp0.v),
                        v.tensor_copy(out=sv[:, 8:9], in_=pb[1][:, 2:3])))
        do_v(lambda v: (v.tensor_tensor(out=C(4), in0=C(2), in1=sv[:, 8:9].broadcast_to([128, 128]), op=ALU.mult),
                        v.tensor_tensor(out=C(5), in0=C(1), in1=sv[:, 8:9].broadcast_to([128, 128]), op=ALU.mult)))

        for it in range(NS_ITERS):
            do_p(lambda te: (te.wait_ge(v0, cv0.v),
                             te.matmul(pb[2][:, 0:128], C(2), C(4), start=True, stop=True)))
            do_v(lambda v: (v.wait_ge(p0, cp0.v),
                            v.tensor_tensor(out=C(6), in0=EY2, in1=pb[2][:, 0:128], op=ALU.subtract)))
            do_p(lambda te: (te.wait_ge(v0, cv0.v),
                             te.matmul(pb[2][:, 128:256], C(6), C(5), start=True, stop=True)))
            do_v(lambda v: (v.wait_ge(p0, cp0.v), v.tensor_copy(out=C(5), in_=pb[2][:, 128:256])))
            do_p(lambda te: (te.wait_ge(v0, cv0.v),
                             te.transpose(pb[2][:, 256:384], C(5), ID)))
            do_v(lambda v: (v.wait_ge(p0, cp0.v), v.tensor_copy(out=C(4), in_=pb[2][:, 256:384])))

        do_v(lambda v: v.tensor_tensor(out=sv[:, 9:10], in0=sv[:, 0:1], in1=sv[:, 5:6], op=ALU.mult))
        do_p(lambda te: (te.wait_ge(v0, cv0.v),
                         te.matmul(pb[3][:, 0:128], C(3), C(5), start=True, stop=True),
                         te.matmul(pb[3][:, 128:129], C(5), sv[:, 9:10], start=True, stop=True)))
        do_v(lambda v: (v.wait_ge(p0, cp0.v),
                        v.tensor_copy(out=ctx["adT"][:], in_=pb[3][:, 0:128]),
                        v.tensor_copy(out=sv[:, 10:11], in_=pb[3][:, 128:129]),
                        v.tensor_copy(out=ctx["ccol"][:], in_=sv[:, 1:2])))
        do_p(lambda te: (te.wait_ge(v0, cv0.v),
                         te.transpose(pb[3][0:1, 256:384], sv[:, 10:11], ID)))
        do_v(lambda v: (v.wait_ge(p0, cp0.v),
                        v.tensor_copy(out=ctx["brow"][:], in_=pb[3][0:1, 256:384])))


def ph_qkv(ctx):
    nc = ctx["nc"]
    ar = ctx["ar"]
    ar.alloc("qf", 4 * S * 2)
    ar.alloc("kf", 4 * S * 2)
    ar.alloc("vaug", 32 * 520 * 2)
    pb = ctx["pb"]
    d0, cd0 = _sem(ctx, "d0")
    d2, cd2 = _sem(ctx, "d2")
    d3, cd3 = _sem(ctx, "d3")
    p0, cp0 = _sem(ctx, "p0")
    v0, cv0 = _sem(ctx, "v0")
    s0, cs0 = _sem(ctx, "s0")
    xn0 = lambda kc, a, n: ar.bf("xn0full", kc * S + a, n)
    qfv = lambda c, a, n: ar.bf("qf", c * S + a, n)
    kfv = lambda c, a, n: ar.bf("kf", c * S + a, n)
    vaug = lambda a, n: ar.bf("vaug", a, n)
    cum = lambda ncg: 8 * min(ncg, 8) + 32 * max(0, ncg - 8)

    p0b, v0b, s0b = cp0.v, cv0.v, cs0.v
    d2b, d3b = cd2.v, cd3.v
    with nc.Block() as blk:
        @blk.vector
        def _(v):
            v.memset(vaug(0, 32 * 520), 1.0).then_inc(v0, 1)
        cv0.inc()
        v0b = cv0.v

        @blk.gpsimd
        def _(gp):
            for cg in range(12):
                buf = ctx["wsa"] if cg % 2 == 0 else ctx["wsb"]
                if cg >= 2:
                    gp.wait_ge(p0, p0b + cum(cg - 1))
                gp.dma_start(out=buf[:, 0:1024],
                             in_=bass.AP(ctx["wattn_h"], cg * 128,
                                         [[1536, 128], [128 * 1536, 8], [1, 128]])
                             ).then_inc(d2 if cg % 2 == 0 else d3, 16)
                (cd2 if cg % 2 == 0 else cd3).inc(16)

        @blk.tensor
        def _(te):
            te.wait_ge(v0, v0b)
            te.wait_ge(d0, cd0.v)
            n = 0
            for cg in range(12):
                buf = ctx["wsa"] if cg % 2 == 0 else ctx["wsb"]
                te.wait_ge(d2 if cg % 2 == 0 else d3,
                           (d2b if cg % 2 == 0 else d3b) + 16 * (cg // 2 + 1))
                if cg < 8:
                    for tc in range(8):
                        j = cg * 8 + tc
                        if j >= 4:
                            te.wait_ge(s0, s0b + (j - 3))
                        bank = pb[tc % 4]
                        for kc in range(8):
                            te.matmul(bank[:, 0:512], buf[:, kc * 128:(kc + 1) * 128],
                                      xn0(kc, tc * 512, 512), start=(kc == 0), stop=False)
                        te.matmul(bank[:, 0:512], ctx["biasrows"][0:1, cg * 128:(cg + 1) * 128],
                                  ctx["onerb"][0:1, 0:512], start=False, stop=True).then_inc(p0, 1)
                        n += 1
                else:
                    for tc in range(8):
                        for tt in range(4):
                            idx = (cg - 8) * 32 + tc * 4 + tt
                            if idx >= 2:
                                te.wait_ge(v0, v0b + (idx - 1))
                            t0 = tc * 512 + tt * 128
                            bank = pb[4 + idx % 2]
                            for kc in range(8):
                                te.matmul(bank[:, 0:128], xn0(kc, t0, 128),
                                          buf[:, kc * 128:(kc + 1) * 128],
                                          start=(kc == 0), stop=False)
                            te.matmul(bank[:, 0:128], ctx["onerb"][0:1, 0:128],
                                      ctx["biasrows"][0:1, cg * 128:(cg + 1) * 128],
                                      start=False, stop=True).then_inc(p0, 1)
                            n += 1
            cp0.inc(n)

        @blk.scalar
        def _(sc):
            for cg in range(8):
                for tc in range(8):
                    sc.wait_ge(p0, p0b + cg * 8 + tc + 1)
                    bank = pb[tc % 4]
                    if cg < 4:
                        sc.activation(out=qfv(cg, tc * 512, 512), in_=bank[:, 0:512],
                                      func=AF.Copy, scale=0.125).then_inc(s0, 1)
                    else:
                        sc.activation(out=kfv(cg - 4, tc * 512, 512), in_=bank[:, 0:512],
                                      func=AF.Copy).then_inc(s0, 1)
            cs0.inc(64)

        @blk.vector
        def _(v):
            for cg in range(8, 12):
                for tc in range(8):
                    for tt in range(4):
                        idx = (cg - 8) * 32 + tc * 4 + tt
                        v.wait_ge(p0, p0b + 64 + idx + 1)
                        bank = pb[4 + idx % 2]
                        tile = tc * 4 + tt
                        hh = (cg - 8) * 2
                        v.tensor_copy(out=vaug(tile * 520 + hh * 65, 64), in_=bank[:, 0:64])
                        v.tensor_copy(out=vaug(tile * 520 + (hh + 1) * 65, 64), in_=bank[:, 64:128]).then_inc(v0, 1)
            cv0.inc(128)


def ph_kquant(ctx):
    nc = ctx["nc"]
    ar = ctx["ar"]
    pb = ctx["pb"]
    p0, cp0 = _sem(ctx, "p0")
    v0, cv0 = _sem(ctx, "v0")
    s0, cs0 = _sem(ctx, "s0")
    d2, cd2 = _sem(ctx, "d2")
    sv, srow = ctx["sv"], ctx["srow"]
    QL = S - 32
    kfv = lambda c, a, n: ar.bf("kf", c * S + a, n)

    with (
        nc.sbuf_tensor("kst", [128, 8], F32) as kst,
        nc.sbuf_tensor("kstt", [8, 128], F32) as kstt,
        nc.sbuf_tensor("st8", [8, 4], F32) as st8,
        nc.sbuf_tensor("svt", [2, 12], F32) as svt,
        nc.Block() as blk,
    ):
        @blk.vector
        def _(v):
            v.wait_ge(s0, cs0.v)
            for c in range(4):
                v.tensor_reduce(out=kst[:, c:c + 1], in_=kfv(c, 0, QL), axis=AX.X, op=ALU.min)
                mmv = v.tensor_reduce(out=kst[:, 4 + c:5 + c], in_=kfv(c, 0, QL),
                                      axis=AX.X, op=ALU.max)
            mmv.then_inc(v0, 1)
            cv0.inc()

        @blk.tensor
        def _(te):
            te.wait_ge(v0, cv0.v)
            te.transpose(pb[0][0:8, 0:128], kst[:, 0:8], ctx["cst"][:, 0:128]).then_inc(p0, 1)
            cp0.inc()

        @blk.vector
        def _(v):
            v.wait_ge(p0, cp0.v)
            v.tensor_copy(out=kstt[:], in_=pb[0][0:8, 0:128])
            v.drain()
            k3 = kstt[0:8, :].rearrange("p (h c) -> p h c", c=64)
            v.tensor_reduce(out=st8[0:8, 0:2], in_=k3, axis=AX.X, op=ALU.min)
            v.tensor_reduce(out=st8[0:8, 2:4], in_=k3, axis=AX.X, op=ALU.max).then_inc(v0, 1)
            cv0.inc()

        @blk.gpsimd
        def _(gp):
            gp.wait_ge(v0, cv0.v)
            gp.dma_start(out=srow[0:1, 8:16], in_=st8[0:4, 0:2]).then_inc(d2, 16)
            cd2.inc(16)
            gp.dma_start(out=srow[0:1, 16:24], in_=st8[4:8, 2:4]).then_inc(d2, 16)
            cd2.inc(16)

        @blk.vector
        def _(v):
            v.wait_ge(d2, cd2.v)
            v.tensor_scalar(out=srow[0:1, 8:24], in0=srow[0:1, 8:24],
                            scalar1=-1e4, scalar2=1e4, op0=ALU.max, op1=ALU.min)
            v.drain()
            v.tensor_tensor(out=srow[0:1, 24:32], in0=srow[0:1, 16:24], in1=srow[0:1, 8:16],
                            op=ALU.subtract)
            v.drain()
            v.tensor_scalar(out=srow[0:1, 24:32], in0=srow[0:1, 24:32],
                            scalar1=1.0 / 3.0, scalar2=1e-8, op0=ALU.mult, op1=ALU.max)
            v.drain()
            v.reciprocal(out=srow[0:1, 32:40], in_=srow[0:1, 24:32])
            v.drain()
            v.tensor_copy(out=srow[0:1, 16:24], in_=srow[0:1, 8:16]).then_inc(v0, 1)
            cv0.inc()

        @blk.gpsimd
        def _(gp):
            gp.wait_ge(v0, cv0.v)
            srci = srow[0:1, 16:40].rearrange("p (a h) -> p a h", h=2)
            with nc.allow_non_contiguous_dma(reason="24-element stat shuffle"):
                gp.dma_start(out=svt[0:1, :].unsqueeze(2),
                             in_=srci[:, :, 0:1]).then_inc(d2, 16)
                cd2.inc(16)
                gp.dma_start(out=svt[1:2, :].unsqueeze(2),
                             in_=srci[:, :, 1:2]).then_inc(d2, 16)
                cd2.inc(16)

        @blk.tensor
        def _(te):
            te.wait_ge(d2, cd2.v)
            te.matmul(pb[1][:, 0:12], ctx["hselt"][:], svt[:], start=True, stop=True).then_inc(p0, 1)
            cp0.inc()

        @blk.vector
        def _(v):
            v.wait_ge(p0, cp0.v)
            v.tensor_copy(out=sv[:, 11:23], in_=pb[1][:, 0:12]).then_inc(v0, 1)
            cv0.inc()

        @blk.vector
        def _(v):
            kqt = ar.f32("tmpA", 0, 512)
            kqr = ar.f32("tmpA", 512, 512)
            kqi = ar.i32("tmpA", 1024, 512)
            for c in range(4):
                mn, scl, rsc = sv[:, 11 + c:12 + c], sv[:, 15 + c:16 + c], sv[:, 19 + c:20 + c]
                for bi in range(8):
                    lo = bi * 512
                    w = min(512, QL - lo)
                    v.tensor_scalar(out=kqt[:, 0:w], in0=kfv(c, lo, w),
                                    scalar1=-1e4, scalar2=1e4, op0=ALU.max, op1=ALU.min)
                    v.drain()
                    v.tensor_tensor(out=kqt[:, 0:w], in0=kqt[:, 0:w],
                                    in1=mn.broadcast_to([128, w]), op=ALU.subtract)
                    v.drain()
                    v.tensor_tensor(out=kqt[:, 0:w], in0=kqt[:, 0:w],
                                    in1=rsc.broadcast_to([128, w]), op=ALU.mult)
                    v.drain()
                    v.tensor_copy(out=kqi[:, 0:w], in_=kqt[:, 0:w])
                    v.drain()
                    v.tensor_copy(out=kqr[:, 0:w], in_=kqi[:, 0:w])
                    v.drain()
                    v.tensor_tensor(out=kqt[:, 0:w], in0=kqt[:, 0:w],
                                    in1=kqr[:, 0:w], op=ALU.subtract)
                    v.drain()
                    v.tensor_scalar(out=kqt[:, 0:w], in0=kqt[:, 0:w],
                                    scalar1=0.5, scalar2=None, op0=ALU.is_ge)
                    v.drain()
                    v.tensor_tensor(out=kqr[:, 0:w], in0=kqr[:, 0:w],
                                    in1=kqt[:, 0:w], op=ALU.add)
                    v.drain()
                    v.tensor_tensor(out=kqr[:, 0:w], in0=kqr[:, 0:w],
                                    in1=scl.broadcast_to([128, w]), op=ALU.mult)
                    v.drain()
                    v.tensor_tensor(out=kfv(c, lo, w), in0=kqr[:, 0:w],
                                    in1=mn.broadcast_to([128, w]), op=ALU.add)
                    v.drain()
            v.drain(fusable=False).then_inc(v0, 1); cv0.inc()


def ph_vquant(ctx):
    nc = ctx["nc"]
    ar = ctx["ar"]
    v0, cv0 = _sem(ctx, "v0")
    vaug = lambda a, n: ar.bf("vaug", a, n)

    ar.alloc("vqtmp", (32 + 512 * 3) * 4)
    vst = ar.f32("vqtmp", 0, 32)
    vtmp = ar.f32("vqtmp", 32, 512)
    vtr = ar.f32("vqtmp", 544, 512)
    vti = ar.i32("vqtmp", 1056, 512)
    with (
        nc.Block() as blk,
    ):
        @blk.vector
        def _(v):
            for tile in range(32):
                nr = 128 if tile < 31 else 96
                sl3 = vaug(tile * 520, 520)[0:nr, :].rearrange(
                    "p (h c) -> p h c", c=65)[:, :, 0:64]
                vt3 = vtmp[0:nr, :].rearrange("p (h c) -> p h c", c=64)
                vi3 = vti[0:nr, :].rearrange("p (h c) -> p h c", c=64)
                mn = vst[0:nr, 0:8]
                scl = vst[0:nr, 16:24]
                rsc = vst[0:nr, 24:32]
                bc = lambda t: t.unsqueeze(2).broadcast_to([nr, 8, 64])
                v.drain()
                v.tensor_scalar(out=vt3, in0=sl3, scalar1=-1e4, scalar2=1e4,
                                op0=ALU.max, op1=ALU.min)
                v.drain()
                v.tensor_reduce(out=mn, in_=vt3, axis=AX.X, op=ALU.min)
                v.tensor_reduce(out=vst[0:nr, 8:16], in_=vt3, axis=AX.X, op=ALU.max)
                v.drain()
                v.tensor_tensor(out=scl, in0=vst[0:nr, 8:16], in1=mn, op=ALU.subtract)
                v.drain()
                v.tensor_scalar(out=scl, in0=scl, scalar1=1.0 / 3.0, scalar2=1e-8,
                                op0=ALU.mult, op1=ALU.max)
                v.drain()
                v.reciprocal(out=rsc, in_=scl)
                v.drain()
                v.tensor_tensor(out=vt3, in0=vt3, in1=bc(mn), op=ALU.subtract)
                v.drain()
                v.tensor_tensor(out=vt3, in0=vt3, in1=bc(rsc), op=ALU.mult)
                v.drain()
                v.tensor_copy(out=vti[0:nr, :], in_=vtmp[0:nr, :])
                v.drain()
                v.tensor_copy(out=vtr[0:nr, :], in_=vti[0:nr, :])
                v.drain()
                v.tensor_tensor(out=vtmp[0:nr, :], in0=vtmp[0:nr, :],
                                in1=vtr[0:nr, :], op=ALU.subtract)
                v.drain()
                v.tensor_scalar(out=vtmp[0:nr, :], in0=vtmp[0:nr, :],
                                scalar1=0.5, scalar2=None, op0=ALU.is_ge)
                v.drain()
                v.tensor_tensor(out=vtr[0:nr, :], in0=vtr[0:nr, :],
                                in1=vtmp[0:nr, :], op=ALU.add)
                v.drain()
                vr3 = vtr[0:nr, :].rearrange("p (h c) -> p h c", c=64)
                v.tensor_tensor(out=vr3, in0=vr3, in1=bc(scl), op=ALU.mult)
                v.drain()
                mmv = v.tensor_tensor(out=sl3, in0=vr3, in1=bc(mn), op=ALU.add)
            mmv.then_inc(v0, 1)
            cv0.inc()
    ar.release("vqtmp")


def ph_mhat(ctx):
    nc = ctx["nc"]
    ar = ctx["ar"]
    pb = ctx["pb"]
    p0, cp0 = _sem(ctx, "p0")
    v0, cv0 = _sem(ctx, "v0")
    s0, cs0 = _sem(ctx, "s0")
    sv, srow = ctx["sv"], ctx["srow"]
    sq = lambda a, n: ar.bf("tmpA", a, n)

    p0b, v0b = cp0.v, cv0.v
    with nc.Block() as blk:
        @blk.vector
        def _(v):
            v.wait_ge(s0, cs0.v)
            for c in range(4):
                if c >= 1:
                    v.wait_ge(p0, p0b + c)
                buf = ar.bf("qf", c * S, S)
                v.tensor_tensor(out=sq(0, S), in0=buf, in1=buf, op=ALU.mult).then_inc(v0, 1)
            v.wait_ge(p0, p0b + 4)
            for tc in range(8):
                v.tensor_reduce(out=srow[0:1, 48 + tc:49 + tc], in_=pb[tc][0:1, 0:512],
                                axis=AX.X, op=ALU.max)
            v.drain()
            v.tensor_reduce(out=srow[0:1, 40:41], in_=srow[0:1, 48:56], axis=AX.X, op=ALU.max).then_inc(v0, 1)
            for c in range(4):
                v.wait_ge(p0, p0b + 4 + c)
                buf = ar.bf("kf", c * S, S)
                v.tensor_tensor(out=sq(0, S), in0=buf, in1=buf, op=ALU.mult).then_inc(v0, 1)
            v.wait_ge(p0, p0b + 8)
            for tc in range(8):
                v.tensor_reduce(out=srow[0:1, 48 + tc:49 + tc], in_=pb[tc][0:1, 0:512],
                                axis=AX.X, op=ALU.max)
            v.drain()
            v.tensor_reduce(out=srow[0:1, 41:42], in_=srow[0:1, 48:56], axis=AX.X, op=ALU.max)
            v.drain()
            v.tensor_tensor(out=srow[0:1, 42:43], in0=srow[0:1, 40:41], in1=srow[0:1, 41:42],
                            op=ALU.mult).then_inc(v0, 1)
            cv0.inc(10)

        @blk.tensor
        def _(te):
            for half in range(2):
                for c in range(4):
                    te.wait_ge(v0, v0b + half * 5 + c + 1)
                    for tc in range(8):
                        mm = te.matmul(pb[tc][0:1, 0:512], ctx["onecb"][:], sq(tc * 512, 512),
                                       start=(c == 0), stop=(c == 3))
                        if tc == 7:
                            mm.then_inc(p0, 1)
            cp0.inc(8)

        @blk.scalar
        def _(sc):
            sc.wait_ge(v0, cv0.v)
            sc.activation(out=srow[0:1, 43:44], in_=srow[0:1, 42:43], func=AF.Sqrt).then_inc(s0, 1)
            cs0.inc()

        @blk.vector
        def _(v):
            v.wait_ge(s0, cs0.v)
            v.tensor_scalar(out=srow[0:1, 44:45], in0=srow[0:1, 43:44], scalar1=-1.0,
                            scalar2=None, op0=ALU.mult).then_inc(v0, 1)
            cv0.inc()

        @blk.tensor
        def _(te):
            te.wait_ge(v0, cv0.v)
            te.matmul(pb[1][:, 16:17], ctx["onerf"][:], srow[0:1, 44:45],
                      start=True, stop=True).then_inc(p0, 1)
            cp0.inc()

        @blk.vector
        def _(v):
            v.wait_ge(p0, cp0.v)
            v.tensor_copy(out=sv[:, 23:24], in_=pb[1][:, 16:17]).then_inc(v0, 1)
            cv0.inc()


def _attn_seq():
    seq = []
    for h in range(8):
        for j in range(8):
            seq.append((h, j, 4 * (j + 1)))
    return seq


def ph_attn(ctx):
    nc = ctx["nc"]
    ar = ctx["ar"]
    pb = ctx["pb"]
    p0, cp0 = _sem(ctx, "p0")
    v0, cv0 = _sem(ctx, "v0")
    s0, cs0 = _sem(ctx, "s0")
    d3, cd3 = _sem(ctx, "d3")
    sv = ctx["sv"]
    ering = ctx["eringb"]
    qfb = lambda c: ar.bf("qf", c * S, S)
    kfb = lambda c: ar.bf("kf", c * S, S)
    vaug = lambda a, n: ar.bf("vaug", a, n)
    seq = _attn_seq()

    p0b, v0b, s0b, d3b = cp0.v, cv0.v, cs0.v, cd3.v

    def replay():
        pe, act, dve = 0, 0, 0
        ev = []
        for (h, j, nk) in seq:
            for i in range(nk):
                pe += 1; ev.append(("qk", h, j, i, pe, act, dve))
                act += 1; ev.append(("exp", h, j, i, pe, act, dve))
                if i >= 4 * j:
                    dve += 1; ev.append(("mask", h, j, i, pe, act, dve))
                if i >= 1:
                    pe += 1; ev.append(("av", h, j, i - 1, pe, act, dve))
            pe += 1; ev.append(("av", h, j, nk - 1, pe, act, dve))
            dve += 1; ev.append(("recip", h, j, -1, pe, act, dve))
            pe += 1; ev.append(("bcast", h, j, -1, pe, act, dve))
            dve += 1; ev.append(("ymul", h, j, -1, pe, act, dve))
            ev.append(("ydma", h, j, -1, pe, act, dve))
        return ev

    ev = replay()
    exp_after_count = {}
    av_after = {}
    bcast_after = {}
    ymul_after = {}
    recip_after = {}
    qk_after = {}
    mask_after = {}
    for (kind, h, j, i, pe, act, dve) in ev:
        if kind == "exp":
            exp_after_count[(h, j, i)] = act
        elif kind == "av":
            av_after[(h, j, i)] = pe
        elif kind == "qk":
            qk_after[(h, j, i)] = pe
        elif kind == "mask":
            mask_after[(h, j, i)] = dve
        elif kind == "bcast":
            bcast_after[(h, j)] = pe
        elif kind == "ymul":
            ymul_after[(h, j)] = dve
        elif kind == "recip":
            recip_after[(h, j)] = dve

    with (
        nc.sbuf_tensor("zrec", [1, 512], F32) as zrec,
        nc.sbuf_tensor("zbc", [64, 512], F32) as zbc,
        nc.sbuf_tensor("ytile", [64, 512], BF16) as ytile,
        nc.Block() as blk,
    ):
        @blk.tensor
        def _(te):
            te.wait_ge(v0, v0b)
            te.wait_ge(s0, s0b)
            for si, (h, j, nk) in enumerate(seq):
                ch, half = h // 2, h % 2
                rows = slice(half * 64, (half + 1) * 64)
                av = pb[4 + si % 2]

                def emit_av(i):
                    if i == 0 and si >= 2:
                        h2, j2 = seq[si - 2][0], seq[si - 2][1]
                        te.wait_ge(v0, v0b + ymul_after[(h2, j2)])
                    if i >= 4 * j:
                        te.wait_ge(v0, v0b + mask_after[(h, j, i)])
                    else:
                        te.wait_ge(s0, s0b + exp_after_count[(h, j, i)])
                    te.matmul(av[0:65, 0:512], vaug(i * 520 + h * 65, 65),
                              ering[:, (i % 2) * 512:(i % 2 + 1) * 512],
                              start=(i == 0), stop=(i == nk - 1)).then_inc(p0, 1)

                for i in range(nk):
                    if i >= 2:
                        te.wait_ge(s0, s0b + exp_after_count[(h, j, i - 2)])
                    elif si > 0:
                        ph_, pj_, pn_ = seq[si - 1]
                        te.wait_ge(s0, s0b + exp_after_count[(ph_, pj_, pn_ - 2 + i)])
                    te.matmul(pb[i % 2][:, 0:512],
                              kfb(ch)[rows, i * 128:(i + 1) * 128],
                              qfb(ch)[rows, j * 512:(j + 1) * 512], start=True, stop=True).then_inc(p0, 1)
                    if i >= 1:
                        emit_av(i - 1)
                emit_av(nk - 1)
                te.wait_ge(v0, v0b + recip_after[(h, j)])
                if si >= 1:
                    h1, j1 = seq[si - 1][0], seq[si - 1][1]
                    te.wait_ge(v0, v0b + ymul_after[(h1, j1)])
                te.matmul(pb[6][0:64, 0:512], ctx["onerf"][0:1, 0:64], zrec[0:1, 0:512],
                          start=True, stop=True).then_inc(p0, 1)
            cp0.inc(sum(2 * nk + 1 for (_, _, nk) in seq))

        @blk.scalar
        def _(sc):
            sc.wait_ge(v0, v0b)
            for (h, j, nk) in seq:
                for i in range(nk):
                    sc.wait_ge(p0, p0b + qk_after[(h, j, i)])
                    sc.activation(out=ering[:, (i % 2) * 512:(i % 2 + 1) * 512],
                                  in_=pb[i % 2][:, 0:512], func=AF.Exp,
                                  bias=sv[:, 23:24], scale=1.0).then_inc(s0, 1)
            cs0.inc(sum(nk for (_, _, nk) in seq))

        @blk.vector
        def _(v):
            ndma = 0
            for si, (h, j, nk) in enumerate(seq):
                av = pb[4 + si % 2]
                for i in range(nk):
                    if i >= 4 * j:
                        v.wait_ge(s0, s0b + exp_after_count[(h, j, i)])
                        r = i - 4 * j
                        v.tensor_tensor(out=ering[:, (i % 2) * 512:(i % 2 + 1) * 512],
                                        in0=ering[:, (i % 2) * 512:(i % 2 + 1) * 512],
                                        in1=ar.bf("maskt", r * 512, 512), op=ALU.mult).then_inc(v0, 1)
                v.wait_ge(p0, p0b + av_after[(h, j, nk - 1)])
                v.reciprocal(out=zrec[0:1, 0:512], in_=av[64:65, 0:512]).then_inc(v0, 1)
                v.wait_ge(p0, p0b + bcast_after[(h, j)])
                if si >= 1:
                    v.wait_ge(d3, d3b + 16 * si)
                v.tensor_copy(out=zbc[:], in_=pb[6][0:64, 0:512])
                v.drain()
                v.tensor_tensor(out=ytile[:, 0:512], in0=av[0:64, 0:512],
                                in1=zbc[:], op=ALU.mult).then_inc(v0, 1)
            cv0.inc(max(d[1] for d in [(0, ymul_after[seq[-1][0], seq[-1][1]])]))

        @blk.gpsimd
        def _(gp):
            for si, (h, j, nk) in enumerate(seq):
                gp.wait_ge(v0, v0b + ymul_after[(h, j)])
                gp.dma_start(out=ctx["y0_in"].ap()[h * 64:(h + 1) * 64, j * 512:(j + 1) * 512],
                             in_=ytile[:, 0:512]).then_inc(d3, 16)
                cd3.inc(16)
            gp.wait_ge(d3, cd3.v)


def ph_l0tail(ctx):
    nc = ctx["nc"]
    ar = ctx["ar"]
    pb = ctx["pb"]
    d1, cd1 = _sem(ctx, "d1")
    d2, cd2 = _sem(ctx, "d2")
    d3, cd3 = _sem(ctx, "d3")
    d4, cd4 = _sem(ctx, "d4")
    d5, cd5 = _sem(ctx, "d5")
    p0, cp0 = _sem(ctx, "p0")
    v0, cv0 = _sem(ctx, "v0")
    s0, cs0 = _sem(ctx, "s0")
    g0, cg0 = _sem(ctx, "g0")
    cc, ccc = _sem(ctx, "cc")

    with nc.Block() as blk:
        @blk.gpsimd
        def _(gp):
            with allow_ici_all_to_all():
                gp.collective_compute(
                    "AllGather", ALU.bypass,
                    replica_groups=[[0, 1], [2, 3], [4, 5], [6, 7]],
                    ins=[ctx["y0_in"].ap().opt()],
                    outs=[ctx["y0_out"].ap().opt()],
                ).then_inc(cc, 1)
            ccc.inc()

    ar.release("xn0full", "qf", "kf", "vaug", "tmpA", "maskt")
    ar.alloc("x1", 8 * TOK * 4)
    ar.alloc("xn0e", 8 * EXT * 2)
    ar.alloc("mixr", 8 * TOK * 2)
    ar.alloc("mixk", 8 * TOK * 2)
    x1 = lambda f, a, n: ar.f32("x1", f * TOK + a, n)
    xn0e = lambda kc, a, n: ar.bf("xn0e", kc * EXT + a, n)
    xrv = lambda kc, a, n: ar.bf("mixr", kc * TOK + a, n)
    xkv = lambda kc, a, n: ar.bf("mixk", kc * TOK + a, n)

    ar.alloc("xtt", 2 * E * 4)
    xtt = ar.f32("xtt", 0, 2 * E)
    with (
        nc.Block() as blk,
    ):
        d1b, p0b, v0b = cd1.v, cp0.v, cv0.v
        d4b, d5b = cd4.v, cd5.v

        @blk.sync
        def _(sy):
            for f in range(8):
                sy.dma_start_transpose(xn0e(f, 0, EXT),
                                       ctx["xn0_own_tm"].ap()[:, f * 128:(f + 1) * 128]
                                       ).then_inc(d1, 16)
                cd1.inc(16)

        @blk.gpsimd
        def _(gp):
            for i in range(16):
                if i >= 2:
                    gp.wait_ge(p0, p0b + 8 * (i - 1))
                gp.dma_start(out=xtt[:, (i % 2) * E:(i % 2 + 1) * E],
                             in_=ctx["x_own"][i * 128:(i + 1) * 128, :]
                             ).then_inc(d4 if i % 2 == 0 else d5, 16)
                (cd4 if i % 2 == 0 else cd5).inc(16)

        @blk.tensor
        def _(te):
            for i in range(16):
                te.wait_ge(d4 if i % 2 == 0 else d5,
                           (d4b if i % 2 == 0 else d5b) + 16 * (i // 2 + 1))
                for f in range(8):
                    idx = i * 8 + f
                    if idx >= 4:
                        te.wait_ge(v0, v0b + idx - 3)
                    te.transpose(pb[idx % 4][:, 0:128],
                                 xtt[:, (i % 2) * E + f * 128:(i % 2) * E + (f + 1) * 128],
                                 ctx["cst"][:, 0:128]).then_inc(p0, 1)
            cp0.inc(128)

        @blk.vector
        def _(v):
            for i in range(16):
                for f in range(8):
                    idx = i * 8 + f
                    v.wait_ge(p0, p0b + idx + 1)
                    v.tensor_copy(out=x1(f, i * 128, 128), in_=pb[idx % 4][:, 0:128]).then_inc(v0, 1)
            cv0.inc(128)

        @blk.scalar
        def _(sc):
            sc.wait_ge(d1, cd1.v)

    ar.release("xtt")
    with nc.Block() as blk:
        @blk.vector
        def _(v):
            v.wait_ge(d1, cd1.v)
            for kc in range(8):
                for half in range(2):
                    a = half * 1024
                    xn = xn0e(kc, 1 + a, 1024)
                    xp = xn0e(kc, a, 1024)
                    d_r = xrv(kc, a, 1024)
                    d_k = xkv(kc, a, 1024)
                    v.tensor_tensor(out=d_k, in0=xn, in1=xp, op=ALU.subtract)
                    v.drain()
                    v.tensor_tensor(out=d_r, in0=d_k,
                                    in1=ctx["mixmu"][:, kc:kc + 1].broadcast_to([128, 1024]),
                                    op=ALU.mult)
                    v.drain()
                    v.tensor_tensor(out=d_k, in0=d_k,
                                    in1=ctx["mixmu"][:, 8 + kc:9 + kc].broadcast_to([128, 1024]),
                                    op=ALU.mult)
                    v.drain()
                    v.tensor_tensor(out=d_r, in0=d_r, in1=xp, op=ALU.add)
                    mmv = v.tensor_tensor(out=d_k, in0=d_k, in1=xp, op=ALU.add)
            mmv.then_inc(v0, 1)
            cv0.inc()

    ar.release("xn0e")
    ar.alloc("kact", 8 * TOK * 2)
    ar.alloc("wfull", 8 * 1024 * 2)
    wfull = lambda kc, dc: ar.bf("wfull", kc * 1024 + dc * 128, 128)
    kact = lambda kc, a, n: ar.bf("kact", kc * TOK + a, n)

    def stream_wfull(gp, wname):
        for kc in range(8):
            gp.dma_start(out=ar.bf("wfull", kc * 1024, 1024),
                         in_=bass.AP(ctx[wname + "_h"], kc * 128 * E, [[E, 128], [1, E]])
                         ).then_inc(d2, 16)
            cd2.inc(16)

    with nc.Block() as blk:
        d2b, p0b, v0b = cd2.v, cp0.v, cv0.v

        @blk.gpsimd
        def _(gp):
            stream_wfull(gp, "wkT")

        @blk.tensor
        def _(te):
            te.wait_ge(d2, d2b + 128)
            n = 0
            for dc in range(8):
                for tc in range(4):
                    idx = dc * 4 + tc
                    if idx >= 4:
                        te.wait_ge(v0, v0b + 2 * (idx - 3))
                    bank = pb[idx % 4]
                    for kc in range(8):
                        mm = te.matmul(bank[:, 0:512], wfull(kc, dc), xkv(kc, tc * 512, 512),
                                       start=(kc == 0), stop=(kc == 7))
                    mm.then_inc(p0, 1)
                    n += 1
            cp0.inc(n)

        @blk.vector
        def _(v):
            for dc in range(8):
                for tc in range(4):
                    idx = dc * 4 + tc
                    v.wait_ge(p0, p0b + idx + 1)
                    bank = pb[idx % 4]
                    v.tensor_scalar(out=kact(dc, tc * 512, 512), in0=bank[:, 0:512],
                                    scalar1=0.0, scalar2=None, op0=ALU.max)
                    v.drain()
                    v.tensor_tensor(out=kact(dc, tc * 512, 512), in0=kact(dc, tc * 512, 512),
                                    in1=kact(dc, tc * 512, 512), op=ALU.mult).then_inc(v0, 2)
            cv0.inc(64)

    ar.release("mixk")
    ar.alloc("wfull2", 8 * 1024 * 2)
    ar.alloc("kvslab", 8 * 512 * 2)
    ar.alloc("sgslab", 8 * 512 * 2)
    wfull2 = lambda kc, dc: ar.bf("wfull2", kc * 1024 + dc * 128, 128)
    kvs = lambda dc: ar.bf("kvslab", dc * 512, 512)
    sgs = lambda dc: ar.bf("sgslab", dc * 512, 512)

    with nc.Block() as blk:
        d2b, p0b, v0b, s0b = cd2.v, cp0.v, cv0.v, cs0.v

        @blk.gpsimd
        def _(gp):
            stream_wfull(gp, "wvT")
            for kc in range(8):
                gp.dma_start(out=ar.bf("wfull2", kc * 1024, 1024),
                             in_=bass.AP(ctx["wrT_h"], kc * 128 * E, [[E, 128], [1, E]])
                             ).then_inc(d2, 16)
                cd2.inc(16)

        @blk.tensor
        def _(te):
            te.wait_ge(d2, d2b + 256)
            n = 0
            for tc in range(4):
                for dc in range(8):
                    bank = pb[dc % 4]
                    if dc >= 4 or tc > 0:
                        te.wait_ge(v0, v0b + _l0_vr_vwait(tc, dc))
                    for kc in range(8):
                        mm = te.matmul(bank[:, 0:512], wfull(kc, dc), kact(kc, tc * 512, 512),
                                       start=(kc == 0), stop=(kc == 7))
                    mm.then_inc(p0, 1)
                    n += 1
                for dc in range(8):
                    bank = pb[4 + dc % 3]
                    if tc > 0 or dc >= 3:
                        te.wait_ge(s0, s0b + _l0_r_swait(tc, dc))
                    for kc in range(8):
                        mm = te.matmul(bank[:, 0:512], wfull2(kc, dc), xrv(kc, tc * 512, 512),
                                       start=(kc == 0), stop=(kc == 7))
                    mm.then_inc(p0, 1)
                    n += 1
            cp0.inc(n)

        @blk.vector
        def _(v):
            for tc in range(4):
                for dc in range(8):
                    v.wait_ge(p0, p0b + tc * 16 + dc + 1)
                    v.tensor_copy(out=kvs(dc), in_=pb[dc % 4][:, 0:512]).then_inc(v0, 1)
                for dc in range(8):
                    v.wait_ge(s0, s0b + tc * 8 + dc + 1)
                    v.tensor_tensor(out=sgs(dc), in0=sgs(dc), in1=kvs(dc), op=ALU.mult)
                    v.drain()
                    v.tensor_tensor(out=x1(dc, tc * 512, 512), in0=x1(dc, tc * 512, 512),
                                    in1=sgs(dc), op=ALU.add).then_inc(v0, 1)
            cv0.inc(64)

        @blk.scalar
        def _(sc):
            for tc in range(4):
                for dc in range(8):
                    sc.wait_ge(p0, p0b + tc * 16 + 8 + dc + 1)
                    if tc > 0:
                        sc.wait_ge(v0, v0b + 16 * (tc - 1) + 8 + dc + 1)
                    sc.activation(out=sgs(dc), in_=pb[4 + dc % 3][:, 0:512], func=AF.Sigmoid).then_inc(s0, 1)
            cs0.inc(32)

    ar.release("mixr", "kact", "kvslab", "sgslab", "wfull2")
    ar.alloc("yall", 8 * TOK * 2)
    yall = lambda kc, a, n: ar.bf("yall", kc * TOK + a, n)

    with nc.Block() as blk:
        d2b, d3b, p0b, v0b = cd2.v, cd3.v, cp0.v, cv0.v

        @blk.gpsimd
        def _(gp):
            stream_wfull(gp, "wproj")
            gp.wait_ge(cc, ccc.v)
            rgC = gp.alloc_register()
            gp.reg_load(rgC, ctx["offs"][0:1, 1:2])
            gp.dma_start(out=yall(0, 0, 8 * TOK).rearrange("p (kc t) -> p kc t", t=TOK),
                         in_=bass.AP(ctx["y0_out"], rgC,
                                     [[S, 128], [128 * S, 8], [1, TOK]])
                         ).then_inc(d3, 16)
            cd3.inc(16)

        @blk.tensor
        def _(te):
            te.wait_ge(d2, d2b + 128)
            te.wait_ge(d3, d3b + 16)
            n = 0
            for dc in range(8):
                for tc in range(4):
                    idx = dc * 4 + tc
                    if idx >= 4:
                        te.wait_ge(v0, v0b + idx - 3)
                    bank = pb[idx % 4]
                    for kc in range(8):
                        te.matmul(bank[:, 0:512], wfull(kc, dc), yall(kc, tc * 512, 512),
                                  start=(kc == 0), stop=False)
                    te.matmul(bank[:, 0:512], ctx["biasrows"][0:1, 1536 + dc * 128:1536 + (dc + 1) * 128],
                              ctx["onerb"][0:1, 0:512], start=False, stop=True).then_inc(p0, 1)
                    n += 1
            cp0.inc(n)

        @blk.vector
        def _(v):
            for dc in range(8):
                for tc in range(4):
                    idx = dc * 4 + tc
                    v.wait_ge(p0, p0b + idx + 1)
                    v.tensor_tensor(out=x1(dc, tc * 512, 512), in0=x1(dc, tc * 512, 512),
                                    in1=pb[idx % 4][:, 0:512], op=ALU.add).then_inc(v0, 1)
            cv0.inc(32)

    ar.release("yall", "wfull")
    ar.alloc("xn1", 8 * TOK * 2)
    _emit_ln_fm(ctx, src_f32=lambda f, a, n: ar.f32("x1", f * TOK + a, n),
                dst_bf=lambda f, a, n: ar.bf("xn1", f * TOK + a, n),
                n_tok=TOK)


def _l0_vr_vwait(tc, dc):
    if dc >= 4:
        return 16 * tc + (dc - 4) + 1
    return 16 * (tc - 1) + (dc + 4) + 1 if tc > 0 else 0


def _l0_r_swait(tc, dc):
    if dc >= 3:
        return tc * 8 + dc - 2
    return tc * 8


def _emit_ln_fm(ctx, src_f32, dst_bf, n_tok, gamma_fn=None, beta_fn=None,
                 res_dst=None, x1_acc=None):
    nc = ctx["nc"]
    pb = ctx["pb"]
    p0, cp0 = _sem(ctx, "p0")
    v0, cv0 = _sem(ctx, "v0")
    s0, cs0 = _sem(ctx, "s0")
    ntc = n_tok // 512

    def pe_sx(tc, kc):
        return tc * 16 + kc + 1

    def pe_bc(tc, kc):
        return tc * 16 + 8 + kc + 1

    ar = ctx["ar"]
    ar.alloc("lntmp", 4 * 512 * 4 + 2 * 512 * 2 + 2 * 512 * 4)
    lnr = ar.f32("lntmp", 0, 4 * 512)[0:1, :]
    lnrb = ar.bf("lntmp", 4 * 512 * 2, 2 * 512)[0:1, :]
    sqf = ar.f32("lntmp", 4 * 512 + 512, 2 * 512)
    with (
        nc.Block() as blk,
    ):
        p0b, v0b, s0b = cp0.v, cv0.v, cs0.v

        @blk.scalar
        def _(sc):
            sc.wait_ge(v0, v0b)
            for tc in range(ntc):
                for kc in range(8):
                    idx = tc * 8 + kc
                    if idx >= 2:
                        sc.wait_ge(p0, p0b + pe_sx((idx - 2) // 8, (idx - 2) % 8))
                    sc.activation(out=sqf[:, (kc % 2) * 512:(kc % 2 + 1) * 512],
                                  in_=src_f32(kc, tc * 512, 512), func=AF.Square).then_inc(s0, 1)
                sc.wait_ge(v0, v0b + tc * 10 + 1)
                sc.activation(out=lnr[0:1, 512:1024], in_=lnr[0:1, 1024:1536], func=AF.Sqrt).then_inc(s0, 1)
            cs0.inc(ntc * 9)

        @blk.tensor
        def _(te):
            te.wait_ge(v0, v0b)
            for tc in range(ntc):
                if tc >= 1:
                    te.wait_ge(v0, v0b + tc * 10)
                for kc in range(8):
                    te.matmul(pb[6][0:1, 0:512], ctx["onec"][:], src_f32(kc, tc * 512, 512),
                              start=(kc == 0), stop=(kc == 7))
                    te.wait_ge(s0, s0b + tc * 9 + kc + 1)
                    te.matmul(pb[7][0:1, 0:512], ctx["onec"][:],
                              sqf[:, (kc % 2) * 512:(kc % 2 + 1) * 512],
                              start=(kc == 0), stop=(kc == 7)).then_inc(p0, 1)
                for kc in range(8):
                    te.wait_ge(v0, v0b + tc * 10 + 2 + kc)
                    ga = ctx["onerb"][0:1, 0:128] if gamma_fn is None else gamma_fn(kc)
                    te.matmul(pb[6][:, 0:512], ga, lnrb[0:1, 0:512], start=True, stop=True)
                    if beta_fn is None:
                        te.matmul(pb[7][:, 0:512], ga, lnrb[0:1, 512:1024],
                                  start=True, stop=True).then_inc(p0, 1)
                    else:
                        te.matmul(pb[7][:, 0:512], ga, lnrb[0:1, 512:1024], start=True, stop=False)
                        te.matmul(pb[7][:, 0:512], beta_fn(kc), ctx["onerb"][0:1, 0:512],
                                  start=False, stop=True).then_inc(p0, 1)
            cp0.inc(ntc * 16)

        @blk.vector
        def _(v):
            for tc in range(ntc):
                v.wait_ge(p0, p0b + pe_sx(tc, 7))
                v.tensor_scalar(out=lnr[0:1, 0:512], in0=pb[6][0:1, 0:512],
                                scalar1=1.0 / E, scalar2=None, op0=ALU.mult)
                v.tensor_scalar(out=lnr[0:1, 1024:1536], in0=pb[7][0:1, 0:512],
                                scalar1=1.0 / E, scalar2=None, op0=ALU.mult)
                v.drain()
                v.tensor_tensor(out=lnr[0:1, 1536:2048], in0=lnr[0:1, 0:512],
                                in1=lnr[0:1, 0:512], op=ALU.mult)
                v.drain()
                v.tensor_tensor(out=lnr[0:1, 1024:1536], in0=lnr[0:1, 1024:1536],
                                in1=lnr[0:1, 1536:2048], op=ALU.subtract)
                v.drain()
                v.tensor_scalar(out=lnr[0:1, 1024:1536], in0=lnr[0:1, 1024:1536],
                                scalar1=1e-5, scalar2=None, op0=ALU.add).then_inc(v0, 1)
                v.wait_ge(s0, s0b + tc * 9 + 9)
                v.reciprocal(out=lnr[0:1, 1024:1536], in_=lnr[0:1, 512:1024])
                v.drain()
                v.tensor_tensor(out=lnr[0:1, 1536:2048], in0=lnr[0:1, 0:512],
                                in1=lnr[0:1, 1024:1536], op=ALU.mult)
                v.drain()
                v.tensor_scalar(out=lnr[0:1, 1536:2048], in0=lnr[0:1, 1536:2048],
                                scalar1=-1.0, scalar2=None, op0=ALU.mult)
                v.drain()
                v.tensor_copy(out=lnrb[0:1, 0:512], in_=lnr[0:1, 1024:1536])
                v.tensor_copy(out=lnrb[0:1, 512:1024], in_=lnr[0:1, 1536:2048]).then_inc(v0, 1)
                for kc in range(8):
                    v.wait_ge(p0, p0b + pe_bc(tc, kc))
                    if res_dst is not None:
                        v.tensor_tensor(out=res_dst(kc, tc * 512, 512),
                                        in0=src_f32(kc, tc * 512, 512),
                                        in1=pb[6][:, 0:512], op=ALU.mult)
                        v.drain()
                        if x1_acc is None:
                            v.tensor_tensor(out=res_dst(kc, tc * 512, 512),
                                            in0=res_dst(kc, tc * 512, 512),
                                            in1=pb[7][:, 0:512], op=ALU.add).then_inc(v0, 1)
                        else:
                            v.tensor_tensor(out=res_dst(kc, tc * 512, 512),
                                            in0=res_dst(kc, tc * 512, 512),
                                            in1=pb[7][:, 0:512], op=ALU.add)
                            v.drain()
                            v.tensor_tensor(out=x1_acc(kc, tc * 512, 512),
                                            in0=x1_acc(kc, tc * 512, 512),
                                            in1=res_dst(kc, tc * 512, 512),
                                            op=ALU.add).then_inc(v0, 1)
                    else:
                        v.tensor_tensor(out=dst_bf(kc, tc * 512, 512),
                                        in0=src_f32(kc, tc * 512, 512),
                                        in1=pb[6][:, 0:512], op=ALU.mult)
                        v.drain()
                        v.tensor_tensor(out=dst_bf(kc, tc * 512, 512),
                                        in0=dst_bf(kc, tc * 512, 512),
                                        in1=pb[7][:, 0:512], op=ALU.add).then_inc(v0, 1)
            cv0.inc(ntc * 10)
    ar.release("lntmp")


def ph_l1head(ctx):
    nc = ctx["nc"]
    ar = ctx["ar"]
    pb = ctx["pb"]
    d2, cd2 = _sem(ctx, "d2")
    d3, cd3 = _sem(ctx, "d3")
    p0, cp0 = _sem(ctx, "p0")
    v0, cv0 = _sem(ctx, "v0")
    cc, ccc = _sem(ctx, "cc")
    xn1 = lambda kc, a, n: ar.bf("xn1", kc * TOK + a, n)
    ar.alloc("uown", 16 * 256 * 2)
    uown = lambda tt: ar.bf("uown", tt * 256, 256)

    with nc.Block() as blk:
        d2b, p0b, v0b = cd2.v, cp0.v, cv0.v

        @blk.gpsimd
        def _(gp):
            gp.dma_start(out=ctx["wsa"][:, 0:1024],
                         in_=bass.AP(ctx["wiT_h"], 0, [[256, 128], [128 * 256, 4], [1, 256]])
                         ).then_inc(d2, 16)
            cd2.inc(16)
            gp.dma_start(out=ctx["wsb"][:, 0:1024],
                         in_=bass.AP(ctx["wiT_h"], 4 * 128 * 256, [[256, 128], [128 * 256, 4], [1, 256]])
                         ).then_inc(d2, 16)
            cd2.inc(16)

        @blk.tensor
        def _(te):
            te.wait_ge(d2, d2b + 32)
            n = 0
            for tt in range(16):
                bank = pb[tt % 4]
                if tt >= 4:
                    te.wait_ge(v0, v0b + tt - 3)
                for kc in range(8):
                    wbuf = ctx["wsa"] if kc < 4 else ctx["wsb"]
                    te.matmul(bank[:, 0:256], xn1(kc, tt * 128, 128),
                              wbuf[:, (kc % 4) * 256:(kc % 4 + 1) * 256],
                              start=(kc == 0), stop=False)
                te.matmul(bank[:, 0:256], ctx["onerb"][0:1, 0:128],
                          ctx["biasrows"][0:1, 3584:3840], start=False, stop=True).then_inc(p0, 1)
                n += 1
            cp0.inc(n)

        @blk.vector
        def _(v):
            for tt in range(16):
                v.wait_ge(p0, p0b + tt + 1)
                v.tensor_copy(out=uown(tt), in_=pb[tt % 4][:, 0:256]).then_inc(v0, 1)
            cv0.inc(16)

        @blk.gpsimd
        def _(gp):
            for tt in range(16):
                gp.wait_ge(v0, v0b + tt + 1)
                gp.dma_start(out=ctx["u_in"].ap()[tt * 128:(tt + 1) * 128, :],
                             in_=uown(tt)).then_inc(d3, 16)
                cd3.inc(16)
            gp.wait_ge(d3, cd3.v)
            with allow_ici_all_to_all():
                gp.collective_compute(
                    "AllGather", ALU.bypass,
                    replica_groups=[[0, 1], [2, 3], [4, 5], [6, 7]],
                    ins=[ctx["u_in"].ap().opt()],
                    outs=[ctx["u_out"].ap().opt()],
                ).then_inc(cc, 1)
            ccc.inc()

    ar.release("uown")


def ph_scan(ctx):
    nc = ctx["nc"]
    ar = ctx["ar"]
    pb = ctx["pb"]
    d2, cd2 = _sem(ctx, "d2")
    d3, cd3 = _sem(ctx, "d3")
    p0, cp0 = _sem(ctx, "p0")
    v0, cv0 = _sem(ctx, "v0")
    cc, ccc = _sem(ctx, "cc")
    T = SCAN_T
    ar.alloc("ustrip", 2 * 32 * 128 * 2)
    ar.alloc("yfm", S * 2)
    ustrip = lambda k: ar.bf("ustrip", (k % 2) * 32 * 128, 32 * 128)
    yfm = lambda a, n: ar.bf("yfm", a, n)
    ss = [ctx["ss0"], ctx["ss1"]]
    pv = lambda t: pb[t % 3][:, 0:128]
    py = lambda t: pb[3 + (t // 512) % 2][:, (t % 512):(t % 512) + 1]

    p0b, v0b, d2b = cp0.v, cv0.v, cd2.v
    NSTRIP = (T + 31) // 32
    d3b = cd3.v + 16

    with nc.Block() as blk:
        @blk.gpsimd
        def _(gp):
            gp.wait_ge(cc, ccc.v)
            rgD = gp.alloc_register()
            gp.reg_load(rgD, ctx["offs"][0:1, 2:3])
            gp.dma_start(out=ctx["u_g"].ap(),
                         in_=bass.AP(ctx["u_out"], rgD, [[256, S], [1, 128]])
                         ).then_inc(d3, 16)
            cd3.inc(16)
            gp.wait_ge(d3, cd3.v)
            for k in range(NSTRIP):
                if k >= 2:
                    gp.wait_ge(v0, v0b + 32 * (k - 1))
                gp.dma_start(out=ustrip(k)[0:1, :],
                             in_=ctx["u_g"].ap()[k * 32:(k + 1) * 32, :]
                             ).then_inc(d2 if k % 2 == 0 else d3, 16)
                (cd2 if k % 2 == 0 else cd3).inc(16)

        @blk.tensor
        def _(te):
            te.wait_ge(d2, d2b + 16)
            te.matmul(pv(0), ctx["brow"][:], ustrip(0)[0:1, 0:128], start=True, stop=False)
            te.matmul(pv(1), ctx["brow"][:], ustrip(0)[0:1, 128:256], start=True, stop=False)
            n = 0
            for t in range(T):
                if (t + 2) % 32 == 0 and (t + 2) < T:
                    k = (t + 2) // 32
                    te.wait_ge(d2 if k % 2 == 0 else d3,
                               (d2b if k % 2 == 0 else d3b) + 16 * (k // 2 + 1))
                te.wait_ge(v0, v0b + t)
                te.matmul(pv(t), ctx["adT"][:], ss[(t + 1) % 2][:], start=False, stop=True).then_inc(p0, 1)
                n += 1
                if t >= 1:
                    te.matmul(py(t - 1), ss[(t - 1) % 2][:], ctx["ccol"][:], start=True, stop=True)
                if t + 2 < T:
                    k = (t + 2) // 32
                    te.matmul(pv(t + 2), ctx["brow"][:],
                              ustrip(k)[0:1, ((t + 2) % 32) * 128:((t + 2) % 32 + 1) * 128],
                              start=True, stop=False)
            te.wait_ge(v0, v0b + T)
            te.matmul(py(T - 1), ss[(T - 1) % 2][:], ctx["ccol"][:], start=True, stop=True).then_inc(p0, 1)
            n += 1
            cp0.inc(n)

        @blk.vector
        def _(v):
            for t in range(T):
                v.wait_ge(p0, p0b + t + 1)
                v.tensor_scalar(out=ss[t % 2][:], in0=pv(t), scalar1=-10.0, scalar2=10.0,
                                op0=ALU.max, op1=ALU.min).then_inc(v0, 1)
                if t % 512 == 2 and t > 512:
                    kk = t // 512 - 1
                    v.tensor_copy(out=yfm(kk * 512, 512), in_=pb[3 + kk % 2][:, 0:512])
            v.wait_ge(p0, p0b + T + 1)
            kk = (T - 1) // 512
            w = (T - 1) % 512 + 1
            v.tensor_copy(out=yfm(kk * 512, w), in_=pb[3 + kk % 2][:, 0:w]).then_inc(v0, 1)
            cv0.inc(T + 1)

        @blk.gpsimd
        def _(gp):
            gp.wait_ge(v0, v0b + T + 1)
            gp.dma_start(out=ctx["ys_in"].ap(), in_=yfm(0, S)[:, 0:S]).then_inc(d2, 16)
            cd2.inc(16)
            gp.wait_ge(d2, cd2.v)
            with allow_ici_all_to_all():
                gp.collective_compute(
                    "AllGather", ALU.bypass,
                    replica_groups=[[0, 1], [2, 3], [4, 5], [6, 7]],
                    ins=[ctx["ys_in"].ap().opt()],
                    outs=[ctx["ys_out"].ap().opt()],
                ).then_inc(cc, 1)
            ccc.inc()

    ar.release("ustrip")


def ph_ffn1(ctx):
    nc = ctx["nc"]
    ar = ctx["ar"]
    pb = ctx["pb"]
    d2, cd2 = _sem(ctx, "d2")
    d3, cd3 = _sem(ctx, "d3")
    d4, cd4 = _sem(ctx, "d4")
    d5, cd5 = _sem(ctx, "d5")
    p0, cp0 = _sem(ctx, "p0")
    v0, cv0 = _sem(ctx, "v0")
    s0, cs0 = _sem(ctx, "s0")
    xn1 = lambda kc, a, n: ar.bf("xn1", kc * TOK + a, n)
    x1 = lambda f, a, n: ar.f32("x1", f * TOK + a, n)
    ar.alloc("hslab", 24 * 512 * 2)
    ar.alloc("ftmp", 2 * 512 * 4)
    ar.alloc("wsc", 24 * 128 * 2)
    hs = lambda hc: ar.bf("hslab", hc * 512, 512)
    SC2 = 1.0 / (127.0 * 127.0)

    for tc in range(4):
        with nc.Block() as blk:
            p0b, v0b, s0b = cp0.v, cv0.v, cs0.v
            d2b, d3b, d4b, d5b = cd2.v, cd3.v, cd4.v, cd5.v

            @blk.gpsimd
            def _(gp, tc=tc):
                for cg in range(24):
                    bufg = ctx["wsa"]
                    bufu = ctx["wsb"]
                    if cg >= 1:
                        gp.wait_ge(p0, p0b + 2 * cg - 1)
                    elif tc > 0:
                        gp.wait_ge(p0, p0b)
                    dsem = d2 if cg % 2 == 0 else d3
                    gp.dma_start(out=bufg[:, 0:1024],
                                 in_=bass.AP(ctx["wgT_h"], cg * 128,
                                             [[HID, 128], [128 * HID, 8], [1, 128]])
                                 ).then_inc(dsem, 16)
                    if cg >= 1:
                        gp.wait_ge(p0, p0b + 2 * cg)
                    gp.dma_start(out=bufu[:, 0:1024],
                                 in_=bass.AP(ctx["wuT_h"], cg * 128,
                                             [[HID, 128], [128 * HID, 8], [1, 128]])
                                 ).then_inc(dsem, 16)
                    (cd2 if cg % 2 == 0 else cd3).inc(32)
                for dc in range(8):
                    gp.wait_ge(p0, p0b + 48 + dc) if dc > 0 else None
                    gp.dma_start(out=ar.bf("wsc", 0, 3072),
                                 in_=bass.AP(ctx["wdT_h"], dc * 128,
                                             [[E, 128], [128 * E, 24], [1, 128]])
                                 ).then_inc(d4 if dc % 2 == 0 else d5, 16)
                    (cd4 if dc % 2 == 0 else cd5).inc(16)

            @blk.tensor
            def _(te, tc=tc):
                n = 0
                for cg in range(24):
                    te.wait_ge(d2 if cg % 2 == 0 else d3,
                               (d2b if cg % 2 == 0 else d3b) + 32 * (cg // 2 + 1))
                    bg = pb[cg % 2]
                    bu = pb[2 + cg % 2]
                    if cg >= 2:
                        te.wait_ge(v0, v0b + cg - 1)
                    for kc in range(8):
                        mm = te.matmul(bg[:, 0:512], ctx["wsa"][:, kc * 128:(kc + 1) * 128],
                                       xn1(kc, tc * 512, 512), start=(kc == 0), stop=(kc == 7))
                    mm.then_inc(p0, 1)
                    for kc in range(8):
                        mm = te.matmul(bu[:, 0:512], ctx["wsb"][:, kc * 128:(kc + 1) * 128],
                                       xn1(kc, tc * 512, 512), start=(kc == 0), stop=(kc == 7))
                    mm.then_inc(p0, 1)
                    n += 2
                for dc in range(8):
                    te.wait_ge(d4 if dc % 2 == 0 else d5,
                               (d4b if dc % 2 == 0 else d5b) + 16 * (dc // 2 + 1))
                    bank = pb[4 + dc % 2]
                    if dc == 0:
                        te.wait_ge(v0, v0b + 24)
                    if dc >= 2:
                        te.wait_ge(s0, s0b + 23 + dc)
                    for hc in range(24):
                        mm = te.matmul(bank[:, 0:512], ar.bf("wsc", hc * 128, 128),
                                       hs(hc), start=(hc == 0), stop=(hc == 23))
                    mm.then_inc(p0, 1)
                    n += 1
                cp0.inc(n)

            @blk.scalar
            def _(sc, tc=tc):
                for cg in range(24):
                    sc.wait_ge(p0, p0b + 2 * cg + 2)
                    sc.activation(out=hs(cg), in_=pb[cg % 2][:, 0:512], func=AF.Silu,
                                  scale=1.0 / 127.0).then_inc(s0, 1)
                for dc in range(8):
                    sc.wait_ge(p0, p0b + 48 + dc + 1)
                    if dc >= 2:
                        sc.wait_ge(v0, v0b + 23 + dc)
                    sc.activation(out=ar.f32("ftmp", (dc % 2) * 512, 512),
                                  in_=pb[4 + dc % 2][:, 0:512], func=AF.Copy, scale=SC2).then_inc(s0, 1)
                cs0.inc(32)

            @blk.vector
            def _(v, tc=tc):
                for cg in range(24):
                    v.wait_ge(s0, s0b + cg + 1)
                    v.tensor_tensor(out=hs(cg), in0=hs(cg), in1=pb[2 + cg % 2][:, 0:512],
                                    op=ALU.mult).then_inc(v0, 1)
                for dc in range(8):
                    v.wait_ge(s0, s0b + 24 + dc + 1)
                    v.tensor_tensor(out=x1(dc, tc * 512, 512), in0=x1(dc, tc * 512, 512),
                                    in1=ar.f32("ftmp", (dc % 2) * 512, 512), op=ALU.add).then_inc(v0, 1)
                cv0.inc(32)

    ar.release("hslab", "ftmp", "wsc")


def ph_tail(ctx):
    nc = ctx["nc"]
    ar = ctx["ar"]
    pb = ctx["pb"]
    d2, cd2 = _sem(ctx, "d2")
    d3, cd3 = _sem(ctx, "d3")
    d4, cd4 = _sem(ctx, "d4")
    p0, cp0 = _sem(ctx, "p0")
    v0, cv0 = _sem(ctx, "v0")
    cc, ccc = _sem(ctx, "cc")
    xn1 = lambda kc, a, n: ar.bf("xn1", kc * TOK + a, n)
    x1 = lambda f, a, n: ar.f32("x1", f * TOK + a, n)
    ar.release("yfm")
    ar.alloc("res", 8 * TOK * 4)
    ar.alloc("ysl", 2 * TOK * 2)
    res = lambda f, a, n: ar.f32("res", f * TOK + a, n)
    ysl = lambda kc: ar.bf("ysl", kc * TOK, TOK)

    with nc.Block() as blk:
        d2b, p0b, v0b = cd2.v, cp0.v, cv0.v

        @blk.gpsimd
        def _(gp):
            gp.wait_ge(cc, ccc.v)
            rgE = gp.alloc_register()
            gp.reg_load(rgE, ctx["offs"][0:1, 1:2])
            gp.dma_start(out=ysl(0).rearrange("p t -> p 1 t").broadcast_to([128, 2, TOK])
                         if False else
                         ar.bf("ysl", 0, 2 * TOK).rearrange("p (kc t) -> p kc t", t=TOK),
                         in_=bass.AP(ctx["ys_out"], rgE,
                                     [[S, 128], [128 * S, 2], [1, TOK]])
                         ).then_inc(d2, 16)
            cd2.inc(16)
            gp.dma_start(out=ctx["wsa"][:, 0:1024],
                         in_=bass.AP(ctx["woT_h"], 0, [[E, 128], [1, E]])).then_inc(d2, 16)
            cd2.inc(16)
            gp.dma_start(out=ctx["wsb"][:, 0:1024],
                         in_=bass.AP(ctx["woT_h"], 128 * E, [[E, 128], [1, E]])).then_inc(d2, 16)
            cd2.inc(16)

        @blk.tensor
        def _(te):
            te.wait_ge(d2, d2b + 48)
            n = 0
            for dc in range(8):
                for tcc in range(4):
                    idx = dc * 4 + tcc
                    if idx >= 4:
                        te.wait_ge(v0, v0b + idx - 3)
                    bank = pb[idx % 4]
                    te.matmul(bank[:, 0:512], ctx["wsa"][:, dc * 128:(dc + 1) * 128],
                              ysl(0)[:, tcc * 512:(tcc + 1) * 512], start=True, stop=False)
                    te.matmul(bank[:, 0:512], ctx["wsb"][:, dc * 128:(dc + 1) * 128],
                              ysl(1)[:, tcc * 512:(tcc + 1) * 512], start=False, stop=False)
                    te.matmul(bank[:, 0:512],
                              ctx["biasrows"][0:1, 2560 + dc * 128:2560 + (dc + 1) * 128],
                              ctx["onerb"][0:1, 0:512], start=False, stop=True).then_inc(p0, 1)
                    n += 1
            cp0.inc(n)

        @blk.vector
        def _(v):
            for dc in range(8):
                for tcc in range(4):
                    idx = dc * 4 + tcc
                    v.wait_ge(p0, p0b + idx + 1)
                    v.tensor_tensor(out=res(dc, tcc * 512, 512), in0=pb[idx % 4][:, 0:512],
                                    in1=xn1(dc, tcc * 512, 512), op=ALU.add).then_inc(v0, 1)
            cv0.inc(32)

    ar.release("xn1", "ysl")
    gam = ctx["gbrow"]
    _emit_ln_fm(ctx, src_f32=res, dst_bf=None, n_tok=TOK,
                gamma_fn=lambda kc: gam[0:1, kc * 128:(kc + 1) * 128],
                beta_fn=lambda kc: gam[0:1, 1024 + kc * 128:1024 + (kc + 1) * 128],
                res_dst=lambda f, a, n: ar.f32("res", f * TOK + a, n),
                x1_acc=x1)

    ar.release("res")
    ar.alloc("otile", 2 * E * 4)
    otile = ar.f32("otile", 0, 2 * E)
    with (
        nc.Block() as blk,
    ):
        p0b, v0b, d3b, d4b = cp0.v, cv0.v, cd3.v, cd4.v

        @blk.tensor
        def _(te):
            for tt in range(16):
                for f in range(8):
                    idx = tt * 8 + f
                    if idx >= 4:
                        te.wait_ge(v0, v0b + idx - 3)
                    te.transpose(pb[idx % 4][:, 0:128], x1(f, tt * 128, 128),
                                 ctx["cst"][:, 0:128]).then_inc(p0, 1)
            cp0.inc(128)

        @blk.vector
        def _(v):
            for tt in range(16):
                if tt >= 2:
                    v.wait_ge(d3 if tt % 2 == 0 else d4,
                              (d3b if tt % 2 == 0 else d4b) + 16 * (tt // 2))
                for f in range(8):
                    idx = tt * 8 + f
                    v.wait_ge(p0, p0b + idx + 1)
                    v.tensor_copy(out=otile[:, (tt % 2) * E + f * 128:(tt % 2) * E + (f + 1) * 128],
                                  in_=pb[idx % 4][:, 0:128]).then_inc(v0, 1)
            cv0.inc(128)

        @blk.gpsimd
        def _(gp):
            for tt in range(16):
                gp.wait_ge(v0, v0b + (tt + 1) * 8)
                gp.dma_start(out=ctx["out_x2"][tt * 128:(tt + 1) * 128, :],
                             in_=otile[:, (tt % 2) * E:(tt % 2 + 1) * E]
                             ).then_inc(d3 if tt % 2 == 0 else d4, 16)
                (cd3 if tt % 2 == 0 else cd4).inc(16)
            gp.wait_ge(d3, cd3.v)
            gp.wait_ge(d4, cd4.v)


def ph_debug(ctx):
    nc = ctx["nc"]
    ar = ctx["ar"]
    d2, cd2 = _sem(ctx, "d2")
    with nc.Block() as blk:
        @blk.gpsimd
        def _(gp):
            srcs = {
                "xn0": lambda: ctx["xn0_tm"].ap()[1:4097, :],
                "qf": lambda: ar.bf("qf", 0, 4 * S),
                "kf": lambda: ar.bf("kf", 0, 4 * S),
                "vaug": lambda: ar.bf("vaug", 0, 32 * 520),
                "negm": lambda: ctx["sv"][:, 23:24],
                "st0dbg": lambda: ctx["ss0"][:],
                "svdbg": lambda: ctx["sv"][:, 8:24],
                "srowdbg": lambda: ctx["srow"][0:1, 8:45],
                "xt32dbg": lambda: ctx["arena_t"][:, 38912:43008].bitcast(F32),
                "tmpAdbg": lambda: ctx["arena_t"][:, 34816:38912].bitcast(F32),
                "adT": lambda: ctx["adT"][:],
                "brow": lambda: ctx["brow"][:],
                "y0": lambda: ctx["y0_in"].ap(),
                "x1": lambda: ar.f32("x1", 0, 8 * TOK),
                "xn1": lambda: ar.bf("xn1", 0, 8 * TOK),
                "uown": lambda: ctx["u_in"].ap().rearrange("a b -> (a b)").rearrange(
                    "(p f) -> p f", p=128),
                "yfm": lambda: ar.bf("yfm", 0, S),
                "res": lambda: ar.f32("res", 0, 8 * TOK),
                "xn0e": lambda: ar.bf("xn0e", 0, 8 * EXT),
            }
            for nm, ap in ctx["dbg"].items():
                try:
                    src = srcs[nm]()
                except (KeyError, RuntimeError):
                    continue
                gp.dma_start(out=ap, in_=src).then_inc(d2, 16)
                cd2.inc(16)
            gp.wait_ge(d2, cd2.v)
    if N_PHASES < 12:
        with nc.Block() as blk:
            @blk.gpsimd
            def _(gp):
                gp.dma_start(out=ctx["out_x2"].rearrange("a b -> (a b)").rearrange(
                    "(p f) -> p f", p=128),
                    in_=ctx["arena_t"][:, 0:2 * E * TOK // 128].bitcast(F32)
                    ).then_inc(d2, 16)
                cd2.inc(16)
                gp.wait_ge(d2, cd2.v)


_NC_CACHE = {}


def _host_prepare(inputs):
    x = np.ascontiguousarray(inputs["L0_attn_x"])
    Wa = inputs["L0_attn_W_attn"]; ba = inputs["L0_attn_b_attn"]
    id32 = np.eye(128, dtype=np.float32)
    eye2 = (2 * np.eye(128)).astype(np.float32)
    masks = np.zeros((128, 4 * 512), bnp16)
    for r in range(4):
        k_idx = np.arange(128)[:, None] + r * 128
        q_idx = np.arange(512)[None, :]
        masks[:, r * 512:(r + 1) * 512] = (k_idx <= q_idx).astype(bnp16)
    shared = dict(
        wproj=np.ascontiguousarray(inputs["L0_attn_W_proj"]),
        bproj_row=np.ascontiguousarray(inputs["L0_attn_b_proj"][None, :].astype(bnp16)),
        wrT=np.ascontiguousarray(inputs["L0_ffn_W_r"].T),
        wkT=np.ascontiguousarray(inputs["L0_ffn_W_k"].T),
        wvT=np.ascontiguousarray(inputs["L0_ffn_W_v"].T),
        mur=np.ascontiguousarray(inputs["L0_ffn_mu_r"].reshape(8, 128).T.astype(np.float32)),
        muk=np.ascontiguousarray(inputs["L0_ffn_mu_k"].reshape(8, 128).T.astype(np.float32)),
        wiT=np.ascontiguousarray(inputs["L1_attn_Wi"].T),
        bi_row=np.ascontiguousarray(inputs["L1_attn_bi"][None, :].astype(bnp16)),
        woT=np.ascontiguousarray(inputs["L1_attn_Wo"].T),
        bo_row=np.ascontiguousarray(inputs["L1_attn_bo"][None, :].astype(bnp16)),
        gam_row=np.ascontiguousarray(inputs["L1_attn_gamma"][None, :].astype(bnp16)),
        bet_row=np.ascontiguousarray(inputs["L1_attn_beta"][None, :].astype(bnp16)),
        wgT=np.ascontiguousarray(inputs["L1_ffn_w_gate"].T),
        wuT=np.ascontiguousarray(inputs["L1_ffn_w_up"].T),
        wdT=np.ascontiguousarray(inputs["L1_ffn_w_down"].T),
        a_hip=np.ascontiguousarray(inputs["L1_attn_A_hippo"]),
        id32=id32, eye2=eye2,
        ones_col=np.ones((128, 1), np.float32),
        ones_colb=np.ones((128, 1), bnp16),
        ones_rowb=np.ones((1, 512), bnp16),
        masks=masks,
        hsel=np.concatenate([np.repeat([[1, 0]], 64, 1).astype(np.float32).reshape(1, 128),
                             np.repeat([[0, 1]], 64, 1).astype(np.float32).reshape(1, 128)]),
        ones_rowf=np.ones((1, 128), np.float32),
    )
    maps = []
    for c in range(8):
        b, gg = c // 2, c % 2
        hsl = slice(gg * 512, (gg + 1) * 512)
        m = dict(shared)
        m.update(
            x_b=np.ascontiguousarray(x[b]),
            x_own=np.ascontiguousarray(x[b, gg * TOK:(gg + 1) * TOK]),
            x_bh=np.ascontiguousarray(x[b].astype(bnp16)),
            wattn=np.ascontiguousarray(np.concatenate(
                [Wa[:, hsl], Wa[:, 1024 + gg * 512:1024 + (gg + 1) * 512],
                 Wa[:, 2048 + gg * 512:2048 + (gg + 1) * 512]], axis=1)),
            battn_row=np.ascontiguousarray(np.concatenate(
                [ba[hsl], ba[1024 + gg * 512:1024 + (gg + 1) * 512],
                 ba[2048 + gg * 512:2048 + (gg + 1) * 512]])[None, :].astype(bnp16)),
            bp_g=np.ascontiguousarray(inputs["L1_attn_Bp"][gg]),
            cv_g=np.ascontiguousarray(inputs["L1_attn_Cp"][gg, 0][:, None]),
            ldt_g=np.array([[inputs["L1_attn_log_dt"][gg]]], np.float32),
            sc_g=np.array([[inputs["L1_attn_scale"][gg]]], np.float32),
            off_xel=np.array([[gg * TOK * E]], np.int32),
            off_y0=np.array([[gg * TOK]], np.int32),
            off_uel=np.array([[gg * 128]], np.int32),
        )
        maps.append(m)
    return maps


def kernel(**inputs):
    maps = _host_prepare(inputs)
    key = (N_PHASES, tuple(DBG), SCAN_T, AR_KB)
    if key not in _NC_CACHE:
        _NC_CACHE[key] = build_nc()
    nc = _NC_CACHE[key]
    res = run_bass_kernel_spmd(nc, maps, core_ids=list(range(8)))
    out = np.empty((B, S, E), np.float32)
    for c in range(8):
        b, gg = c // 2, c % 2
        out[b, gg * TOK:(gg + 1) * TOK, :] = res.results[c]["out_x2"]
    globals()["_LAST_RES"] = res
    return out


# version 05
# speedup vs baseline: 1.1328x; 1.0110x over previous
import os
import contextlib
import numpy as np
import ml_dtypes
import concourse.bass as bass
import concourse.mybir as mybir
from concourse.bass_utils import run_bass_kernel_spmd
from concourse.aot_env import allow_ici_all_to_all

F32 = mybir.dt.float32
BF16 = mybir.dt.bfloat16
I32 = mybir.dt.int32
AF = mybir.ActivationFunctionType
ALU = mybir.AluOpType
AX = mybir.AxisListType
Pool = mybir.EngineType.Pool

bnp16 = ml_dtypes.bfloat16

B, S, E = 4, 4096, 1024
TOK = 2048
NST = 128
HID = 3072
EXT = 2176
NS_ITERS = 25

N_PHASES = int(os.environ.get("KPHASES", "99"))
DBG = [d for d in os.environ.get("KDBG", "").split(",") if d]
SCAN_T = int(os.environ.get("KSCANT", str(S)))
AR_KB = int(os.environ.get("KARENA", "180"))

PHASE_NAMES = ["prep", "ns", "qkv", "kquant", "vquant", "mhat", "attn",
               "l0tail", "l1head", "scan", "ffn1", "tail"]


class Ctr:
    def __init__(self):
        self.v = 0

    def inc(self, k=1):
        self.v += k
        return self.v


class Arena:
    def __init__(self, tensor, nbytes):
        self.t = tensor
        self.regions = {}
        self.free = [(0, nbytes)]

    def alloc(self, name, nbytes):
        nbytes = (nbytes + 3) & ~3
        for i, (o, sz) in enumerate(self.free):
            if sz >= nbytes:
                self.regions[name] = (o, nbytes)
                if sz == nbytes:
                    self.free.pop(i)
                else:
                    self.free[i] = (o + nbytes, sz - nbytes)
                return
        raise RuntimeError(f"arena OOM for {name} ({nbytes}B); free={self.free}")

    def release(self, *names):
        for name in names:
            o, sz = self.regions.pop(name)
            self.free.append((o, sz))
        self.free.sort()
        out = []
        for seg in self.free:
            if out and out[-1][0] + out[-1][1] == seg[0]:
                out[-1] = (out[-1][0], out[-1][1] + seg[1])
            else:
                out.append(list(seg))
        self.free = [tuple(s) for s in out]

    def bf(self, name, off_el=0, cols=None):
        o, sz = self.regions[name]
        n = (sz // 2 - off_el) if cols is None else cols
        return self.t[:, o // 2 + off_el: o // 2 + off_el + n]

    def f32(self, name, off_el=0, cols=None):
        o, sz = self.regions[name]
        ap = self.t[:, o // 2: o // 2 + sz // 2].bitcast(F32)
        n = (sz // 4 - off_el) if cols is None else cols
        return ap[:, off_el: off_el + n]

    def i32(self, name, off_el=0, cols=None):
        return self.f32(name, off_el, cols).bitcast(I32)


def build_nc():
    nc = bass.Bass("TRN2", target_bir_lowering=False)
    stack = contextlib.ExitStack()
    ctx = {"nc": nc}

    def inp(name, shape, dt=F32):
        t = nc.dram_tensor(name, shape, dt, kind="ExternalInput")
        ctx[name] = t.ap()
        ctx[name + "_h"] = t

    inp("x_b", [S, E]); inp("x_bh", [S, E], BF16)
    inp("x_own", [TOK, E])
    inp("wattn", [E, 1536]); inp("battn_row", [1, 1536], BF16)
    inp("wproj", [E, E]); inp("bproj_row", [1, E], BF16)
    inp("wrT", [E, E]); inp("wkT", [E, E]); inp("wvT", [E, E])
    inp("mur", [128, 8]); inp("muk", [128, 8])
    inp("wiT", [E, 256]); inp("bi_row", [1, 256], BF16)
    inp("woT", [256, E]); inp("bo_row", [1, E], BF16)
    inp("gam_row", [1, E], BF16); inp("bet_row", [1, E], BF16)
    inp("wgT", [E, HID], I32); inp("wuT", [E, HID], I32); inp("wdT", [HID, E], I32)
    inp("a_hip", [NST, NST]); inp("bp_g", [NST, 1]); inp("cv_g", [NST, 1])
    inp("ldt_g", [1, 1]); inp("sc_g", [1, 1])
    inp("id32", [128, 128]); inp("eye2", [128, 128])
    inp("ones_col", [128, 1]); inp("ones_colb", [128, 1], BF16)
    inp("ones_rowb", [1, 512], BF16)
    inp("masks", [128, 4 * 512], BF16)
    inp("hsel", [2, 128])
    inp("ones_rowf", [1, 128])
    inp("off_xel", [1, 1], I32)
    inp("off_y0", [1, 1], I32)
    inp("off_uel", [1, 1], I32)

    ctx["out_x2"] = nc.dram_tensor("out_x2", [TOK, E], F32, kind="ExternalOutput").ap()
    ctx["dbg"] = {}
    dbg_specs = {
        "xn0": ([4096, E], BF16), "qf": ([128, 4 * S], BF16), "kf": ([128, 4 * S], BF16),
        "vaug": ([128, 32 * 520], BF16), "negm": ([128, 1], F32),
        "y0": ([512, S], BF16), "x1": ([128, 8 * TOK], F32), "xn1": ([128, 8 * TOK], BF16),
        "adT": ([128, 128], BF16), "brow": ([1, 128], BF16), "uown": ([128, 16 * 256], BF16),
        "yfm": ([128, S], BF16), "res": ([128, 8 * TOK], F32), "xn0e": ([128, 8 * EXT], BF16),
        "st0dbg": ([128, 128], BF16), "xt32dbg": ([128, 2048], F32), "tmpAdbg": ([128, 2048], F32),
        "svdbg": ([128, 16], F32), "srowdbg": ([1, 37], F32),
    }
    for nm in DBG:
        shp, dt = dbg_specs[nm]
        ctx["dbg"][nm] = nc.dram_tensor("dbg_" + nm, shp, dt, kind="ExternalOutput").ap()

    for nm, shp, dt in [("xn0_tm", [4352, E], BF16), ("xn0_own_tm", [EXT, E], BF16),
                        ("u_g", [S, 128], BF16),
                        ("wg_bf", [E, HID], BF16), ("wu_bf", [E, HID], BF16),
                        ("wd_bf", [HID, E], BF16),
                        ("y0_in", [512, S], BF16), ("y0_out", [1024, S], BF16),
                        ("u_in", [TOK, 256], BF16), ("u_out", [S, 256], BF16),
                        ("ys_in", [NST, S], BF16), ("ys_out", [2 * NST, S], BF16)]:
        ctx[nm] = nc.dram_tensor(nm, shp, dt)

    ctx["sems"] = {}
    ctx["ctrs"] = {}
    for name in ["d0", "d1", "d2", "d3", "d4", "d5", "p0", "v0", "s0", "g0", "cc"]:
        ctx["sems"][name] = stack.enter_context(nc.semaphore(name))
        ctx["ctrs"][name] = Ctr()

    def sbuf(name, shape, dt=F32):
        t = stack.enter_context(nc.sbuf_tensor(name, shape, dt))
        ctx[name] = t
        return t

    sbuf("cst", [128, 256], F32)
    sbuf("onec", [128, 1], F32)
    sbuf("onecb", [128, 1], BF16)
    sbuf("onerb", [1, 512], BF16)
    sbuf("sv", [128, 32], F32)
    sbuf("srow", [1, 64], F32)
    sbuf("biasrows", [1, 4096], BF16)
    sbuf("gbrow", [1, 2048], BF16)
    sbuf("adT", [128, 128], BF16)
    sbuf("brow", [1, 128], BF16)
    sbuf("ccol", [128, 1], BF16)
    sbuf("ss0", [128, 128], BF16)
    sbuf("ss1", [128, 128], BF16)
    sbuf("wsa", [128, 1024], BF16)
    sbuf("wsb", [128, 1024], BF16)
    sbuf("eringb", [128, 2 * 512], BF16)
    sbuf("offs", [1, 4], I32)
    sbuf("hselt", [2, 128])
    sbuf("onerf", [1, 128])
    sbuf("mixmu", [128, 16])

    arena_t = stack.enter_context(nc.sbuf_tensor("arena", [128, AR_KB * 512], BF16))
    ctx["ar"] = Arena(arena_t, AR_KB * 1024)
    ctx["arena_t"] = arena_t

    ctx["pb"] = [stack.enter_context(nc.psum_tensor(f"pb{i}", [128, 512], F32))
                 for i in range(8)]

    with stack:
        phases = [ph_prep, ph_ns, ph_qkv, ph_kquant, ph_vquant, ph_mhat, ph_attn,
                  ph_l0tail, ph_l1head, ph_scan, ph_ffn1, ph_tail]
        for i, ph in enumerate(phases):
            if i < N_PHASES:
                ph(ctx)
        ph_debug(ctx)
    return nc


def _sem(ctx, n):
    return ctx["sems"][n], ctx["ctrs"][n]


def ph_prep(ctx):
    KPREP = int(os.environ.get("KPREP", "4"))
    KNT = int(os.environ.get("KNT", "32"))
    nc = ctx["nc"]
    ar = ctx["ar"]
    d0, cd0 = _sem(ctx, "d0")
    d1, cd1 = _sem(ctx, "d1")
    d2, cd2 = _sem(ctx, "d2")
    v0, cv0 = _sem(ctx, "v0")
    s0, cs0 = _sem(ctx, "s0")
    g0, cg0 = _sem(ctx, "g0")
    NT = KNT
    ar.alloc("maskt", 4 * 512 * 2)
    ar.alloc("xn0full", 8 * S * 2)
    ar.alloc("tmpA", 8 * 1024)
    ar.alloc("preploc", (2 * E) * 4 + (2 * E) * 2 + (2 * E) * 2 + E * 2)

    d3, cd3 = _sem(ctx, "d3")
    d4, cd4 = _sem(ctx, "d4")
    d5, cd5 = _sem(ctx, "d5")
    v0b, s0b = cv0.v, cs0.v
    d4b, d5b = cd4.v, cd5.v
    const_list = [
        ("id32", ctx["cst"][:, 0:128]), ("eye2", ctx["cst"][:, 128:256]),
        ("ones_col", ctx["onec"][:]), ("ones_colb", ctx["onecb"][:]),
        ("ones_rowb", ctx["onerb"][:]), ("masks", ar.bf("maskt", 0, 4 * 512)),
        ("bp_g", ctx["sv"][:, 0:1]), ("cv_g", ctx["sv"][:, 1:2]),
        ("ldt_g", ctx["sv"][0:1, 2:3]), ("sc_g", ctx["sv"][0:1, 3:4]),
        ("battn_row", ctx["biasrows"][0:1, 0:1536]),
        ("bproj_row", ctx["biasrows"][0:1, 1536:2560]),
        ("bo_row", ctx["biasrows"][0:1, 2560:3584]),
        ("bi_row", ctx["biasrows"][0:1, 3584:3840]),
        ("gam_row", ctx["gbrow"][0:1, 0:1024]),
        ("bet_row", ctx["gbrow"][0:1, 1024:2048]),
        ("hsel", ctx["hselt"][:]),
        ("ones_rowf", ctx["onerf"][:]),
        ("mur", ctx["mixmu"][:, 0:8]),
        ("muk", ctx["mixmu"][:, 8:16]),
        ("off_xel", ctx["offs"][0:1, 0:1]),
        ("off_y0", ctx["offs"][0:1, 1:2]),
        ("off_uel", ctx["offs"][0:1, 2:3]),
    ]
    d0b = cd0.v + len(const_list) * 16
    d1b = cd1.v
    xt32 = ar.f32("preploc", 0, 2 * E)
    xtb = ar.bf("preploc", 4 * E, 2 * E)
    xnt = ar.bf("preploc", 6 * E, 2 * E)
    zrow = ar.bf("preploc", 8 * E, E)[0:1, :]
    sqd = ar.f32("tmpA", 0, 2 * E)
    with (
        nc.sbuf_tensor("st0", [128, 16], F32) as st0,
        nc.sbuf_tensor("st1", [128, 2], F32) as st1,
        nc.Block() as blk,
    ):
        @blk.sync
        def _(sy):
            for src, dst in const_list:
                sy.dma_start(out=dst, in_=ctx[src]).then_inc(d0, 16); cd0.inc(16)
            for i in range(NT):
                if i >= 2:
                    sy.wait_ge(v0, v0b + 2 * (i - 1))
                    sy.wait_ge(s0, s0b + 2 * (i - 2) + 1)
                dsem = d0 if i % 2 == 0 else d1
                sy.dma_start(out=xt32[:, (i % 2) * E:(i % 2 + 1) * E],
                             in_=ctx["x_b"][i * 128:(i + 1) * 128, :]).then_inc(dsem, 16)
                sy.dma_start(out=xtb[:, (i % 2) * E:(i % 2 + 1) * E],
                             in_=ctx["x_bh"][i * 128:(i + 1) * 128, :]).then_inc(dsem, 16)
                (cd0 if i % 2 == 0 else cd1).inc(32)

        @blk.scalar
        def _(sc):
            if KPREP < 2:
                return
            for i in range(NT):
                sl = slice((i % 2) * E, (i % 2 + 1) * E)
                o = (i % 2) * 8
                sc.wait_ge(d0 if i % 2 == 0 else d1,
                           (d0b if i % 2 == 0 else d1b) + 32 * (i // 2 + 1))
                if i >= 2:
                    sc.wait_ge(v0, v0b + 2 * (i - 1) - 1)
                sc.activation(out=sqd[:, (i % 2) * E:(i % 2 + 1) * E], in_=xt32[:, sl], func=AF.Square).then_inc(s0, 1)
                sc.wait_ge(v0, v0b + 2 * i + 1)
                sc.activation(out=st1[:, (i % 2):(i % 2) + 1], in_=st0[:, 3 + o:4 + o], func=AF.Sqrt).then_inc(s0, 1)
            cs0.inc(2 * NT)

        @blk.vector
        def _(v):
            if KPREP < 2:
                v.drain(fusable=False).then_inc(g0, 1)
                return
            v.memset(zrow[:], 0.0)
            v.memset(st0[:], 0.0)
            v.memset(ctx["ss1"][:], 0.0)
            v.memset(ctx["ss0"][:], 0.0).then_inc(g0, 1)
            v.drain()
            for i in range(NT):
                sl = slice((i % 2) * E, (i % 2 + 1) * E)
                o = (i % 2) * 8
                v.wait_ge(s0, s0b + 2 * i + 1)
                v.wait_ge(d0 if i % 2 == 0 else d1,
                          (d0b if i % 2 == 0 else d1b) + 32 * (i // 2 + 1))
                v.tensor_reduce(out=st0[:, 1 + o:2 + o], in_=sqd[:, (i % 2) * E:(i % 2 + 1) * E],
                                axis=AX.X, op=ALU.add)
                v.tensor_reduce(out=st0[:, 0 + o:1 + o], in_=xt32[:, sl], axis=AX.X, op=ALU.add)
                v.drain()
                v.tensor_scalar(out=st0[:, 4 + o:5 + o], in0=st0[:, 0 + o:1 + o],
                                scalar1=1.0 / E, scalar2=None, op0=ALU.mult)
                v.tensor_scalar(out=st0[:, 2 + o:3 + o], in0=st0[:, 1 + o:2 + o],
                                scalar1=1.0 / E, scalar2=None, op0=ALU.mult)
                v.drain()
                v.tensor_copy(out=st0[:, 0 + o:1 + o], in_=st0[:, 4 + o:5 + o])
                v.tensor_tensor(out=st0[:, 7 + o:8 + o], in0=st0[:, 4 + o:5 + o],
                                in1=st0[:, 4 + o:5 + o], op=ALU.mult)
                v.drain()
                v.tensor_tensor(out=st0[:, 3 + o:4 + o], in0=st0[:, 2 + o:3 + o],
                                in1=st0[:, 7 + o:8 + o], op=ALU.subtract)
                v.drain()
                v.tensor_scalar(out=st0[:, 3 + o:4 + o], in0=st0[:, 3 + o:4 + o],
                                scalar1=1e-5, scalar2=None, op0=ALU.add).then_inc(v0, 1)
                v.wait_ge(s0, s0b + 2 * i + 2)
                v.reciprocal(out=st0[:, 5 + o:6 + o], in_=st1[:, (i % 2):(i % 2) + 1])
                v.drain()
                v.tensor_tensor(out=st0[:, 6 + o:7 + o], in0=st0[:, 0 + o:1 + o],
                                in1=st0[:, 5 + o:6 + o], op=ALU.mult)
                v.drain()
                v.tensor_scalar(out=st0[:, 6 + o:7 + o], in0=st0[:, 6 + o:7 + o],
                                scalar1=-1.0, scalar2=None, op0=ALU.mult)
                v.drain()
                if i >= 2:
                    j = i - 2
                    if j % 2 == 0:
                        v.wait_ge(d4, d4b + 32 + 16 * (j // 2 + 1))
                    else:
                        v.wait_ge(d5, d5b + 16 * (j // 2 + 1))
                v.tensor_tensor(out=xnt[:, sl], in0=xt32[:, sl],
                                in1=st0[:, 5 + o:6 + o].broadcast_to([128, E]), op=ALU.mult)
                v.drain()
                if i == 0:
                    v.tensor_copy(out=ctx["ss0"][:, 0:16], in_=st0[:, 0:16])
                    v.drain()
                if i == NT - 1 and os.environ.get("KRERED"):
                    v.tensor_reduce(out=st0[:, 8:9], in_=xt32[:, 0:E], axis=AX.X, op=ALU.add)
                    v.tensor_reduce(out=st0[:, 9:10], in_=sqd[:, 0:E], axis=AX.X, op=ALU.add)
                    v.drain()
                    v.tensor_copy(out=ctx["ss0"][:, 16:32], in_=st0[:, 0:16])
                    v.tensor_copy(out=ctx["ss0"][:, 32:34], in_=st0[:, 8:10])
                    v.drain()
                v.tensor_tensor(out=xnt[:, sl], in0=xnt[:, sl],
                                in1=st0[:, 6 + o:7 + o].broadcast_to([128, E]),
                                op=ALU.add).then_inc(v0, 1)
            cv0.inc(2 * NT)
        cg0.inc()

        @blk.gpsimd
        def _(gp):
            if KPREP < 3:
                return
            gp.wait_ge(g0, cg0.v)
            gp.dma_start(out=ctx["xn0_tm"].ap()[0:1, :], in_=zrow[:]).then_inc(d4, 16); cd4.inc(16)
            gp.dma_start(out=bass.AP(ctx["xn0_tm"], 4097 * E, [[0, 1], [E, 255], [1, E]]),
                         in_=zrow.unsqueeze(1).broadcast_to([1, 255, E])
                         ).then_inc(d4, 16); cd4.inc(16)
            for i in range(NT):
                sl = slice((i % 2) * E, (i % 2 + 1) * E)
                gp.wait_ge(v0, v0b + 2 * i + 2)
                gp.dma_start(out=ctx["xn0_tm"].ap()[1 + i * 128:1 + (i + 1) * 128, :],
                             in_=xnt[:, sl]).then_inc(d4 if i % 2 == 0 else d5, 16)
                (cd4 if i % 2 == 0 else cd5).inc(16)

    if KPREP < 4:
        return
    with nc.Block() as blk:
        @blk.gpsimd
        def _(gp):
            gp.wait_ge(d0, cd0.v)
            gp.wait_ge(d4, cd4.v)
            gp.wait_ge(d5, cd5.v)
            rgA = gp.alloc_register()
            gp.reg_load(rgA, ctx["offs"][0:1, 0:1])
            gp.dma_start(out=ctx["xn0_own_tm"].ap(),
                         in_=bass.AP(ctx["xn0_tm"], rgA, [[E, EXT], [1, E]])
                         ).then_inc(d2, 16)
            cd2.inc(16)

        @blk.sync
        def _(sy):
            sy.wait_ge(d2, cd2.v)
            for f in range(8):
                sy.dma_start_transpose(ar.bf("xn0full", f * S, S),
                                       ctx["xn0_tm"].ap()[1:4097, f * 128:(f + 1) * 128]
                                       ).then_inc(d0, 16)
                cd0.inc(16)

        @blk.scalar
        def _(sc):
            sc.wait_ge(d0, cd0.v)

    ar.release("preploc")


def ph_ns(ctx):
    nc = ctx["nc"]
    ar = ctx["ar"]
    pb = ctx["pb"]
    p0, cp0 = _sem(ctx, "p0")
    v0, cv0 = _sem(ctx, "v0")
    s0, cs0 = _sem(ctx, "s0")
    g0, cg0 = _sem(ctx, "g0")
    d0, cd0 = _sem(ctx, "d0")
    ID = ctx["cst"][:, 0:128]
    EY2 = ctx["cst"][:, 128:256]
    sv = ctx["sv"]
    srow = ctx["srow"]

    with (
        nc.sbuf_tensor("ns", [128, 8 * 128], F32) as ns,
        nc.Block() as blk,
    ):
        C = lambda k: ns[:, k * 128:(k + 1) * 128]

        @blk.sync
        def _(sy):
            sy.dma_start(out=C(7), in_=ctx["a_hip"]).then_inc(d0, 16); cd0.inc(16)

        @blk.scalar
        def _(sc):
            sc.wait_ge(d0, cd0.v)
            sc.activation(out=sv[0:1, 4:5], in_=sv[0:1, 2:3], func=AF.Exp).then_inc(s0, 1)
            cs0.inc()

        @blk.gpsimd
        def _(gp):
            d5c, cd5c = _sem(ctx, "d5")
            for wsrc, wdst in [("wgT_h", "wg_bf"), ("wuT_h", "wu_bf"), ("wdT_h", "wd_bf")]:
                gp.dma_start(out=ctx[wdst].ap().rearrange("a b -> (a b)"),
                             in_=ctx[wsrc].ap().rearrange("a b -> (a b)")
                             ).then_inc(d5c, 16)
                cd5c.inc(16)
                gp.wait_ge(ctx["sems"]["g0"], 0)
        ctx["ffn_cast_d5"] = ctx["ctrs"]["d5"].v

        def pe(fn):
            @blk.tensor
            def _(te):
                fn(te)

        def ve(fn):
            @blk.vector
            def _(v):
                fn(v)

        def do_v(fn):
            def g(v):
                fn(v)
                v.drain(fusable=False).then_inc(v0, 1)
            ve(g); cv0.inc()

        def do_p(fn):
            def g(te):
                fn(te)
                te.drain(fusable=False).then_inc(p0, 1)
            pe(g); cp0.inc()

        do_p(lambda te: (te.wait_ge(s0, cs0.v),
                         te.matmul(pb[1][:, 0:2], ctx["onerf"][:], sv[0:1, 3:5],
                                   start=True, stop=True)))
        do_v(lambda v: (v.wait_ge(p0, cp0.v),
                        v.tensor_copy(out=sv[:, 5:6], in_=pb[1][:, 1:2]),
                        v.tensor_copy(out=sv[:, 6:7], in_=pb[1][:, 0:1])))
        do_v(lambda v: (v.tensor_tensor(out=C(0), in0=C(7), in1=sv[:, 6:7].broadcast_to([128, 128]), op=ALU.mult),
                        v.tensor_scalar(out=sv[:, 7:8], in0=sv[:, 5:6], scalar1=0.5, scalar2=None, op0=ALU.mult),
                        v.drain(),
                        v.tensor_tensor(out=C(0), in0=C(0), in1=sv[:, 7:8].broadcast_to([128, 128]), op=ALU.mult)))
        do_p(lambda te: (te.wait_ge(v0, cv0.v), te.transpose(pb[0][:, 0:128], C(0), ID)))
        do_v(lambda v: (v.wait_ge(p0, cp0.v),
                        v.tensor_tensor(out=C(1), in0=ID, in1=C(0), op=ALU.add),
                        v.tensor_tensor(out=C(2), in0=ID, in1=pb[0][:, 0:128], op=ALU.add),
                        v.tensor_tensor(out=C(3), in0=ID, in1=C(0), op=ALU.subtract)))

        @blk.scalar
        def _(sc):
            sc.wait_ge(v0, cv0.v)
            sc.activation(out=C(7), in_=C(1), func=AF.Abs).then_inc(s0, 1)
            cs0.inc()

        do_p(lambda te: (te.wait_ge(s0, cs0.v),
                         te.matmul(pb[1][0:1, 0:128], ctx["onec"][:], C(7), start=True, stop=True),
                         te.transpose(pb[0][:, 128:256], C(7), ID)))
        do_v(lambda v: (v.wait_ge(p0, cp0.v), v.tensor_copy(out=C(7), in_=pb[0][:, 128:256])))
        do_p(lambda te: (te.wait_ge(v0, cv0.v),
                         te.matmul(pb[1][0:1, 128:256], ctx["onec"][:], C(7), start=True, stop=True)))
        do_v(lambda v: (v.wait_ge(p0, cp0.v),
                        v.tensor_reduce(out=srow[0:1, 0:1], in_=pb[1][0:1, 0:128], axis=AX.X, op=ALU.max),
                        v.tensor_reduce(out=srow[0:1, 1:2], in_=pb[1][0:1, 128:256], axis=AX.X, op=ALU.max),
                        v.drain(),
                        v.tensor_tensor(out=srow[0:1, 2:3], in0=srow[0:1, 0:1], in1=srow[0:1, 1:2], op=ALU.mult),
                        v.drain(),
                        v.reciprocal(out=srow[0:1, 3:4], in_=srow[0:1, 2:3])))

        do_p(lambda te: (te.wait_ge(v0, cv0.v),
                         te.matmul(pb[1][:, 2:3], ctx["onerf"][:], srow[0:1, 3:4],
                                   start=True, stop=True)))
        do_v(lambda v: (v.wait_ge(p0, cp0.v),
                        v.tensor_copy(out=sv[:, 8:9], in_=pb[1][:, 2:3])))
        do_v(lambda v: (v.tensor_tensor(out=C(4), in0=C(2), in1=sv[:, 8:9].broadcast_to([128, 128]), op=ALU.mult),
                        v.tensor_tensor(out=C(5), in0=C(1), in1=sv[:, 8:9].broadcast_to([128, 128]), op=ALU.mult)))

        for it in range(NS_ITERS):
            do_p(lambda te: (te.wait_ge(v0, cv0.v),
                             te.matmul(pb[2][:, 0:128], C(2), C(4), start=True, stop=True)))
            do_v(lambda v: (v.wait_ge(p0, cp0.v),
                            v.tensor_tensor(out=C(6), in0=EY2, in1=pb[2][:, 0:128], op=ALU.subtract)))
            do_p(lambda te: (te.wait_ge(v0, cv0.v),
                             te.matmul(pb[2][:, 128:256], C(6), C(5), start=True, stop=True)))
            do_v(lambda v: (v.wait_ge(p0, cp0.v), v.tensor_copy(out=C(5), in_=pb[2][:, 128:256])))
            do_p(lambda te: (te.wait_ge(v0, cv0.v),
                             te.transpose(pb[2][:, 256:384], C(5), ID)))
            do_v(lambda v: (v.wait_ge(p0, cp0.v), v.tensor_copy(out=C(4), in_=pb[2][:, 256:384])))

        do_v(lambda v: v.tensor_tensor(out=sv[:, 9:10], in0=sv[:, 0:1], in1=sv[:, 5:6], op=ALU.mult))
        do_p(lambda te: (te.wait_ge(v0, cv0.v),
                         te.matmul(pb[3][:, 0:128], C(3), C(5), start=True, stop=True),
                         te.matmul(pb[3][:, 128:129], C(5), sv[:, 9:10], start=True, stop=True)))
        do_v(lambda v: (v.wait_ge(p0, cp0.v),
                        v.tensor_copy(out=ctx["adT"][:], in_=pb[3][:, 0:128]),
                        v.tensor_copy(out=sv[:, 10:11], in_=pb[3][:, 128:129]),
                        v.tensor_copy(out=ctx["ccol"][:], in_=sv[:, 1:2])))
        do_p(lambda te: (te.wait_ge(v0, cv0.v),
                         te.transpose(pb[3][0:1, 256:384], sv[:, 10:11], ID)))
        do_v(lambda v: (v.wait_ge(p0, cp0.v),
                        v.tensor_copy(out=ctx["brow"][:], in_=pb[3][0:1, 256:384])))


def ph_qkv(ctx):
    nc = ctx["nc"]
    ar = ctx["ar"]
    ar.alloc("qf", 4 * S * 2)
    ar.alloc("kf", 4 * S * 2)
    ar.alloc("vaug", 32 * 520 * 2)
    pb = ctx["pb"]
    d0, cd0 = _sem(ctx, "d0")
    d2, cd2 = _sem(ctx, "d2")
    d3, cd3 = _sem(ctx, "d3")
    p0, cp0 = _sem(ctx, "p0")
    v0, cv0 = _sem(ctx, "v0")
    s0, cs0 = _sem(ctx, "s0")
    xn0 = lambda kc, a, n: ar.bf("xn0full", kc * S + a, n)
    qfv = lambda c, a, n: ar.bf("qf", c * S + a, n)
    kfv = lambda c, a, n: ar.bf("kf", c * S + a, n)
    vaug = lambda a, n: ar.bf("vaug", a, n)
    cum = lambda ncg: 8 * min(ncg, 8) + 32 * max(0, ncg - 8)

    p0b, v0b, s0b = cp0.v, cv0.v, cs0.v
    d2b, d3b = cd2.v, cd3.v
    with nc.Block() as blk:
        @blk.vector
        def _(v):
            v.memset(vaug(0, 32 * 520), 1.0).then_inc(v0, 1)
        cv0.inc()
        v0b = cv0.v

        @blk.gpsimd
        def _(gp):
            for cg in range(12):
                buf = ctx["wsa"] if cg % 2 == 0 else ctx["wsb"]
                if cg >= 2:
                    gp.wait_ge(p0, p0b + cum(cg - 1))
                gp.dma_start(out=buf[:, 0:1024],
                             in_=bass.AP(ctx["wattn_h"], cg * 128,
                                         [[1536, 128], [128 * 1536, 8], [1, 128]])
                             ).then_inc(d2 if cg % 2 == 0 else d3, 16)
                (cd2 if cg % 2 == 0 else cd3).inc(16)

        @blk.tensor
        def _(te):
            te.wait_ge(v0, v0b)
            te.wait_ge(d0, cd0.v)
            n = 0
            for cg in range(12):
                buf = ctx["wsa"] if cg % 2 == 0 else ctx["wsb"]
                te.wait_ge(d2 if cg % 2 == 0 else d3,
                           (d2b if cg % 2 == 0 else d3b) + 16 * (cg // 2 + 1))
                if cg < 8:
                    for tc in range(8):
                        j = cg * 8 + tc
                        if j >= 4:
                            te.wait_ge(s0, s0b + (j - 3))
                        bank = pb[tc % 4]
                        for kc in range(8):
                            te.matmul(bank[:, 0:512], buf[:, kc * 128:(kc + 1) * 128],
                                      xn0(kc, tc * 512, 512), start=(kc == 0), stop=False)
                        te.matmul(bank[:, 0:512], ctx["biasrows"][0:1, cg * 128:(cg + 1) * 128],
                                  ctx["onerb"][0:1, 0:512], start=False, stop=True).then_inc(p0, 1)
                        n += 1
                else:
                    for tc in range(8):
                        for tt in range(4):
                            idx = (cg - 8) * 32 + tc * 4 + tt
                            if idx >= 2:
                                te.wait_ge(v0, v0b + (idx - 1))
                            t0 = tc * 512 + tt * 128
                            bank = pb[4 + idx % 2]
                            for kc in range(8):
                                te.matmul(bank[:, 0:128], xn0(kc, t0, 128),
                                          buf[:, kc * 128:(kc + 1) * 128],
                                          start=(kc == 0), stop=False)
                            te.matmul(bank[:, 0:128], ctx["onerb"][0:1, 0:128],
                                      ctx["biasrows"][0:1, cg * 128:(cg + 1) * 128],
                                      start=False, stop=True).then_inc(p0, 1)
                            n += 1
            cp0.inc(n)

        @blk.scalar
        def _(sc):
            for cg in range(8):
                for tc in range(8):
                    sc.wait_ge(p0, p0b + cg * 8 + tc + 1)
                    bank = pb[tc % 4]
                    if cg < 4:
                        sc.activation(out=qfv(cg, tc * 512, 512), in_=bank[:, 0:512],
                                      func=AF.Copy, scale=0.125).then_inc(s0, 1)
                    else:
                        sc.activation(out=kfv(cg - 4, tc * 512, 512), in_=bank[:, 0:512],
                                      func=AF.Copy).then_inc(s0, 1)
            cs0.inc(64)

        @blk.vector
        def _(v):
            for cg in range(8, 12):
                for tc in range(8):
                    for tt in range(4):
                        idx = (cg - 8) * 32 + tc * 4 + tt
                        v.wait_ge(p0, p0b + 64 + idx + 1)
                        bank = pb[4 + idx % 2]
                        tile = tc * 4 + tt
                        hh = (cg - 8) * 2
                        v.tensor_copy(out=vaug(tile * 520 + hh * 65, 64), in_=bank[:, 0:64])
                        v.tensor_copy(out=vaug(tile * 520 + (hh + 1) * 65, 64), in_=bank[:, 64:128]).then_inc(v0, 1)
            cv0.inc(128)


def ph_kquant(ctx):
    nc = ctx["nc"]
    ar = ctx["ar"]
    pb = ctx["pb"]
    p0, cp0 = _sem(ctx, "p0")
    v0, cv0 = _sem(ctx, "v0")
    s0, cs0 = _sem(ctx, "s0")
    d2, cd2 = _sem(ctx, "d2")
    sv, srow = ctx["sv"], ctx["srow"]
    QL = S - 32
    kfv = lambda c, a, n: ar.bf("kf", c * S + a, n)

    with (
        nc.sbuf_tensor("kst", [128, 8], F32) as kst,
        nc.sbuf_tensor("kstt", [8, 128], F32) as kstt,
        nc.sbuf_tensor("st8", [8, 4], F32) as st8,
        nc.sbuf_tensor("svt", [2, 12], F32) as svt,
        nc.Block() as blk,
    ):
        @blk.vector
        def _(v):
            v.wait_ge(s0, cs0.v)
            for c in range(4):
                v.tensor_reduce(out=kst[:, c:c + 1], in_=kfv(c, 0, QL), axis=AX.X, op=ALU.min)
                mmv = v.tensor_reduce(out=kst[:, 4 + c:5 + c], in_=kfv(c, 0, QL),
                                      axis=AX.X, op=ALU.max)
            mmv.then_inc(v0, 1)
            cv0.inc()

        @blk.tensor
        def _(te):
            te.wait_ge(v0, cv0.v)
            te.transpose(pb[0][0:8, 0:128], kst[:, 0:8], ctx["cst"][:, 0:128]).then_inc(p0, 1)
            cp0.inc()

        @blk.vector
        def _(v):
            v.wait_ge(p0, cp0.v)
            v.tensor_copy(out=kstt[:], in_=pb[0][0:8, 0:128])
            v.drain()
            k3 = kstt[0:8, :].rearrange("p (h c) -> p h c", c=64)
            v.tensor_reduce(out=st8[0:8, 0:2], in_=k3, axis=AX.X, op=ALU.min)
            v.tensor_reduce(out=st8[0:8, 2:4], in_=k3, axis=AX.X, op=ALU.max).then_inc(v0, 1)
            cv0.inc()

        @blk.gpsimd
        def _(gp):
            gp.wait_ge(v0, cv0.v)
            gp.dma_start(out=srow[0:1, 8:16], in_=st8[0:4, 0:2]).then_inc(d2, 16)
            cd2.inc(16)
            gp.dma_start(out=srow[0:1, 16:24], in_=st8[4:8, 2:4]).then_inc(d2, 16)
            cd2.inc(16)

        @blk.vector
        def _(v):
            v.wait_ge(d2, cd2.v)
            v.tensor_scalar(out=srow[0:1, 8:24], in0=srow[0:1, 8:24],
                            scalar1=-1e4, scalar2=1e4, op0=ALU.max, op1=ALU.min)
            v.drain()
            v.tensor_tensor(out=srow[0:1, 24:32], in0=srow[0:1, 16:24], in1=srow[0:1, 8:16],
                            op=ALU.subtract)
            v.drain()
            v.tensor_scalar(out=srow[0:1, 24:32], in0=srow[0:1, 24:32],
                            scalar1=1.0 / 3.0, scalar2=1e-8, op0=ALU.mult, op1=ALU.max)
            v.drain()
            v.reciprocal(out=srow[0:1, 32:40], in_=srow[0:1, 24:32])
            v.drain()
            v.tensor_copy(out=srow[0:1, 16:24], in_=srow[0:1, 8:16]).then_inc(v0, 1)
            cv0.inc()

        @blk.gpsimd
        def _(gp):
            gp.wait_ge(v0, cv0.v)
            srci = srow[0:1, 16:40].rearrange("p (a h) -> p a h", h=2)
            with nc.allow_non_contiguous_dma(reason="24-element stat shuffle"):
                gp.dma_start(out=svt[0:1, :].unsqueeze(2),
                             in_=srci[:, :, 0:1]).then_inc(d2, 16)
                cd2.inc(16)
                gp.dma_start(out=svt[1:2, :].unsqueeze(2),
                             in_=srci[:, :, 1:2]).then_inc(d2, 16)
                cd2.inc(16)

        @blk.tensor
        def _(te):
            te.wait_ge(d2, cd2.v)
            te.matmul(pb[1][:, 0:12], ctx["hselt"][:], svt[:], start=True, stop=True).then_inc(p0, 1)
            cp0.inc()

        @blk.vector
        def _(v):
            v.wait_ge(p0, cp0.v)
            v.tensor_copy(out=sv[:, 11:23], in_=pb[1][:, 0:12]).then_inc(v0, 1)
            cv0.inc()

        @blk.vector
        def _(v):
            kqt = ar.f32("tmpA", 0, 512)
            kqr = ar.f32("tmpA", 512, 512)
            kqi = ar.i32("tmpA", 1024, 512)
            for c in range(4):
                mn, scl, rsc = sv[:, 11 + c:12 + c], sv[:, 15 + c:16 + c], sv[:, 19 + c:20 + c]
                for bi in range(8):
                    lo = bi * 512
                    w = min(512, QL - lo)
                    v.tensor_scalar(out=kqt[:, 0:w], in0=kfv(c, lo, w),
                                    scalar1=-1e4, scalar2=1e4, op0=ALU.max, op1=ALU.min)
                    v.drain()
                    v.tensor_tensor(out=kqt[:, 0:w], in0=kqt[:, 0:w],
                                    in1=mn.broadcast_to([128, w]), op=ALU.subtract)
                    v.drain()
                    v.tensor_tensor(out=kqt[:, 0:w], in0=kqt[:, 0:w],
                                    in1=rsc.broadcast_to([128, w]), op=ALU.mult)
                    v.drain()
                    v.tensor_copy(out=kqi[:, 0:w], in_=kqt[:, 0:w])
                    v.drain()
                    v.tensor_copy(out=kqr[:, 0:w], in_=kqi[:, 0:w])
                    v.drain()
                    v.tensor_tensor(out=kqt[:, 0:w], in0=kqt[:, 0:w],
                                    in1=kqr[:, 0:w], op=ALU.subtract)
                    v.drain()
                    v.tensor_scalar(out=kqt[:, 0:w], in0=kqt[:, 0:w],
                                    scalar1=0.5, scalar2=None, op0=ALU.is_ge)
                    v.drain()
                    v.tensor_tensor(out=kqr[:, 0:w], in0=kqr[:, 0:w],
                                    in1=kqt[:, 0:w], op=ALU.add)
                    v.drain()
                    v.tensor_tensor(out=kqr[:, 0:w], in0=kqr[:, 0:w],
                                    in1=scl.broadcast_to([128, w]), op=ALU.mult)
                    v.drain()
                    v.tensor_tensor(out=kfv(c, lo, w), in0=kqr[:, 0:w],
                                    in1=mn.broadcast_to([128, w]), op=ALU.add)
                    v.drain()
            v.drain(fusable=False).then_inc(v0, 1); cv0.inc()


def ph_vquant(ctx):
    nc = ctx["nc"]
    ar = ctx["ar"]
    v0, cv0 = _sem(ctx, "v0")
    vaug = lambda a, n: ar.bf("vaug", a, n)

    ar.alloc("vqtmp", (32 + 512 * 3) * 4)
    vst = ar.f32("vqtmp", 0, 32)
    vtmp = ar.f32("vqtmp", 32, 512)
    vtr = ar.f32("vqtmp", 544, 512)
    vti = ar.i32("vqtmp", 1056, 512)
    with (
        nc.Block() as blk,
    ):
        @blk.vector
        def _(v):
            for tile in range(32):
                nr = 128 if tile < 31 else 96
                sl3 = vaug(tile * 520, 520)[0:nr, :].rearrange(
                    "p (h c) -> p h c", c=65)[:, :, 0:64]
                vt3 = vtmp[0:nr, :].rearrange("p (h c) -> p h c", c=64)
                vi3 = vti[0:nr, :].rearrange("p (h c) -> p h c", c=64)
                mn = vst[0:nr, 0:8]
                scl = vst[0:nr, 16:24]
                rsc = vst[0:nr, 24:32]
                bc = lambda t: t.unsqueeze(2).broadcast_to([nr, 8, 64])
                v.drain()
                v.tensor_scalar(out=vt3, in0=sl3, scalar1=-1e4, scalar2=1e4,
                                op0=ALU.max, op1=ALU.min)
                v.drain()
                v.tensor_reduce(out=mn, in_=vt3, axis=AX.X, op=ALU.min)
                v.tensor_reduce(out=vst[0:nr, 8:16], in_=vt3, axis=AX.X, op=ALU.max)
                v.drain()
                v.tensor_tensor(out=scl, in0=vst[0:nr, 8:16], in1=mn, op=ALU.subtract)
                v.drain()
                v.tensor_scalar(out=scl, in0=scl, scalar1=1.0 / 3.0, scalar2=1e-8,
                                op0=ALU.mult, op1=ALU.max)
                v.drain()
                v.reciprocal(out=rsc, in_=scl)
                v.drain()
                v.tensor_tensor(out=vt3, in0=vt3, in1=bc(mn), op=ALU.subtract)
                v.drain()
                v.tensor_tensor(out=vt3, in0=vt3, in1=bc(rsc), op=ALU.mult)
                v.drain()
                v.tensor_copy(out=vti[0:nr, :], in_=vtmp[0:nr, :])
                v.drain()
                v.tensor_copy(out=vtr[0:nr, :], in_=vti[0:nr, :])
                v.drain()
                v.tensor_tensor(out=vtmp[0:nr, :], in0=vtmp[0:nr, :],
                                in1=vtr[0:nr, :], op=ALU.subtract)
                v.drain()
                v.tensor_scalar(out=vtmp[0:nr, :], in0=vtmp[0:nr, :],
                                scalar1=0.5, scalar2=None, op0=ALU.is_ge)
                v.drain()
                v.tensor_tensor(out=vtr[0:nr, :], in0=vtr[0:nr, :],
                                in1=vtmp[0:nr, :], op=ALU.add)
                v.drain()
                vr3 = vtr[0:nr, :].rearrange("p (h c) -> p h c", c=64)
                v.tensor_tensor(out=vr3, in0=vr3, in1=bc(scl), op=ALU.mult)
                v.drain()
                mmv = v.tensor_tensor(out=sl3, in0=vr3, in1=bc(mn), op=ALU.add)
            mmv.then_inc(v0, 1)
            cv0.inc()
    ar.release("vqtmp")


def ph_mhat(ctx):
    nc = ctx["nc"]
    ar = ctx["ar"]
    pb = ctx["pb"]
    p0, cp0 = _sem(ctx, "p0")
    v0, cv0 = _sem(ctx, "v0")
    s0, cs0 = _sem(ctx, "s0")
    sv, srow = ctx["sv"], ctx["srow"]
    sq = lambda a, n: ar.bf("tmpA", a, n)

    p0b, v0b = cp0.v, cv0.v
    with nc.Block() as blk:
        @blk.vector
        def _(v):
            v.wait_ge(s0, cs0.v)
            for c in range(4):
                if c >= 1:
                    v.wait_ge(p0, p0b + c)
                buf = ar.bf("qf", c * S, S)
                v.tensor_tensor(out=sq(0, S), in0=buf, in1=buf, op=ALU.mult).then_inc(v0, 1)
            v.wait_ge(p0, p0b + 4)
            for tc in range(8):
                v.tensor_reduce(out=srow[0:1, 48 + tc:49 + tc], in_=pb[tc][0:1, 0:512],
                                axis=AX.X, op=ALU.max)
            v.drain()
            v.tensor_reduce(out=srow[0:1, 40:41], in_=srow[0:1, 48:56], axis=AX.X, op=ALU.max).then_inc(v0, 1)
            for c in range(4):
                v.wait_ge(p0, p0b + 4 + c)
                buf = ar.bf("kf", c * S, S)
                v.tensor_tensor(out=sq(0, S), in0=buf, in1=buf, op=ALU.mult).then_inc(v0, 1)
            v.wait_ge(p0, p0b + 8)
            for tc in range(8):
                v.tensor_reduce(out=srow[0:1, 48 + tc:49 + tc], in_=pb[tc][0:1, 0:512],
                                axis=AX.X, op=ALU.max)
            v.drain()
            v.tensor_reduce(out=srow[0:1, 41:42], in_=srow[0:1, 48:56], axis=AX.X, op=ALU.max)
            v.drain()
            v.tensor_tensor(out=srow[0:1, 42:43], in0=srow[0:1, 40:41], in1=srow[0:1, 41:42],
                            op=ALU.mult).then_inc(v0, 1)
            cv0.inc(10)

        @blk.tensor
        def _(te):
            for half in range(2):
                for c in range(4):
                    te.wait_ge(v0, v0b + half * 5 + c + 1)
                    for tc in range(8):
                        mm = te.matmul(pb[tc][0:1, 0:512], ctx["onecb"][:], sq(tc * 512, 512),
                                       start=(c == 0), stop=(c == 3))
                        if tc == 7:
                            mm.then_inc(p0, 1)
            cp0.inc(8)

        @blk.scalar
        def _(sc):
            sc.wait_ge(v0, cv0.v)
            sc.activation(out=srow[0:1, 43:44], in_=srow[0:1, 42:43], func=AF.Sqrt).then_inc(s0, 1)
            cs0.inc()

        @blk.vector
        def _(v):
            v.wait_ge(s0, cs0.v)
            v.tensor_scalar(out=srow[0:1, 44:45], in0=srow[0:1, 43:44], scalar1=-1.0,
                            scalar2=None, op0=ALU.mult).then_inc(v0, 1)
            cv0.inc()

        @blk.tensor
        def _(te):
            te.wait_ge(v0, cv0.v)
            te.matmul(pb[1][:, 16:17], ctx["onerf"][:], srow[0:1, 44:45],
                      start=True, stop=True).then_inc(p0, 1)
            cp0.inc()

        @blk.vector
        def _(v):
            v.wait_ge(p0, cp0.v)
            v.tensor_copy(out=sv[:, 23:24], in_=pb[1][:, 16:17]).then_inc(v0, 1)
            cv0.inc()


def _attn_seq():
    seq = []
    for h in range(8):
        for j in range(8):
            seq.append((h, j, 4 * (j + 1)))
    return seq


def ph_attn(ctx):
    nc = ctx["nc"]
    ar = ctx["ar"]
    pb = ctx["pb"]
    p0, cp0 = _sem(ctx, "p0")
    v0, cv0 = _sem(ctx, "v0")
    s0, cs0 = _sem(ctx, "s0")
    d3, cd3 = _sem(ctx, "d3")
    sv = ctx["sv"]
    ering = ctx["eringb"]
    qfb = lambda c: ar.bf("qf", c * S, S)
    kfb = lambda c: ar.bf("kf", c * S, S)
    vaug = lambda a, n: ar.bf("vaug", a, n)
    seq = _attn_seq()

    p0b, v0b, s0b, d3b = cp0.v, cv0.v, cs0.v, cd3.v

    def replay():
        pe, act, dve = 0, 0, 0
        ev = []
        for (h, j, nk) in seq:
            for i in range(nk):
                pe += 1; ev.append(("qk", h, j, i, pe, act, dve))
                act += 1; ev.append(("exp", h, j, i, pe, act, dve))
                if i >= 4 * j:
                    dve += 1; ev.append(("mask", h, j, i, pe, act, dve))
                if i >= 1:
                    pe += 1; ev.append(("av", h, j, i - 1, pe, act, dve))
            pe += 1; ev.append(("av", h, j, nk - 1, pe, act, dve))
            dve += 1; ev.append(("recip", h, j, -1, pe, act, dve))
            pe += 1; ev.append(("bcast", h, j, -1, pe, act, dve))
            dve += 1; ev.append(("ymul", h, j, -1, pe, act, dve))
            ev.append(("ydma", h, j, -1, pe, act, dve))
        return ev

    ev = replay()
    exp_after_count = {}
    av_after = {}
    bcast_after = {}
    ymul_after = {}
    recip_after = {}
    qk_after = {}
    mask_after = {}
    for (kind, h, j, i, pe, act, dve) in ev:
        if kind == "exp":
            exp_after_count[(h, j, i)] = act
        elif kind == "av":
            av_after[(h, j, i)] = pe
        elif kind == "qk":
            qk_after[(h, j, i)] = pe
        elif kind == "mask":
            mask_after[(h, j, i)] = dve
        elif kind == "bcast":
            bcast_after[(h, j)] = pe
        elif kind == "ymul":
            ymul_after[(h, j)] = dve
        elif kind == "recip":
            recip_after[(h, j)] = dve

    with (
        nc.sbuf_tensor("zrec", [1, 512], F32) as zrec,
        nc.sbuf_tensor("zbc", [64, 512], F32) as zbc,
        nc.sbuf_tensor("ytile", [64, 512], BF16) as ytile,
        nc.Block() as blk,
    ):
        @blk.tensor
        def _(te):
            te.wait_ge(v0, v0b)
            te.wait_ge(s0, s0b)
            for si, (h, j, nk) in enumerate(seq):
                ch, half = h // 2, h % 2
                rows = slice(half * 64, (half + 1) * 64)
                av = pb[4 + si % 2]

                def emit_av(i):
                    if i == 0 and si >= 2:
                        h2, j2 = seq[si - 2][0], seq[si - 2][1]
                        te.wait_ge(v0, v0b + ymul_after[(h2, j2)])
                    if i >= 4 * j:
                        te.wait_ge(v0, v0b + mask_after[(h, j, i)])
                    else:
                        te.wait_ge(s0, s0b + exp_after_count[(h, j, i)])
                    te.matmul(av[0:65, 0:512], vaug(i * 520 + h * 65, 65),
                              ering[:, (i % 2) * 512:(i % 2 + 1) * 512],
                              start=(i == 0), stop=(i == nk - 1)).then_inc(p0, 1)

                for i in range(nk):
                    if i >= 2:
                        te.wait_ge(s0, s0b + exp_after_count[(h, j, i - 2)])
                    elif si > 0:
                        ph_, pj_, pn_ = seq[si - 1]
                        te.wait_ge(s0, s0b + exp_after_count[(ph_, pj_, pn_ - 2 + i)])
                    te.matmul(pb[i % 2][:, 0:512],
                              kfb(ch)[rows, i * 128:(i + 1) * 128],
                              qfb(ch)[rows, j * 512:(j + 1) * 512], start=True, stop=True).then_inc(p0, 1)
                    if i >= 1:
                        emit_av(i - 1)
                emit_av(nk - 1)
                te.wait_ge(v0, v0b + recip_after[(h, j)])
                if si >= 1:
                    h1, j1 = seq[si - 1][0], seq[si - 1][1]
                    te.wait_ge(v0, v0b + ymul_after[(h1, j1)])
                te.matmul(pb[6][0:64, 0:512], ctx["onerf"][0:1, 0:64], zrec[0:1, 0:512],
                          start=True, stop=True).then_inc(p0, 1)
            cp0.inc(sum(2 * nk + 1 for (_, _, nk) in seq))

        @blk.scalar
        def _(sc):
            sc.wait_ge(v0, v0b)
            for (h, j, nk) in seq:
                for i in range(nk):
                    sc.wait_ge(p0, p0b + qk_after[(h, j, i)])
                    sc.activation(out=ering[:, (i % 2) * 512:(i % 2 + 1) * 512],
                                  in_=pb[i % 2][:, 0:512], func=AF.Exp,
                                  bias=sv[:, 23:24], scale=1.0).then_inc(s0, 1)
            cs0.inc(sum(nk for (_, _, nk) in seq))

        @blk.vector
        def _(v):
            ndma = 0
            for si, (h, j, nk) in enumerate(seq):
                av = pb[4 + si % 2]
                for i in range(nk):
                    if i >= 4 * j:
                        v.wait_ge(s0, s0b + exp_after_count[(h, j, i)])
                        r = i - 4 * j
                        v.tensor_tensor(out=ering[:, (i % 2) * 512:(i % 2 + 1) * 512],
                                        in0=ering[:, (i % 2) * 512:(i % 2 + 1) * 512],
                                        in1=ar.bf("maskt", r * 512, 512), op=ALU.mult).then_inc(v0, 1)
                v.wait_ge(p0, p0b + av_after[(h, j, nk - 1)])
                v.reciprocal(out=zrec[0:1, 0:512], in_=av[64:65, 0:512]).then_inc(v0, 1)
                v.wait_ge(p0, p0b + bcast_after[(h, j)])
                if si >= 1:
                    v.wait_ge(d3, d3b + 16 * si)
                v.tensor_copy(out=zbc[:], in_=pb[6][0:64, 0:512])
                v.drain()
                v.tensor_tensor(out=ytile[:, 0:512], in0=av[0:64, 0:512],
                                in1=zbc[:], op=ALU.mult).then_inc(v0, 1)
            cv0.inc(max(d[1] for d in [(0, ymul_after[seq[-1][0], seq[-1][1]])]))

        @blk.gpsimd
        def _(gp):
            for si, (h, j, nk) in enumerate(seq):
                gp.wait_ge(v0, v0b + ymul_after[(h, j)])
                gp.dma_start(out=ctx["y0_in"].ap()[h * 64:(h + 1) * 64, j * 512:(j + 1) * 512],
                             in_=ytile[:, 0:512]).then_inc(d3, 16)
                cd3.inc(16)
            gp.wait_ge(d3, cd3.v)


def ph_l0tail(ctx):
    nc = ctx["nc"]
    ar = ctx["ar"]
    pb = ctx["pb"]
    d1, cd1 = _sem(ctx, "d1")
    d2, cd2 = _sem(ctx, "d2")
    d3, cd3 = _sem(ctx, "d3")
    d4, cd4 = _sem(ctx, "d4")
    d5, cd5 = _sem(ctx, "d5")
    p0, cp0 = _sem(ctx, "p0")
    v0, cv0 = _sem(ctx, "v0")
    s0, cs0 = _sem(ctx, "s0")
    g0, cg0 = _sem(ctx, "g0")
    cc, ccc = _sem(ctx, "cc")

    with nc.Block() as blk:
        @blk.gpsimd
        def _(gp):
            with allow_ici_all_to_all():
                gp.collective_compute(
                    "AllGather", ALU.bypass,
                    replica_groups=[[0, 1], [2, 3], [4, 5], [6, 7]],
                    ins=[ctx["y0_in"].ap().opt()],
                    outs=[ctx["y0_out"].ap().opt()],
                ).then_inc(cc, 1)
            ccc.inc()

    ar.release("xn0full", "qf", "kf", "vaug", "tmpA", "maskt")
    ar.alloc("x1", 8 * TOK * 4)
    ar.alloc("xn0e", 8 * EXT * 2)
    ar.alloc("mixr", 8 * TOK * 2)
    ar.alloc("mixk", 8 * TOK * 2)
    x1 = lambda f, a, n: ar.f32("x1", f * TOK + a, n)
    xn0e = lambda kc, a, n: ar.bf("xn0e", kc * EXT + a, n)
    xrv = lambda kc, a, n: ar.bf("mixr", kc * TOK + a, n)
    xkv = lambda kc, a, n: ar.bf("mixk", kc * TOK + a, n)

    ar.alloc("xtt", 2 * E * 4)
    xtt = ar.f32("xtt", 0, 2 * E)
    with (
        nc.Block() as blk,
    ):
        d1b, p0b, v0b = cd1.v, cp0.v, cv0.v
        d4b, d5b = cd4.v, cd5.v

        @blk.sync
        def _(sy):
            for f in range(8):
                sy.dma_start_transpose(xn0e(f, 0, EXT),
                                       ctx["xn0_own_tm"].ap()[:, f * 128:(f + 1) * 128]
                                       ).then_inc(d1, 16)
                cd1.inc(16)

        @blk.gpsimd
        def _(gp):
            for i in range(16):
                if i >= 2:
                    gp.wait_ge(p0, p0b + 8 * (i - 1))
                gp.dma_start(out=xtt[:, (i % 2) * E:(i % 2 + 1) * E],
                             in_=ctx["x_own"][i * 128:(i + 1) * 128, :]
                             ).then_inc(d4 if i % 2 == 0 else d5, 16)
                (cd4 if i % 2 == 0 else cd5).inc(16)

        @blk.tensor
        def _(te):
            for i in range(16):
                te.wait_ge(d4 if i % 2 == 0 else d5,
                           (d4b if i % 2 == 0 else d5b) + 16 * (i // 2 + 1))
                for f in range(8):
                    idx = i * 8 + f
                    if idx >= 4:
                        te.wait_ge(v0, v0b + idx - 3)
                    te.transpose(pb[idx % 4][:, 0:128],
                                 xtt[:, (i % 2) * E + f * 128:(i % 2) * E + (f + 1) * 128],
                                 ctx["cst"][:, 0:128]).then_inc(p0, 1)
            cp0.inc(128)

        @blk.vector
        def _(v):
            for i in range(16):
                for f in range(8):
                    idx = i * 8 + f
                    v.wait_ge(p0, p0b + idx + 1)
                    v.tensor_copy(out=x1(f, i * 128, 128), in_=pb[idx % 4][:, 0:128]).then_inc(v0, 1)
            cv0.inc(128)

        @blk.scalar
        def _(sc):
            sc.wait_ge(d1, cd1.v)

    ar.release("xtt")
    with nc.Block() as blk:
        @blk.vector
        def _(v):
            v.wait_ge(d1, cd1.v)
            for kc in range(8):
                for half in range(2):
                    a = half * 1024
                    xn = xn0e(kc, 1 + a, 1024)
                    xp = xn0e(kc, a, 1024)
                    d_r = xrv(kc, a, 1024)
                    d_k = xkv(kc, a, 1024)
                    v.tensor_tensor(out=d_k, in0=xn, in1=xp, op=ALU.subtract)
                    v.drain()
                    v.tensor_tensor(out=d_r, in0=d_k,
                                    in1=ctx["mixmu"][:, kc:kc + 1].broadcast_to([128, 1024]),
                                    op=ALU.mult)
                    v.drain()
                    v.tensor_tensor(out=d_k, in0=d_k,
                                    in1=ctx["mixmu"][:, 8 + kc:9 + kc].broadcast_to([128, 1024]),
                                    op=ALU.mult)
                    v.drain()
                    v.tensor_tensor(out=d_r, in0=d_r, in1=xp, op=ALU.add)
                    mmv = v.tensor_tensor(out=d_k, in0=d_k, in1=xp, op=ALU.add)
            mmv.then_inc(v0, 1)
            cv0.inc()

    ar.release("xn0e")
    ar.alloc("kact", 8 * TOK * 2)
    ar.alloc("wfull", 8 * 1024 * 2)
    wfull = lambda kc, dc: ar.bf("wfull", kc * 1024 + dc * 128, 128)
    kact = lambda kc, a, n: ar.bf("kact", kc * TOK + a, n)

    def stream_wfull(gp, wname):
        for kc in range(8):
            gp.dma_start(out=ar.bf("wfull", kc * 1024, 1024),
                         in_=bass.AP(ctx[wname + "_h"], kc * 128 * E, [[E, 128], [1, E]])
                         ).then_inc(d2, 16)
            cd2.inc(16)

    with nc.Block() as blk:
        d2b, p0b, v0b = cd2.v, cp0.v, cv0.v

        @blk.gpsimd
        def _(gp):
            stream_wfull(gp, "wkT")

        @blk.tensor
        def _(te):
            te.wait_ge(d2, d2b + 128)
            n = 0
            for dc in range(8):
                for tc in range(4):
                    idx = dc * 4 + tc
                    if idx >= 4:
                        te.wait_ge(v0, v0b + 2 * (idx - 3))
                    bank = pb[idx % 4]
                    for kc in range(8):
                        mm = te.matmul(bank[:, 0:512], wfull(kc, dc), xkv(kc, tc * 512, 512),
                                       start=(kc == 0), stop=(kc == 7))
                    mm.then_inc(p0, 1)
                    n += 1
            cp0.inc(n)

        @blk.vector
        def _(v):
            for dc in range(8):
                for tc in range(4):
                    idx = dc * 4 + tc
                    v.wait_ge(p0, p0b + idx + 1)
                    bank = pb[idx % 4]
                    v.tensor_scalar(out=kact(dc, tc * 512, 512), in0=bank[:, 0:512],
                                    scalar1=0.0, scalar2=None, op0=ALU.max)
                    v.drain()
                    v.tensor_tensor(out=kact(dc, tc * 512, 512), in0=kact(dc, tc * 512, 512),
                                    in1=kact(dc, tc * 512, 512), op=ALU.mult).then_inc(v0, 2)
            cv0.inc(64)

    ar.release("mixk")
    ar.alloc("wfull2", 8 * 1024 * 2)
    ar.alloc("kvslab", 8 * 512 * 2)
    ar.alloc("sgslab", 8 * 512 * 2)
    wfull2 = lambda kc, dc: ar.bf("wfull2", kc * 1024 + dc * 128, 128)
    kvs = lambda dc: ar.bf("kvslab", dc * 512, 512)
    sgs = lambda dc: ar.bf("sgslab", dc * 512, 512)

    with nc.Block() as blk:
        d2b, p0b, v0b, s0b = cd2.v, cp0.v, cv0.v, cs0.v

        @blk.gpsimd
        def _(gp):
            stream_wfull(gp, "wvT")
            for kc in range(8):
                gp.dma_start(out=ar.bf("wfull2", kc * 1024, 1024),
                             in_=bass.AP(ctx["wrT_h"], kc * 128 * E, [[E, 128], [1, E]])
                             ).then_inc(d2, 16)
                cd2.inc(16)

        @blk.tensor
        def _(te):
            te.wait_ge(d2, d2b + 256)
            n = 0
            for tc in range(4):
                for dc in range(8):
                    bank = pb[dc % 4]
                    if dc >= 4 or tc > 0:
                        te.wait_ge(v0, v0b + _l0_vr_vwait(tc, dc))
                    for kc in range(8):
                        mm = te.matmul(bank[:, 0:512], wfull(kc, dc), kact(kc, tc * 512, 512),
                                       start=(kc == 0), stop=(kc == 7))
                    mm.then_inc(p0, 1)
                    n += 1
                for dc in range(8):
                    bank = pb[4 + dc % 3]
                    if tc > 0 or dc >= 3:
                        te.wait_ge(s0, s0b + _l0_r_swait(tc, dc))
                    for kc in range(8):
                        mm = te.matmul(bank[:, 0:512], wfull2(kc, dc), xrv(kc, tc * 512, 512),
                                       start=(kc == 0), stop=(kc == 7))
                    mm.then_inc(p0, 1)
                    n += 1
            cp0.inc(n)

        @blk.vector
        def _(v):
            for tc in range(4):
                for dc in range(8):
                    v.wait_ge(p0, p0b + tc * 16 + dc + 1)
                    v.tensor_copy(out=kvs(dc), in_=pb[dc % 4][:, 0:512]).then_inc(v0, 1)
                for dc in range(8):
                    v.wait_ge(s0, s0b + tc * 8 + dc + 1)
                    v.tensor_tensor(out=sgs(dc), in0=sgs(dc), in1=kvs(dc), op=ALU.mult)
                    v.drain()
                    v.tensor_tensor(out=x1(dc, tc * 512, 512), in0=x1(dc, tc * 512, 512),
                                    in1=sgs(dc), op=ALU.add).then_inc(v0, 1)
            cv0.inc(64)

        @blk.scalar
        def _(sc):
            for tc in range(4):
                for dc in range(8):
                    sc.wait_ge(p0, p0b + tc * 16 + 8 + dc + 1)
                    if tc > 0:
                        sc.wait_ge(v0, v0b + 16 * (tc - 1) + 8 + dc + 1)
                    sc.activation(out=sgs(dc), in_=pb[4 + dc % 3][:, 0:512], func=AF.Sigmoid).then_inc(s0, 1)
            cs0.inc(32)

    ar.release("mixr", "kact", "kvslab", "sgslab", "wfull2")
    ar.alloc("yall", 8 * TOK * 2)
    yall = lambda kc, a, n: ar.bf("yall", kc * TOK + a, n)

    with nc.Block() as blk:
        d2b, d3b, p0b, v0b = cd2.v, cd3.v, cp0.v, cv0.v

        @blk.gpsimd
        def _(gp):
            stream_wfull(gp, "wproj")
            gp.wait_ge(cc, ccc.v)
            rgC = gp.alloc_register()
            gp.reg_load(rgC, ctx["offs"][0:1, 1:2])
            gp.dma_start(out=yall(0, 0, 8 * TOK).rearrange("p (kc t) -> p kc t", t=TOK),
                         in_=bass.AP(ctx["y0_out"], rgC,
                                     [[S, 128], [128 * S, 8], [1, TOK]])
                         ).then_inc(d3, 16)
            cd3.inc(16)

        @blk.tensor
        def _(te):
            te.wait_ge(d2, d2b + 128)
            te.wait_ge(d3, d3b + 16)
            n = 0
            for dc in range(8):
                for tc in range(4):
                    idx = dc * 4 + tc
                    if idx >= 4:
                        te.wait_ge(v0, v0b + idx - 3)
                    bank = pb[idx % 4]
                    for kc in range(8):
                        te.matmul(bank[:, 0:512], wfull(kc, dc), yall(kc, tc * 512, 512),
                                  start=(kc == 0), stop=False)
                    te.matmul(bank[:, 0:512], ctx["biasrows"][0:1, 1536 + dc * 128:1536 + (dc + 1) * 128],
                              ctx["onerb"][0:1, 0:512], start=False, stop=True).then_inc(p0, 1)
                    n += 1
            cp0.inc(n)

        @blk.vector
        def _(v):
            for dc in range(8):
                for tc in range(4):
                    idx = dc * 4 + tc
                    v.wait_ge(p0, p0b + idx + 1)
                    v.tensor_tensor(out=x1(dc, tc * 512, 512), in0=x1(dc, tc * 512, 512),
                                    in1=pb[idx % 4][:, 0:512], op=ALU.add).then_inc(v0, 1)
            cv0.inc(32)

    ar.release("yall", "wfull")
    ar.alloc("xn1", 8 * TOK * 2)
    _emit_ln_fm(ctx, src_f32=lambda f, a, n: ar.f32("x1", f * TOK + a, n),
                dst_bf=lambda f, a, n: ar.bf("xn1", f * TOK + a, n),
                n_tok=TOK)


def _l0_vr_vwait(tc, dc):
    if dc >= 4:
        return 16 * tc + (dc - 4) + 1
    return 16 * (tc - 1) + (dc + 4) + 1 if tc > 0 else 0


def _l0_r_swait(tc, dc):
    if dc >= 3:
        return tc * 8 + dc - 2
    return tc * 8


def _emit_ln_fm(ctx, src_f32, dst_bf, n_tok, gamma_fn=None, beta_fn=None,
                 res_dst=None, x1_acc=None):
    nc = ctx["nc"]
    pb = ctx["pb"]
    p0, cp0 = _sem(ctx, "p0")
    v0, cv0 = _sem(ctx, "v0")
    s0, cs0 = _sem(ctx, "s0")
    ntc = n_tok // 512

    def pe_sx(tc, kc):
        return tc * 16 + kc + 1

    def pe_bc(tc, kc):
        return tc * 16 + 8 + kc + 1

    ar = ctx["ar"]
    ar.alloc("lntmp", 4 * 512 * 4 + 2 * 512 * 2 + 2 * 512 * 4)
    lnr = ar.f32("lntmp", 0, 4 * 512)[0:1, :]
    lnrb = ar.bf("lntmp", 4 * 512 * 2, 2 * 512)[0:1, :]
    sqf = ar.f32("lntmp", 4 * 512 + 512, 2 * 512)
    with (
        nc.Block() as blk,
    ):
        p0b, v0b, s0b = cp0.v, cv0.v, cs0.v

        @blk.scalar
        def _(sc):
            sc.wait_ge(v0, v0b)
            for tc in range(ntc):
                for kc in range(8):
                    idx = tc * 8 + kc
                    if idx >= 2:
                        sc.wait_ge(p0, p0b + pe_sx((idx - 2) // 8, (idx - 2) % 8))
                    sc.activation(out=sqf[:, (kc % 2) * 512:(kc % 2 + 1) * 512],
                                  in_=src_f32(kc, tc * 512, 512), func=AF.Square).then_inc(s0, 1)
                sc.wait_ge(v0, v0b + tc * 10 + 1)
                sc.activation(out=lnr[0:1, 512:1024], in_=lnr[0:1, 1024:1536], func=AF.Sqrt).then_inc(s0, 1)
            cs0.inc(ntc * 9)

        @blk.tensor
        def _(te):
            te.wait_ge(v0, v0b)
            for tc in range(ntc):
                if tc >= 1:
                    te.wait_ge(v0, v0b + tc * 10)
                for kc in range(8):
                    te.matmul(pb[6][0:1, 0:512], ctx["onec"][:], src_f32(kc, tc * 512, 512),
                              start=(kc == 0), stop=(kc == 7))
                    te.wait_ge(s0, s0b + tc * 9 + kc + 1)
                    te.matmul(pb[7][0:1, 0:512], ctx["onec"][:],
                              sqf[:, (kc % 2) * 512:(kc % 2 + 1) * 512],
                              start=(kc == 0), stop=(kc == 7)).then_inc(p0, 1)
                for kc in range(8):
                    te.wait_ge(v0, v0b + tc * 10 + 2 + kc)
                    ga = ctx["onerb"][0:1, 0:128] if gamma_fn is None else gamma_fn(kc)
                    te.matmul(pb[6][:, 0:512], ga, lnrb[0:1, 0:512], start=True, stop=True)
                    if beta_fn is None:
                        te.matmul(pb[7][:, 0:512], ga, lnrb[0:1, 512:1024],
                                  start=True, stop=True).then_inc(p0, 1)
                    else:
                        te.matmul(pb[7][:, 0:512], ga, lnrb[0:1, 512:1024], start=True, stop=False)
                        te.matmul(pb[7][:, 0:512], beta_fn(kc), ctx["onerb"][0:1, 0:512],
                                  start=False, stop=True).then_inc(p0, 1)
            cp0.inc(ntc * 16)

        @blk.vector
        def _(v):
            for tc in range(ntc):
                v.wait_ge(p0, p0b + pe_sx(tc, 7))
                v.tensor_scalar(out=lnr[0:1, 0:512], in0=pb[6][0:1, 0:512],
                                scalar1=1.0 / E, scalar2=None, op0=ALU.mult)
                v.tensor_scalar(out=lnr[0:1, 1024:1536], in0=pb[7][0:1, 0:512],
                                scalar1=1.0 / E, scalar2=None, op0=ALU.mult)
                v.drain()
                v.tensor_tensor(out=lnr[0:1, 1536:2048], in0=lnr[0:1, 0:512],
                                in1=lnr[0:1, 0:512], op=ALU.mult)
                v.drain()
                v.tensor_tensor(out=lnr[0:1, 1024:1536], in0=lnr[0:1, 1024:1536],
                                in1=lnr[0:1, 1536:2048], op=ALU.subtract)
                v.drain()
                v.tensor_scalar(out=lnr[0:1, 1024:1536], in0=lnr[0:1, 1024:1536],
                                scalar1=1e-5, scalar2=None, op0=ALU.add).then_inc(v0, 1)
                v.wait_ge(s0, s0b + tc * 9 + 9)
                v.reciprocal(out=lnr[0:1, 1024:1536], in_=lnr[0:1, 512:1024])
                v.drain()
                v.tensor_tensor(out=lnr[0:1, 1536:2048], in0=lnr[0:1, 0:512],
                                in1=lnr[0:1, 1024:1536], op=ALU.mult)
                v.drain()
                v.tensor_scalar(out=lnr[0:1, 1536:2048], in0=lnr[0:1, 1536:2048],
                                scalar1=-1.0, scalar2=None, op0=ALU.mult)
                v.drain()
                v.tensor_copy(out=lnrb[0:1, 0:512], in_=lnr[0:1, 1024:1536])
                v.tensor_copy(out=lnrb[0:1, 512:1024], in_=lnr[0:1, 1536:2048]).then_inc(v0, 1)
                for kc in range(8):
                    v.wait_ge(p0, p0b + pe_bc(tc, kc))
                    if res_dst is not None:
                        v.tensor_tensor(out=res_dst(kc, tc * 512, 512),
                                        in0=src_f32(kc, tc * 512, 512),
                                        in1=pb[6][:, 0:512], op=ALU.mult)
                        v.drain()
                        if x1_acc is None:
                            v.tensor_tensor(out=res_dst(kc, tc * 512, 512),
                                            in0=res_dst(kc, tc * 512, 512),
                                            in1=pb[7][:, 0:512], op=ALU.add).then_inc(v0, 1)
                        else:
                            v.tensor_tensor(out=res_dst(kc, tc * 512, 512),
                                            in0=res_dst(kc, tc * 512, 512),
                                            in1=pb[7][:, 0:512], op=ALU.add)
                            v.drain()
                            v.tensor_tensor(out=x1_acc(kc, tc * 512, 512),
                                            in0=x1_acc(kc, tc * 512, 512),
                                            in1=res_dst(kc, tc * 512, 512),
                                            op=ALU.add).then_inc(v0, 1)
                    else:
                        v.tensor_tensor(out=dst_bf(kc, tc * 512, 512),
                                        in0=src_f32(kc, tc * 512, 512),
                                        in1=pb[6][:, 0:512], op=ALU.mult)
                        v.drain()
                        v.tensor_tensor(out=dst_bf(kc, tc * 512, 512),
                                        in0=dst_bf(kc, tc * 512, 512),
                                        in1=pb[7][:, 0:512], op=ALU.add).then_inc(v0, 1)
            cv0.inc(ntc * 10)
    ar.release("lntmp")


def ph_l1head(ctx):
    nc = ctx["nc"]
    ar = ctx["ar"]
    pb = ctx["pb"]
    d2, cd2 = _sem(ctx, "d2")
    d3, cd3 = _sem(ctx, "d3")
    p0, cp0 = _sem(ctx, "p0")
    v0, cv0 = _sem(ctx, "v0")
    cc, ccc = _sem(ctx, "cc")
    xn1 = lambda kc, a, n: ar.bf("xn1", kc * TOK + a, n)
    ar.alloc("uown", 16 * 256 * 2)
    uown = lambda tt: ar.bf("uown", tt * 256, 256)

    with nc.Block() as blk:
        d2b, p0b, v0b = cd2.v, cp0.v, cv0.v

        @blk.gpsimd
        def _(gp):
            gp.dma_start(out=ctx["wsa"][:, 0:1024],
                         in_=bass.AP(ctx["wiT_h"], 0, [[256, 128], [128 * 256, 4], [1, 256]])
                         ).then_inc(d2, 16)
            cd2.inc(16)
            gp.dma_start(out=ctx["wsb"][:, 0:1024],
                         in_=bass.AP(ctx["wiT_h"], 4 * 128 * 256, [[256, 128], [128 * 256, 4], [1, 256]])
                         ).then_inc(d2, 16)
            cd2.inc(16)

        @blk.tensor
        def _(te):
            te.wait_ge(d2, d2b + 32)
            n = 0
            for tt in range(16):
                bank = pb[tt % 4]
                if tt >= 4:
                    te.wait_ge(v0, v0b + tt - 3)
                for kc in range(8):
                    wbuf = ctx["wsa"] if kc < 4 else ctx["wsb"]
                    te.matmul(bank[:, 0:256], xn1(kc, tt * 128, 128),
                              wbuf[:, (kc % 4) * 256:(kc % 4 + 1) * 256],
                              start=(kc == 0), stop=False)
                te.matmul(bank[:, 0:256], ctx["onerb"][0:1, 0:128],
                          ctx["biasrows"][0:1, 3584:3840], start=False, stop=True).then_inc(p0, 1)
                n += 1
            cp0.inc(n)

        @blk.vector
        def _(v):
            for tt in range(16):
                v.wait_ge(p0, p0b + tt + 1)
                v.tensor_copy(out=uown(tt), in_=pb[tt % 4][:, 0:256]).then_inc(v0, 1)
            cv0.inc(16)

        @blk.gpsimd
        def _(gp):
            for tt in range(16):
                gp.wait_ge(v0, v0b + tt + 1)
                gp.dma_start(out=ctx["u_in"].ap()[tt * 128:(tt + 1) * 128, :],
                             in_=uown(tt)).then_inc(d3, 16)
                cd3.inc(16)
            gp.wait_ge(d3, cd3.v)
            with allow_ici_all_to_all():
                gp.collective_compute(
                    "AllGather", ALU.bypass,
                    replica_groups=[[0, 1], [2, 3], [4, 5], [6, 7]],
                    ins=[ctx["u_in"].ap().opt()],
                    outs=[ctx["u_out"].ap().opt()],
                ).then_inc(cc, 1)
            ccc.inc()

    ar.release("uown")


def ph_scan(ctx):
    nc = ctx["nc"]
    ar = ctx["ar"]
    pb = ctx["pb"]
    d2, cd2 = _sem(ctx, "d2")
    d3, cd3 = _sem(ctx, "d3")
    p0, cp0 = _sem(ctx, "p0")
    v0, cv0 = _sem(ctx, "v0")
    cc, ccc = _sem(ctx, "cc")
    T = SCAN_T
    ar.alloc("ustrip", 2 * 32 * 128 * 2)
    ar.alloc("yfm", S * 2)
    ustrip = lambda k: ar.bf("ustrip", (k % 2) * 32 * 128, 32 * 128)
    yfm = lambda a, n: ar.bf("yfm", a, n)
    ss = [ctx["ss0"], ctx["ss1"]]
    pv = lambda t: pb[t % 3][:, 0:128]
    py = lambda t: pb[3 + (t // 512) % 2][:, (t % 512):(t % 512) + 1]

    p0b, v0b, d2b = cp0.v, cv0.v, cd2.v
    NSTRIP = (T + 31) // 32
    d3b = cd3.v + 16

    with nc.Block() as blk:
        @blk.gpsimd
        def _(gp):
            gp.wait_ge(cc, ccc.v)
            rgD = gp.alloc_register()
            gp.reg_load(rgD, ctx["offs"][0:1, 2:3])
            gp.dma_start(out=ctx["u_g"].ap(),
                         in_=bass.AP(ctx["u_out"], rgD, [[256, S], [1, 128]])
                         ).then_inc(d3, 16)
            cd3.inc(16)
            gp.wait_ge(d3, cd3.v)
            for k in range(NSTRIP):
                if k >= 2:
                    gp.wait_ge(v0, v0b + 32 * (k - 1))
                gp.dma_start(out=ustrip(k)[0:1, :],
                             in_=ctx["u_g"].ap()[k * 32:(k + 1) * 32, :]
                             ).then_inc(d2 if k % 2 == 0 else d3, 16)
                (cd2 if k % 2 == 0 else cd3).inc(16)

        @blk.tensor
        def _(te):
            te.wait_ge(d2, d2b + 16)
            te.matmul(pv(0), ctx["brow"][:], ustrip(0)[0:1, 0:128], start=True, stop=False)
            te.matmul(pv(1), ctx["brow"][:], ustrip(0)[0:1, 128:256], start=True, stop=False)
            n = 0
            for t in range(T):
                if (t + 2) % 32 == 0 and (t + 2) < T:
                    k = (t + 2) // 32
                    te.wait_ge(d2 if k % 2 == 0 else d3,
                               (d2b if k % 2 == 0 else d3b) + 16 * (k // 2 + 1))
                te.wait_ge(v0, v0b + t)
                te.matmul(pv(t), ctx["adT"][:], ss[(t + 1) % 2][:], start=False, stop=True).then_inc(p0, 1)
                n += 1
                if t >= 1:
                    te.matmul(py(t - 1), ss[(t - 1) % 2][:], ctx["ccol"][:], start=True, stop=True)
                if t + 2 < T:
                    k = (t + 2) // 32
                    te.matmul(pv(t + 2), ctx["brow"][:],
                              ustrip(k)[0:1, ((t + 2) % 32) * 128:((t + 2) % 32 + 1) * 128],
                              start=True, stop=False)
            te.wait_ge(v0, v0b + T)
            te.matmul(py(T - 1), ss[(T - 1) % 2][:], ctx["ccol"][:], start=True, stop=True).then_inc(p0, 1)
            n += 1
            cp0.inc(n)

        @blk.vector
        def _(v):
            for t in range(T):
                v.wait_ge(p0, p0b + t + 1)
                v.tensor_scalar(out=ss[t % 2][:], in0=pv(t), scalar1=-10.0, scalar2=10.0,
                                op0=ALU.max, op1=ALU.min).then_inc(v0, 1)
                if t % 512 == 2 and t > 512:
                    kk = t // 512 - 1
                    v.tensor_copy(out=yfm(kk * 512, 512), in_=pb[3 + kk % 2][:, 0:512])
            v.wait_ge(p0, p0b + T + 1)
            kk = (T - 1) // 512
            w = (T - 1) % 512 + 1
            v.tensor_copy(out=yfm(kk * 512, w), in_=pb[3 + kk % 2][:, 0:w]).then_inc(v0, 1)
            cv0.inc(T + 1)

        @blk.gpsimd
        def _(gp):
            gp.wait_ge(v0, v0b + T + 1)
            gp.dma_start(out=ctx["ys_in"].ap(), in_=yfm(0, S)[:, 0:S]).then_inc(d2, 16)
            cd2.inc(16)
            gp.wait_ge(d2, cd2.v)
            with allow_ici_all_to_all():
                gp.collective_compute(
                    "AllGather", ALU.bypass,
                    replica_groups=[[0, 1], [2, 3], [4, 5], [6, 7]],
                    ins=[ctx["ys_in"].ap().opt()],
                    outs=[ctx["ys_out"].ap().opt()],
                ).then_inc(cc, 1)
            ccc.inc()

    ar.release("ustrip")


def ph_ffn1(ctx):
    nc = ctx["nc"]
    ar = ctx["ar"]
    pb = ctx["pb"]
    d2, cd2 = _sem(ctx, "d2")
    d3, cd3 = _sem(ctx, "d3")
    d4, cd4 = _sem(ctx, "d4")
    d5, cd5 = _sem(ctx, "d5")
    p0, cp0 = _sem(ctx, "p0")
    v0, cv0 = _sem(ctx, "v0")
    s0, cs0 = _sem(ctx, "s0")
    xn1 = lambda kc, a, n: ar.bf("xn1", kc * TOK + a, n)
    x1 = lambda f, a, n: ar.f32("x1", f * TOK + a, n)
    ffn_cast_d5 = ctx["ctrs"]["d5"].v
    ar.alloc("hslab", 24 * 512 * 2)
    ar.alloc("ftmp", 2 * 512 * 4)
    ar.alloc("wsc", 24 * 128 * 2)
    hs = lambda hc: ar.bf("hslab", hc * 512, 512)
    SC2 = 1.0 / (127.0 * 127.0)

    for tc in range(4):
        with nc.Block() as blk:
            p0b, v0b, s0b = cp0.v, cv0.v, cs0.v
            d2b, d3b, d4b, d5b = cd2.v, cd3.v, cd4.v, cd5.v

            @blk.gpsimd
            def _(gp, tc=tc):
                if tc == 0:
                    gp.wait_ge(d5, ffn_cast_d5)
                for cg in range(24):
                    bufg = ctx["wsa"]
                    bufu = ctx["wsb"]
                    if cg >= 1:
                        gp.wait_ge(p0, p0b + 2 * cg - 1)
                    elif tc > 0:
                        gp.wait_ge(p0, p0b)
                    dsem = d2 if cg % 2 == 0 else d3
                    gp.dma_start(out=bufg[:, 0:1024],
                                 in_=bass.AP(ctx["wg_bf"], cg * 128,
                                             [[HID, 128], [128 * HID, 8], [1, 128]])
                                 ).then_inc(dsem, 16)
                    if cg >= 1:
                        gp.wait_ge(p0, p0b + 2 * cg)
                    gp.dma_start(out=bufu[:, 0:1024],
                                 in_=bass.AP(ctx["wu_bf"], cg * 128,
                                             [[HID, 128], [128 * HID, 8], [1, 128]])
                                 ).then_inc(dsem, 16)
                    (cd2 if cg % 2 == 0 else cd3).inc(32)
                for dc in range(8):
                    gp.wait_ge(p0, p0b + 48 + dc) if dc > 0 else None
                    gp.dma_start(out=ar.bf("wsc", 0, 3072),
                                 in_=bass.AP(ctx["wd_bf"], dc * 128,
                                             [[E, 128], [128 * E, 24], [1, 128]])
                                 ).then_inc(d4 if dc % 2 == 0 else d5, 16)
                    (cd4 if dc % 2 == 0 else cd5).inc(16)

            @blk.tensor
            def _(te, tc=tc):
                n = 0
                for cg in range(24):
                    te.wait_ge(d2 if cg % 2 == 0 else d3,
                               (d2b if cg % 2 == 0 else d3b) + 32 * (cg // 2 + 1))
                    bg = pb[cg % 2]
                    bu = pb[2 + cg % 2]
                    if cg >= 2:
                        te.wait_ge(v0, v0b + cg - 1)
                    for kc in range(8):
                        mm = te.matmul(bg[:, 0:512], ctx["wsa"][:, kc * 128:(kc + 1) * 128],
                                       xn1(kc, tc * 512, 512), start=(kc == 0), stop=(kc == 7))
                    mm.then_inc(p0, 1)
                    for kc in range(8):
                        mm = te.matmul(bu[:, 0:512], ctx["wsb"][:, kc * 128:(kc + 1) * 128],
                                       xn1(kc, tc * 512, 512), start=(kc == 0), stop=(kc == 7))
                    mm.then_inc(p0, 1)
                    n += 2
                for dc in range(8):
                    te.wait_ge(d4 if dc % 2 == 0 else d5,
                               (d4b if dc % 2 == 0 else d5b) + 16 * (dc // 2 + 1))
                    bank = pb[4 + dc % 2]
                    if dc == 0:
                        te.wait_ge(v0, v0b + 24)
                    if dc >= 2:
                        te.wait_ge(s0, s0b + 23 + dc)
                    for hc in range(24):
                        mm = te.matmul(bank[:, 0:512], ar.bf("wsc", hc * 128, 128),
                                       hs(hc), start=(hc == 0), stop=(hc == 23))
                    mm.then_inc(p0, 1)
                    n += 1
                cp0.inc(n)

            @blk.scalar
            def _(sc, tc=tc):
                for cg in range(24):
                    sc.wait_ge(p0, p0b + 2 * cg + 2)
                    sc.activation(out=hs(cg), in_=pb[cg % 2][:, 0:512], func=AF.Silu,
                                  scale=1.0 / 127.0).then_inc(s0, 1)
                for dc in range(8):
                    sc.wait_ge(p0, p0b + 48 + dc + 1)
                    if dc >= 2:
                        sc.wait_ge(v0, v0b + 23 + dc)
                    sc.activation(out=ar.f32("ftmp", (dc % 2) * 512, 512),
                                  in_=pb[4 + dc % 2][:, 0:512], func=AF.Copy, scale=SC2).then_inc(s0, 1)
                cs0.inc(32)

            @blk.vector
            def _(v, tc=tc):
                for cg in range(24):
                    v.wait_ge(s0, s0b + cg + 1)
                    v.tensor_tensor(out=hs(cg), in0=hs(cg), in1=pb[2 + cg % 2][:, 0:512],
                                    op=ALU.mult).then_inc(v0, 1)
                for dc in range(8):
                    v.wait_ge(s0, s0b + 24 + dc + 1)
                    v.tensor_tensor(out=x1(dc, tc * 512, 512), in0=x1(dc, tc * 512, 512),
                                    in1=ar.f32("ftmp", (dc % 2) * 512, 512), op=ALU.add).then_inc(v0, 1)
                cv0.inc(32)

    ar.release("hslab", "ftmp", "wsc")


def ph_tail(ctx):
    nc = ctx["nc"]
    ar = ctx["ar"]
    pb = ctx["pb"]
    d2, cd2 = _sem(ctx, "d2")
    d3, cd3 = _sem(ctx, "d3")
    d4, cd4 = _sem(ctx, "d4")
    p0, cp0 = _sem(ctx, "p0")
    v0, cv0 = _sem(ctx, "v0")
    cc, ccc = _sem(ctx, "cc")
    xn1 = lambda kc, a, n: ar.bf("xn1", kc * TOK + a, n)
    x1 = lambda f, a, n: ar.f32("x1", f * TOK + a, n)
    ar.release("yfm")
    ar.alloc("res", 8 * TOK * 4)
    ar.alloc("ysl", 2 * TOK * 2)
    res = lambda f, a, n: ar.f32("res", f * TOK + a, n)
    ysl = lambda kc: ar.bf("ysl", kc * TOK, TOK)

    with nc.Block() as blk:
        d2b, p0b, v0b = cd2.v, cp0.v, cv0.v

        @blk.gpsimd
        def _(gp):
            gp.wait_ge(cc, ccc.v)
            rgE = gp.alloc_register()
            gp.reg_load(rgE, ctx["offs"][0:1, 1:2])
            gp.dma_start(out=ysl(0).rearrange("p t -> p 1 t").broadcast_to([128, 2, TOK])
                         if False else
                         ar.bf("ysl", 0, 2 * TOK).rearrange("p (kc t) -> p kc t", t=TOK),
                         in_=bass.AP(ctx["ys_out"], rgE,
                                     [[S, 128], [128 * S, 2], [1, TOK]])
                         ).then_inc(d2, 16)
            cd2.inc(16)
            gp.dma_start(out=ctx["wsa"][:, 0:1024],
                         in_=bass.AP(ctx["woT_h"], 0, [[E, 128], [1, E]])).then_inc(d2, 16)
            cd2.inc(16)
            gp.dma_start(out=ctx["wsb"][:, 0:1024],
                         in_=bass.AP(ctx["woT_h"], 128 * E, [[E, 128], [1, E]])).then_inc(d2, 16)
            cd2.inc(16)

        @blk.tensor
        def _(te):
            te.wait_ge(d2, d2b + 48)
            n = 0
            for dc in range(8):
                for tcc in range(4):
                    idx = dc * 4 + tcc
                    if idx >= 4:
                        te.wait_ge(v0, v0b + idx - 3)
                    bank = pb[idx % 4]
                    te.matmul(bank[:, 0:512], ctx["wsa"][:, dc * 128:(dc + 1) * 128],
                              ysl(0)[:, tcc * 512:(tcc + 1) * 512], start=True, stop=False)
                    te.matmul(bank[:, 0:512], ctx["wsb"][:, dc * 128:(dc + 1) * 128],
                              ysl(1)[:, tcc * 512:(tcc + 1) * 512], start=False, stop=False)
                    te.matmul(bank[:, 0:512],
                              ctx["biasrows"][0:1, 2560 + dc * 128:2560 + (dc + 1) * 128],
                              ctx["onerb"][0:1, 0:512], start=False, stop=True).then_inc(p0, 1)
                    n += 1
            cp0.inc(n)

        @blk.vector
        def _(v):
            for dc in range(8):
                for tcc in range(4):
                    idx = dc * 4 + tcc
                    v.wait_ge(p0, p0b + idx + 1)
                    v.tensor_tensor(out=res(dc, tcc * 512, 512), in0=pb[idx % 4][:, 0:512],
                                    in1=xn1(dc, tcc * 512, 512), op=ALU.add).then_inc(v0, 1)
            cv0.inc(32)

    ar.release("xn1", "ysl")
    gam = ctx["gbrow"]
    _emit_ln_fm(ctx, src_f32=res, dst_bf=None, n_tok=TOK,
                gamma_fn=lambda kc: gam[0:1, kc * 128:(kc + 1) * 128],
                beta_fn=lambda kc: gam[0:1, 1024 + kc * 128:1024 + (kc + 1) * 128],
                res_dst=lambda f, a, n: ar.f32("res", f * TOK + a, n),
                x1_acc=x1)

    ar.release("res")
    ar.alloc("otile", 2 * E * 4)
    otile = ar.f32("otile", 0, 2 * E)
    with (
        nc.Block() as blk,
    ):
        p0b, v0b, d3b, d4b = cp0.v, cv0.v, cd3.v, cd4.v

        @blk.tensor
        def _(te):
            for tt in range(16):
                for f in range(8):
                    idx = tt * 8 + f
                    if idx >= 4:
                        te.wait_ge(v0, v0b + idx - 3)
                    te.transpose(pb[idx % 4][:, 0:128], x1(f, tt * 128, 128),
                                 ctx["cst"][:, 0:128]).then_inc(p0, 1)
            cp0.inc(128)

        @blk.vector
        def _(v):
            for tt in range(16):
                if tt >= 2:
                    v.wait_ge(d3 if tt % 2 == 0 else d4,
                              (d3b if tt % 2 == 0 else d4b) + 16 * (tt // 2))
                for f in range(8):
                    idx = tt * 8 + f
                    v.wait_ge(p0, p0b + idx + 1)
                    v.tensor_copy(out=otile[:, (tt % 2) * E + f * 128:(tt % 2) * E + (f + 1) * 128],
                                  in_=pb[idx % 4][:, 0:128]).then_inc(v0, 1)
            cv0.inc(128)

        @blk.gpsimd
        def _(gp):
            for tt in range(16):
                gp.wait_ge(v0, v0b + (tt + 1) * 8)
                gp.dma_start(out=ctx["out_x2"][tt * 128:(tt + 1) * 128, :],
                             in_=otile[:, (tt % 2) * E:(tt % 2 + 1) * E]
                             ).then_inc(d3 if tt % 2 == 0 else d4, 16)
                (cd3 if tt % 2 == 0 else cd4).inc(16)
            gp.wait_ge(d3, cd3.v)
            gp.wait_ge(d4, cd4.v)


def ph_debug(ctx):
    nc = ctx["nc"]
    ar = ctx["ar"]
    d2, cd2 = _sem(ctx, "d2")
    with nc.Block() as blk:
        @blk.gpsimd
        def _(gp):
            srcs = {
                "xn0": lambda: ctx["xn0_tm"].ap()[1:4097, :],
                "qf": lambda: ar.bf("qf", 0, 4 * S),
                "kf": lambda: ar.bf("kf", 0, 4 * S),
                "vaug": lambda: ar.bf("vaug", 0, 32 * 520),
                "negm": lambda: ctx["sv"][:, 23:24],
                "st0dbg": lambda: ctx["ss0"][:],
                "svdbg": lambda: ctx["sv"][:, 8:24],
                "srowdbg": lambda: ctx["srow"][0:1, 8:45],
                "xt32dbg": lambda: ctx["arena_t"][:, 38912:43008].bitcast(F32),
                "tmpAdbg": lambda: ctx["arena_t"][:, 34816:38912].bitcast(F32),
                "adT": lambda: ctx["adT"][:],
                "brow": lambda: ctx["brow"][:],
                "y0": lambda: ctx["y0_in"].ap(),
                "x1": lambda: ar.f32("x1", 0, 8 * TOK),
                "xn1": lambda: ar.bf("xn1", 0, 8 * TOK),
                "uown": lambda: ctx["u_in"].ap().rearrange("a b -> (a b)").rearrange(
                    "(p f) -> p f", p=128),
                "yfm": lambda: ar.bf("yfm", 0, S),
                "res": lambda: ar.f32("res", 0, 8 * TOK),
                "xn0e": lambda: ar.bf("xn0e", 0, 8 * EXT),
            }
            for nm, ap in ctx["dbg"].items():
                try:
                    src = srcs[nm]()
                except (KeyError, RuntimeError):
                    continue
                gp.dma_start(out=ap, in_=src).then_inc(d2, 16)
                cd2.inc(16)
            gp.wait_ge(d2, cd2.v)
    if N_PHASES < 12:
        with nc.Block() as blk:
            @blk.gpsimd
            def _(gp):
                gp.dma_start(out=ctx["out_x2"].rearrange("a b -> (a b)").rearrange(
                    "(p f) -> p f", p=128),
                    in_=ctx["arena_t"][:, 0:2 * E * TOK // 128].bitcast(F32)
                    ).then_inc(d2, 16)
                cd2.inc(16)
                gp.wait_ge(d2, cd2.v)


_NC_CACHE = {}


def _host_prepare(inputs):
    x = np.ascontiguousarray(inputs["L0_attn_x"])
    Wa = inputs["L0_attn_W_attn"]; ba = inputs["L0_attn_b_attn"]
    id32 = np.eye(128, dtype=np.float32)
    eye2 = (2 * np.eye(128)).astype(np.float32)
    masks = np.zeros((128, 4 * 512), bnp16)
    for r in range(4):
        k_idx = np.arange(128)[:, None] + r * 128
        q_idx = np.arange(512)[None, :]
        masks[:, r * 512:(r + 1) * 512] = (k_idx <= q_idx).astype(bnp16)
    shared = dict(
        wproj=np.ascontiguousarray(inputs["L0_attn_W_proj"]),
        bproj_row=np.ascontiguousarray(inputs["L0_attn_b_proj"][None, :].astype(bnp16)),
        wrT=np.ascontiguousarray(inputs["L0_ffn_W_r"].T),
        wkT=np.ascontiguousarray(inputs["L0_ffn_W_k"].T),
        wvT=np.ascontiguousarray(inputs["L0_ffn_W_v"].T),
        mur=np.ascontiguousarray(inputs["L0_ffn_mu_r"].reshape(8, 128).T.astype(np.float32)),
        muk=np.ascontiguousarray(inputs["L0_ffn_mu_k"].reshape(8, 128).T.astype(np.float32)),
        wiT=np.ascontiguousarray(inputs["L1_attn_Wi"].T),
        bi_row=np.ascontiguousarray(inputs["L1_attn_bi"][None, :].astype(bnp16)),
        woT=np.ascontiguousarray(inputs["L1_attn_Wo"].T),
        bo_row=np.ascontiguousarray(inputs["L1_attn_bo"][None, :].astype(bnp16)),
        gam_row=np.ascontiguousarray(inputs["L1_attn_gamma"][None, :].astype(bnp16)),
        bet_row=np.ascontiguousarray(inputs["L1_attn_beta"][None, :].astype(bnp16)),
        wgT=np.ascontiguousarray(inputs["L1_ffn_w_gate"].T),
        wuT=np.ascontiguousarray(inputs["L1_ffn_w_up"].T),
        wdT=np.ascontiguousarray(inputs["L1_ffn_w_down"].T),
        a_hip=np.ascontiguousarray(inputs["L1_attn_A_hippo"]),
        id32=id32, eye2=eye2,
        ones_col=np.ones((128, 1), np.float32),
        ones_colb=np.ones((128, 1), bnp16),
        ones_rowb=np.ones((1, 512), bnp16),
        masks=masks,
        hsel=np.concatenate([np.repeat([[1, 0]], 64, 1).astype(np.float32).reshape(1, 128),
                             np.repeat([[0, 1]], 64, 1).astype(np.float32).reshape(1, 128)]),
        ones_rowf=np.ones((1, 128), np.float32),
    )
    maps = []
    for c in range(8):
        b, gg = c // 2, c % 2
        hsl = slice(gg * 512, (gg + 1) * 512)
        m = dict(shared)
        m.update(
            x_b=np.ascontiguousarray(x[b]),
            x_own=np.ascontiguousarray(x[b, gg * TOK:(gg + 1) * TOK]),
            x_bh=np.ascontiguousarray(x[b].astype(bnp16)),
            wattn=np.ascontiguousarray(np.concatenate(
                [Wa[:, hsl], Wa[:, 1024 + gg * 512:1024 + (gg + 1) * 512],
                 Wa[:, 2048 + gg * 512:2048 + (gg + 1) * 512]], axis=1)),
            battn_row=np.ascontiguousarray(np.concatenate(
                [ba[hsl], ba[1024 + gg * 512:1024 + (gg + 1) * 512],
                 ba[2048 + gg * 512:2048 + (gg + 1) * 512]])[None, :].astype(bnp16)),
            bp_g=np.ascontiguousarray(inputs["L1_attn_Bp"][gg]),
            cv_g=np.ascontiguousarray(inputs["L1_attn_Cp"][gg, 0][:, None]),
            ldt_g=np.array([[inputs["L1_attn_log_dt"][gg]]], np.float32),
            sc_g=np.array([[inputs["L1_attn_scale"][gg]]], np.float32),
            off_xel=np.array([[gg * TOK * E]], np.int32),
            off_y0=np.array([[gg * TOK]], np.int32),
            off_uel=np.array([[gg * 128]], np.int32),
        )
        maps.append(m)
    return maps


def kernel(**inputs):
    maps = _host_prepare(inputs)
    key = (N_PHASES, tuple(DBG), SCAN_T, AR_KB)
    if key not in _NC_CACHE:
        _NC_CACHE[key] = build_nc()
    nc = _NC_CACHE[key]
    res = run_bass_kernel_spmd(nc, maps, core_ids=list(range(8)))
    out = np.empty((B, S, E), np.float32)
    for c in range(8):
        b, gg = c // 2, c % 2
        out[b, gg * TOK:(gg + 1) * TOK, :] = res.results[c]["out_x2"]
    globals()["_LAST_RES"] = res
    return out


# version 06
# speedup vs baseline: 1.2311x; 1.0868x over previous
import os
import contextlib
import numpy as np
import ml_dtypes
import concourse.bass as bass
import concourse.mybir as mybir
from concourse.bass_utils import run_bass_kernel_spmd
from concourse.aot_env import allow_ici_all_to_all

F32 = mybir.dt.float32
BF16 = mybir.dt.bfloat16
I32 = mybir.dt.int32
AF = mybir.ActivationFunctionType
ALU = mybir.AluOpType
AX = mybir.AxisListType
Pool = mybir.EngineType.Pool

bnp16 = ml_dtypes.bfloat16

B, S, E = 4, 4096, 1024
TOK = 2048
NST = 128
HID = 3072
EXT = 2176
NS_ITERS = 25

N_PHASES = int(os.environ.get("KPHASES", "99"))
DBG = [d for d in os.environ.get("KDBG", "").split(",") if d]
SCAN_T = int(os.environ.get("KSCANT", str(S)))
AR_KB = int(os.environ.get("KARENA", "180"))

PHASE_NAMES = ["prep", "ns", "qkv", "kquant", "vquant", "mhat", "attn",
               "l0tail", "l1head", "scan", "ffn1", "tail"]


class Ctr:
    def __init__(self):
        self.v = 0

    def inc(self, k=1):
        self.v += k
        return self.v


class Arena:
    def __init__(self, tensor, nbytes):
        self.t = tensor
        self.regions = {}
        self.free = [(0, nbytes)]

    def alloc(self, name, nbytes):
        nbytes = (nbytes + 3) & ~3
        for i, (o, sz) in enumerate(self.free):
            if sz >= nbytes:
                self.regions[name] = (o, nbytes)
                if sz == nbytes:
                    self.free.pop(i)
                else:
                    self.free[i] = (o + nbytes, sz - nbytes)
                return
        raise RuntimeError(f"arena OOM for {name} ({nbytes}B); free={self.free}")

    def release(self, *names):
        for name in names:
            o, sz = self.regions.pop(name)
            self.free.append((o, sz))
        self.free.sort()
        out = []
        for seg in self.free:
            if out and out[-1][0] + out[-1][1] == seg[0]:
                out[-1] = (out[-1][0], out[-1][1] + seg[1])
            else:
                out.append(list(seg))
        self.free = [tuple(s) for s in out]

    def bf(self, name, off_el=0, cols=None):
        o, sz = self.regions[name]
        n = (sz // 2 - off_el) if cols is None else cols
        return self.t[:, o // 2 + off_el: o // 2 + off_el + n]

    def f32(self, name, off_el=0, cols=None):
        o, sz = self.regions[name]
        ap = self.t[:, o // 2: o // 2 + sz // 2].bitcast(F32)
        n = (sz // 4 - off_el) if cols is None else cols
        return ap[:, off_el: off_el + n]

    def i32(self, name, off_el=0, cols=None):
        return self.f32(name, off_el, cols).bitcast(I32)


def build_nc():
    nc = bass.Bass("TRN2", target_bir_lowering=False)
    stack = contextlib.ExitStack()
    ctx = {"nc": nc}

    def inp(name, shape, dt=F32):
        t = nc.dram_tensor(name, shape, dt, kind="ExternalInput")
        ctx[name] = t.ap()
        ctx[name + "_h"] = t

    inp("x_b", [S, E]); inp("x_bh", [S, E], BF16)
    inp("x_own", [TOK, E])
    inp("wattn", [E, 1536]); inp("battn_row", [1, 1536], BF16)
    inp("wproj", [E, E]); inp("bproj_row", [1, E], BF16)
    inp("wrT", [E, E]); inp("wkT", [E, E]); inp("wvT", [E, E])
    inp("mur", [128, 8]); inp("muk", [128, 8])
    inp("wiT", [E, 256]); inp("bi_row", [1, 256], BF16)
    inp("woT", [256, E]); inp("bo_row", [1, E], BF16)
    inp("gam_row", [1, E], BF16); inp("bet_row", [1, E], BF16)
    inp("wgT", [E, HID], I32); inp("wuT", [E, HID], I32); inp("wdT", [HID, E], I32)
    inp("a_hip", [NST, NST]); inp("bp_g", [NST, 1]); inp("cv_g", [NST, 1])
    inp("ldt_g", [1, 1]); inp("sc_g", [1, 1])
    inp("id32", [128, 128]); inp("eye2", [128, 128])
    inp("ones_col", [128, 1]); inp("ones_colb", [128, 1], BF16)
    inp("ones_rowb", [1, 512], BF16)
    inp("masks", [128, 4 * 512], BF16)
    inp("hsel", [2, 128])
    inp("ones_rowf", [1, 128])
    inp("off_xel", [1, 1], I32)
    inp("off_y0", [1, 1], I32)
    inp("off_uel", [1, 1], I32)

    ctx["out_x2"] = nc.dram_tensor("out_x2", [TOK, E], F32, kind="ExternalOutput").ap()
    ctx["dbg"] = {}
    dbg_specs = {
        "xn0": ([4096, E], BF16), "qf": ([128, 4 * S], BF16), "kf": ([128, 4 * S], BF16),
        "vaug": ([128, 32 * 520], BF16), "negm": ([128, 1], F32),
        "y0": ([512, S], BF16), "x1": ([128, 8 * TOK], F32), "xn1": ([128, 8 * TOK], BF16),
        "adT": ([128, 128], BF16), "brow": ([1, 128], BF16), "uown": ([128, 16 * 256], BF16),
        "yfm": ([128, S], BF16), "res": ([128, 8 * TOK], F32), "xn0e": ([128, 8 * EXT], BF16),
        "st0dbg": ([128, 128], BF16), "xt32dbg": ([128, 2048], F32), "tmpAdbg": ([128, 2048], F32),
        "svdbg": ([128, 16], F32), "srowdbg": ([1, 37], F32),
    }
    for nm in DBG:
        shp, dt = dbg_specs[nm]
        ctx["dbg"][nm] = nc.dram_tensor("dbg_" + nm, shp, dt, kind="ExternalOutput").ap()

    for nm, shp, dt in [("xn0_tm", [4352, E], BF16), ("xn0_own_tm", [EXT, E], BF16),
                        ("u_g", [S, 128], BF16),
                        ("wg_bf", [E, HID], BF16), ("wu_bf", [E, HID], BF16),
                        ("wd_bf", [HID, E], BF16),
                        ("y0_in", [512, S], BF16), ("y0_out", [1024, S], BF16),
                        ("u_in", [TOK, 256], BF16), ("u_out", [S, 256], BF16),
                        ("ys_in", [NST, S], BF16), ("ys_out", [2 * NST, S], BF16)]:
        ctx[nm] = nc.dram_tensor(nm, shp, dt)

    ctx["sems"] = {}
    ctx["ctrs"] = {}
    for name in ["d0", "d1", "d2", "d3", "d4", "d5", "p0", "v0", "s0", "g0", "cc"]:
        ctx["sems"][name] = stack.enter_context(nc.semaphore(name))
        ctx["ctrs"][name] = Ctr()

    def sbuf(name, shape, dt=F32):
        t = stack.enter_context(nc.sbuf_tensor(name, shape, dt))
        ctx[name] = t
        return t

    sbuf("cst", [128, 256], F32)
    sbuf("onec", [128, 1], F32)
    sbuf("onecb", [128, 1], BF16)
    sbuf("onerb", [1, 512], BF16)
    sbuf("sv", [128, 32], F32)
    sbuf("srow", [1, 64], F32)
    sbuf("biasrows", [1, 4096], BF16)
    sbuf("gbrow", [1, 2048], BF16)
    sbuf("adT", [128, 128], BF16)
    sbuf("brow", [1, 128], BF16)
    sbuf("ccol", [128, 1], BF16)
    sbuf("ss0", [128, 128], BF16)
    sbuf("ss1", [128, 128], BF16)
    sbuf("wsa", [128, 1024], BF16)
    sbuf("wsb", [128, 1024], BF16)
    sbuf("eringb", [128, 2 * 512], BF16)
    sbuf("offs", [1, 4], I32)
    sbuf("hselt", [2, 128])
    sbuf("onerf", [1, 128])
    sbuf("mixmu", [128, 16])

    arena_t = stack.enter_context(nc.sbuf_tensor("arena", [128, AR_KB * 512], BF16))
    ctx["ar"] = Arena(arena_t, AR_KB * 1024)
    ctx["arena_t"] = arena_t

    ctx["pb"] = [stack.enter_context(nc.psum_tensor(f"pb{i}", [128, 512], F32))
                 for i in range(8)]

    with stack:
        phases = [ph_prep, ph_ns, ph_qkv, ph_kquant, ph_vquant, ph_mhat, ph_attn,
                  ph_l0tail, ph_l1head, ph_scan, ph_ffn1, ph_tail]
        for i, ph in enumerate(phases):
            if i < N_PHASES:
                ph(ctx)
        ph_debug(ctx)
    return nc


def _sem(ctx, n):
    return ctx["sems"][n], ctx["ctrs"][n]


def ph_prep(ctx):
    KPREP = int(os.environ.get("KPREP", "4"))
    KNT = int(os.environ.get("KNT", "32"))
    nc = ctx["nc"]
    ar = ctx["ar"]
    d0, cd0 = _sem(ctx, "d0")
    d1, cd1 = _sem(ctx, "d1")
    d2, cd2 = _sem(ctx, "d2")
    v0, cv0 = _sem(ctx, "v0")
    s0, cs0 = _sem(ctx, "s0")
    g0, cg0 = _sem(ctx, "g0")
    NT = KNT
    ar.alloc("maskt", 4 * 512 * 2)
    ar.alloc("xn0full", 8 * S * 2)
    ar.alloc("tmpA", 8 * 1024)
    ar.alloc("preploc", (2 * E) * 4 + (2 * E) * 2 + (2 * E) * 2 + E * 2)

    d3, cd3 = _sem(ctx, "d3")
    d4, cd4 = _sem(ctx, "d4")
    d5, cd5 = _sem(ctx, "d5")
    v0b, s0b = cv0.v, cs0.v
    d4b, d5b = cd4.v, cd5.v
    const_list = [
        ("id32", ctx["cst"][:, 0:128]), ("eye2", ctx["cst"][:, 128:256]),
        ("ones_col", ctx["onec"][:]), ("ones_colb", ctx["onecb"][:]),
        ("ones_rowb", ctx["onerb"][:]), ("masks", ar.bf("maskt", 0, 4 * 512)),
        ("bp_g", ctx["sv"][:, 0:1]), ("cv_g", ctx["sv"][:, 1:2]),
        ("ldt_g", ctx["sv"][0:1, 2:3]), ("sc_g", ctx["sv"][0:1, 3:4]),
        ("battn_row", ctx["biasrows"][0:1, 0:1536]),
        ("bproj_row", ctx["biasrows"][0:1, 1536:2560]),
        ("bo_row", ctx["biasrows"][0:1, 2560:3584]),
        ("bi_row", ctx["biasrows"][0:1, 3584:3840]),
        ("gam_row", ctx["gbrow"][0:1, 0:1024]),
        ("bet_row", ctx["gbrow"][0:1, 1024:2048]),
        ("hsel", ctx["hselt"][:]),
        ("ones_rowf", ctx["onerf"][:]),
        ("mur", ctx["mixmu"][:, 0:8]),
        ("muk", ctx["mixmu"][:, 8:16]),
        ("off_xel", ctx["offs"][0:1, 0:1]),
        ("off_y0", ctx["offs"][0:1, 1:2]),
        ("off_uel", ctx["offs"][0:1, 2:3]),
    ]
    d0b = cd0.v + len(const_list) * 16
    d1b = cd1.v
    xt32 = ar.f32("preploc", 0, 2 * E)
    xtb = ar.bf("preploc", 4 * E, 2 * E)
    xnt = ar.bf("preploc", 6 * E, 2 * E)
    zrow = ar.bf("preploc", 8 * E, E)[0:1, :]
    sqd = ar.f32("tmpA", 0, 2 * E)
    with (
        nc.sbuf_tensor("st0", [128, 16], F32) as st0,
        nc.sbuf_tensor("st1", [128, 2], F32) as st1,
        nc.Block() as blk,
    ):
        @blk.sync
        def _(sy):
            for src, dst in const_list:
                sy.dma_start(out=dst, in_=ctx[src]).then_inc(d0, 16); cd0.inc(16)
            for i in range(NT):
                if i >= 2:
                    sy.wait_ge(v0, v0b + 2 * (i - 1))
                    sy.wait_ge(s0, s0b + 2 * (i - 2) + 1)
                dsem = d0 if i % 2 == 0 else d1
                sy.dma_start(out=xt32[:, (i % 2) * E:(i % 2 + 1) * E],
                             in_=ctx["x_b"][i * 128:(i + 1) * 128, :]).then_inc(dsem, 16)
                sy.dma_start(out=xtb[:, (i % 2) * E:(i % 2 + 1) * E],
                             in_=ctx["x_bh"][i * 128:(i + 1) * 128, :]).then_inc(dsem, 16)
                (cd0 if i % 2 == 0 else cd1).inc(32)

        @blk.scalar
        def _(sc):
            if KPREP < 2:
                return
            for i in range(NT):
                sl = slice((i % 2) * E, (i % 2 + 1) * E)
                o = (i % 2) * 8
                sc.wait_ge(d0 if i % 2 == 0 else d1,
                           (d0b if i % 2 == 0 else d1b) + 32 * (i // 2 + 1))
                if i >= 2:
                    sc.wait_ge(v0, v0b + 2 * (i - 1) - 1)
                sc.activation(out=sqd[:, (i % 2) * E:(i % 2 + 1) * E], in_=xt32[:, sl], func=AF.Square).then_inc(s0, 1)
                sc.wait_ge(v0, v0b + 2 * i + 1)
                sc.activation(out=st1[:, (i % 2):(i % 2) + 1], in_=st0[:, 3 + o:4 + o], func=AF.Sqrt).then_inc(s0, 1)
            cs0.inc(2 * NT)

        @blk.vector
        def _(v):
            if KPREP < 2:
                v.drain(fusable=False).then_inc(g0, 1)
                return
            v.memset(zrow[:], 0.0)
            v.memset(st0[:], 0.0)
            v.memset(ctx["ss1"][:], 0.0)
            v.memset(ctx["ss0"][:], 0.0).then_inc(g0, 1)
            v.drain()
            for i in range(NT):
                sl = slice((i % 2) * E, (i % 2 + 1) * E)
                o = (i % 2) * 8
                v.wait_ge(s0, s0b + 2 * i + 1)
                v.wait_ge(d0 if i % 2 == 0 else d1,
                          (d0b if i % 2 == 0 else d1b) + 32 * (i // 2 + 1))
                v.tensor_reduce(out=st0[:, 1 + o:2 + o], in_=sqd[:, (i % 2) * E:(i % 2 + 1) * E],
                                axis=AX.X, op=ALU.add)
                v.tensor_reduce(out=st0[:, 0 + o:1 + o], in_=xt32[:, sl], axis=AX.X, op=ALU.add)
                v.drain()
                v.tensor_scalar(out=st0[:, 4 + o:5 + o], in0=st0[:, 0 + o:1 + o],
                                scalar1=1.0 / E, scalar2=None, op0=ALU.mult)
                v.tensor_scalar(out=st0[:, 2 + o:3 + o], in0=st0[:, 1 + o:2 + o],
                                scalar1=1.0 / E, scalar2=None, op0=ALU.mult)
                v.drain()
                v.tensor_copy(out=st0[:, 0 + o:1 + o], in_=st0[:, 4 + o:5 + o])
                v.tensor_tensor(out=st0[:, 7 + o:8 + o], in0=st0[:, 4 + o:5 + o],
                                in1=st0[:, 4 + o:5 + o], op=ALU.mult)
                v.drain()
                v.tensor_tensor(out=st0[:, 3 + o:4 + o], in0=st0[:, 2 + o:3 + o],
                                in1=st0[:, 7 + o:8 + o], op=ALU.subtract)
                v.drain()
                v.tensor_scalar(out=st0[:, 3 + o:4 + o], in0=st0[:, 3 + o:4 + o],
                                scalar1=1e-5, scalar2=None, op0=ALU.add).then_inc(v0, 1)
                v.wait_ge(s0, s0b + 2 * i + 2)
                v.reciprocal(out=st0[:, 5 + o:6 + o], in_=st1[:, (i % 2):(i % 2) + 1])
                v.drain()
                v.tensor_tensor(out=st0[:, 6 + o:7 + o], in0=st0[:, 0 + o:1 + o],
                                in1=st0[:, 5 + o:6 + o], op=ALU.mult)
                v.drain()
                v.tensor_scalar(out=st0[:, 6 + o:7 + o], in0=st0[:, 6 + o:7 + o],
                                scalar1=-1.0, scalar2=None, op0=ALU.mult)
                v.drain()
                if i >= 2:
                    j = i - 2
                    if j % 2 == 0:
                        v.wait_ge(d4, d4b + 32 + 16 * (j // 2 + 1))
                    else:
                        v.wait_ge(d5, d5b + 16 * (j // 2 + 1))
                v.tensor_tensor(out=xnt[:, sl], in0=xt32[:, sl],
                                in1=st0[:, 5 + o:6 + o].broadcast_to([128, E]), op=ALU.mult)
                v.drain()
                if i == 0:
                    v.tensor_copy(out=ctx["ss0"][:, 0:16], in_=st0[:, 0:16])
                    v.drain()
                if i == NT - 1 and os.environ.get("KRERED"):
                    v.tensor_reduce(out=st0[:, 8:9], in_=xt32[:, 0:E], axis=AX.X, op=ALU.add)
                    v.tensor_reduce(out=st0[:, 9:10], in_=sqd[:, 0:E], axis=AX.X, op=ALU.add)
                    v.drain()
                    v.tensor_copy(out=ctx["ss0"][:, 16:32], in_=st0[:, 0:16])
                    v.tensor_copy(out=ctx["ss0"][:, 32:34], in_=st0[:, 8:10])
                    v.drain()
                v.tensor_tensor(out=xnt[:, sl], in0=xnt[:, sl],
                                in1=st0[:, 6 + o:7 + o].broadcast_to([128, E]),
                                op=ALU.add).then_inc(v0, 1)
            cv0.inc(2 * NT)
        cg0.inc()

        @blk.gpsimd
        def _(gp):
            if KPREP < 3:
                return
            gp.wait_ge(g0, cg0.v)
            gp.dma_start(out=ctx["xn0_tm"].ap()[0:1, :], in_=zrow[:]).then_inc(d4, 16); cd4.inc(16)
            gp.dma_start(out=bass.AP(ctx["xn0_tm"], 4097 * E, [[0, 1], [E, 255], [1, E]]),
                         in_=zrow.unsqueeze(1).broadcast_to([1, 255, E])
                         ).then_inc(d4, 16); cd4.inc(16)
            for i in range(NT):
                sl = slice((i % 2) * E, (i % 2 + 1) * E)
                gp.wait_ge(v0, v0b + 2 * i + 2)
                gp.dma_start(out=ctx["xn0_tm"].ap()[1 + i * 128:1 + (i + 1) * 128, :],
                             in_=xnt[:, sl]).then_inc(d4 if i % 2 == 0 else d5, 16)
                (cd4 if i % 2 == 0 else cd5).inc(16)

    if KPREP < 4:
        return
    with nc.Block() as blk:
        @blk.gpsimd
        def _(gp):
            gp.wait_ge(d0, cd0.v)
            gp.wait_ge(d4, cd4.v)
            gp.wait_ge(d5, cd5.v)
            rgA = gp.alloc_register()
            gp.reg_load(rgA, ctx["offs"][0:1, 0:1])
            gp.dma_start(out=ctx["xn0_own_tm"].ap(),
                         in_=bass.AP(ctx["xn0_tm"], rgA, [[E, EXT], [1, E]])
                         ).then_inc(d2, 16)
            cd2.inc(16)

        @blk.sync
        def _(sy):
            sy.wait_ge(d2, cd2.v)
            for f in range(8):
                sy.dma_start_transpose(ar.bf("xn0full", f * S, S),
                                       ctx["xn0_tm"].ap()[1:4097, f * 128:(f + 1) * 128]
                                       ).then_inc(d0, 16)
                cd0.inc(16)

        @blk.scalar
        def _(sc):
            sc.wait_ge(d0, cd0.v)

    ar.release("preploc")


def ph_ns(ctx):
    nc = ctx["nc"]
    ar = ctx["ar"]
    pb = ctx["pb"]
    p0, cp0 = _sem(ctx, "p0")
    v0, cv0 = _sem(ctx, "v0")
    s0, cs0 = _sem(ctx, "s0")
    g0, cg0 = _sem(ctx, "g0")
    d0, cd0 = _sem(ctx, "d0")
    ID = ctx["cst"][:, 0:128]
    EY2 = ctx["cst"][:, 128:256]
    sv = ctx["sv"]
    srow = ctx["srow"]

    with (
        nc.sbuf_tensor("ns", [128, 8 * 128], F32) as ns,
        nc.Block() as blk,
    ):
        C = lambda k: ns[:, k * 128:(k + 1) * 128]

        @blk.sync
        def _(sy):
            sy.dma_start(out=C(7), in_=ctx["a_hip"]).then_inc(d0, 16); cd0.inc(16)

        @blk.scalar
        def _(sc):
            sc.wait_ge(d0, cd0.v)
            sc.activation(out=sv[0:1, 4:5], in_=sv[0:1, 2:3], func=AF.Exp).then_inc(s0, 1)
            cs0.inc()

        @blk.gpsimd
        def _(gp):
            d5c, cd5c = _sem(ctx, "d5")
            for wsrc, wdst in [("wgT_h", "wg_bf"), ("wuT_h", "wu_bf"), ("wdT_h", "wd_bf")]:
                gp.dma_start(out=ctx[wdst].ap().rearrange("a b -> (a b)"),
                             in_=ctx[wsrc].ap().rearrange("a b -> (a b)")
                             ).then_inc(d5c, 16)
                cd5c.inc(16)
                gp.wait_ge(ctx["sems"]["g0"], 0)
        ctx["ffn_cast_d5"] = ctx["ctrs"]["d5"].v

        def pe(fn):
            @blk.tensor
            def _(te):
                fn(te)

        def ve(fn):
            @blk.vector
            def _(v):
                fn(v)

        def do_v(fn):
            def g(v):
                fn(v)
                v.drain(fusable=False).then_inc(v0, 1)
            ve(g); cv0.inc()

        def do_p(fn):
            def g(te):
                fn(te)
                te.drain(fusable=False).then_inc(p0, 1)
            pe(g); cp0.inc()

        do_p(lambda te: (te.wait_ge(s0, cs0.v),
                         te.matmul(pb[1][:, 0:2], ctx["onerf"][:], sv[0:1, 3:5],
                                   start=True, stop=True)))
        do_v(lambda v: (v.wait_ge(p0, cp0.v),
                        v.tensor_copy(out=sv[:, 5:6], in_=pb[1][:, 1:2]),
                        v.tensor_copy(out=sv[:, 6:7], in_=pb[1][:, 0:1])))
        do_v(lambda v: (v.tensor_tensor(out=C(0), in0=C(7), in1=sv[:, 6:7].broadcast_to([128, 128]), op=ALU.mult),
                        v.tensor_scalar(out=sv[:, 7:8], in0=sv[:, 5:6], scalar1=0.5, scalar2=None, op0=ALU.mult),
                        v.drain(),
                        v.tensor_tensor(out=C(0), in0=C(0), in1=sv[:, 7:8].broadcast_to([128, 128]), op=ALU.mult)))
        do_p(lambda te: (te.wait_ge(v0, cv0.v), te.transpose(pb[0][:, 0:128], C(0), ID)))
        do_v(lambda v: (v.wait_ge(p0, cp0.v),
                        v.tensor_tensor(out=C(1), in0=ID, in1=C(0), op=ALU.add),
                        v.tensor_tensor(out=C(2), in0=ID, in1=pb[0][:, 0:128], op=ALU.add),
                        v.tensor_tensor(out=C(3), in0=ID, in1=C(0), op=ALU.subtract)))

        @blk.scalar
        def _(sc):
            sc.wait_ge(v0, cv0.v)
            sc.activation(out=C(7), in_=C(1), func=AF.Abs).then_inc(s0, 1)
            cs0.inc()

        do_p(lambda te: (te.wait_ge(s0, cs0.v),
                         te.matmul(pb[1][0:1, 0:128], ctx["onec"][:], C(7), start=True, stop=True),
                         te.transpose(pb[0][:, 128:256], C(7), ID)))
        do_v(lambda v: (v.wait_ge(p0, cp0.v), v.tensor_copy(out=C(7), in_=pb[0][:, 128:256])))
        do_p(lambda te: (te.wait_ge(v0, cv0.v),
                         te.matmul(pb[1][0:1, 128:256], ctx["onec"][:], C(7), start=True, stop=True)))
        do_v(lambda v: (v.wait_ge(p0, cp0.v),
                        v.tensor_reduce(out=srow[0:1, 0:1], in_=pb[1][0:1, 0:128], axis=AX.X, op=ALU.max),
                        v.tensor_reduce(out=srow[0:1, 1:2], in_=pb[1][0:1, 128:256], axis=AX.X, op=ALU.max),
                        v.drain(),
                        v.tensor_tensor(out=srow[0:1, 2:3], in0=srow[0:1, 0:1], in1=srow[0:1, 1:2], op=ALU.mult),
                        v.drain(),
                        v.reciprocal(out=srow[0:1, 3:4], in_=srow[0:1, 2:3])))

        do_p(lambda te: (te.wait_ge(v0, cv0.v),
                         te.matmul(pb[1][:, 2:3], ctx["onerf"][:], srow[0:1, 3:4],
                                   start=True, stop=True)))
        do_v(lambda v: (v.wait_ge(p0, cp0.v),
                        v.tensor_copy(out=sv[:, 8:9], in_=pb[1][:, 2:3])))
        do_v(lambda v: (v.tensor_tensor(out=C(4), in0=C(2), in1=sv[:, 8:9].broadcast_to([128, 128]), op=ALU.mult),
                        v.tensor_tensor(out=C(5), in0=C(1), in1=sv[:, 8:9].broadcast_to([128, 128]), op=ALU.mult)))

        for it in range(NS_ITERS):
            do_p(lambda te: (te.wait_ge(v0, cv0.v),
                             te.matmul(pb[2][:, 0:128], C(2), C(4), start=True, stop=True)))
            do_v(lambda v: (v.wait_ge(p0, cp0.v),
                            v.tensor_tensor(out=C(6), in0=EY2, in1=pb[2][:, 0:128], op=ALU.subtract)))
            do_p(lambda te: (te.wait_ge(v0, cv0.v),
                             te.matmul(pb[2][:, 128:256], C(6), C(5), start=True, stop=True)))
            do_v(lambda v: (v.wait_ge(p0, cp0.v), v.tensor_copy(out=C(5), in_=pb[2][:, 128:256])))
            do_p(lambda te: (te.wait_ge(v0, cv0.v),
                             te.transpose(pb[2][:, 256:384], C(5), ID)))
            do_v(lambda v: (v.wait_ge(p0, cp0.v), v.tensor_copy(out=C(4), in_=pb[2][:, 256:384])))

        do_v(lambda v: v.tensor_tensor(out=sv[:, 9:10], in0=sv[:, 0:1], in1=sv[:, 5:6], op=ALU.mult))
        do_p(lambda te: (te.wait_ge(v0, cv0.v),
                         te.matmul(pb[3][:, 0:128], C(3), C(5), start=True, stop=True),
                         te.matmul(pb[3][:, 128:129], C(5), sv[:, 9:10], start=True, stop=True)))
        do_v(lambda v: (v.wait_ge(p0, cp0.v),
                        v.tensor_copy(out=ctx["adT"][:], in_=pb[3][:, 0:128]),
                        v.tensor_copy(out=sv[:, 10:11], in_=pb[3][:, 128:129]),
                        v.tensor_copy(out=ctx["ccol"][:], in_=sv[:, 1:2])))
        do_p(lambda te: (te.wait_ge(v0, cv0.v),
                         te.transpose(pb[3][0:1, 256:384], sv[:, 10:11], ID)))
        do_v(lambda v: (v.wait_ge(p0, cp0.v),
                        v.tensor_copy(out=ctx["brow"][:], in_=pb[3][0:1, 256:384])))


def ph_qkv(ctx):
    nc = ctx["nc"]
    ar = ctx["ar"]
    ar.alloc("qf", 4 * S * 2)
    ar.alloc("kf", 4 * S * 2)
    ar.alloc("vaug", 32 * 520 * 2)
    pb = ctx["pb"]
    d0, cd0 = _sem(ctx, "d0")
    d2, cd2 = _sem(ctx, "d2")
    d3, cd3 = _sem(ctx, "d3")
    p0, cp0 = _sem(ctx, "p0")
    v0, cv0 = _sem(ctx, "v0")
    s0, cs0 = _sem(ctx, "s0")
    xn0 = lambda kc, a, n: ar.bf("xn0full", kc * S + a, n)
    qfv = lambda c, a, n: ar.bf("qf", c * S + a, n)
    kfv = lambda c, a, n: ar.bf("kf", c * S + a, n)
    vaug = lambda a, n: ar.bf("vaug", a, n)
    cum = lambda ncg: 8 * min(ncg, 8) + 32 * max(0, ncg - 8)

    p0b, v0b, s0b = cp0.v, cv0.v, cs0.v
    d2b, d3b = cd2.v, cd3.v
    with nc.Block() as blk:
        @blk.vector
        def _(v):
            v.memset(vaug(0, 32 * 520), 1.0).then_inc(v0, 1)
        cv0.inc()
        v0b = cv0.v

        @blk.gpsimd
        def _(gp):
            for cg in range(12):
                buf = ctx["wsa"] if cg % 2 == 0 else ctx["wsb"]
                if cg >= 2:
                    gp.wait_ge(p0, p0b + cum(cg - 1))
                gp.dma_start(out=buf[:, 0:1024],
                             in_=bass.AP(ctx["wattn_h"], cg * 128,
                                         [[1536, 128], [128 * 1536, 8], [1, 128]])
                             ).then_inc(d2 if cg % 2 == 0 else d3, 16)
                (cd2 if cg % 2 == 0 else cd3).inc(16)

        @blk.tensor
        def _(te):
            te.wait_ge(v0, v0b)
            te.wait_ge(d0, cd0.v)
            n = 0
            for cg in range(12):
                buf = ctx["wsa"] if cg % 2 == 0 else ctx["wsb"]
                te.wait_ge(d2 if cg % 2 == 0 else d3,
                           (d2b if cg % 2 == 0 else d3b) + 16 * (cg // 2 + 1))
                if cg < 8:
                    for tc in range(8):
                        j = cg * 8 + tc
                        if j >= 4:
                            te.wait_ge(s0, s0b + (j - 3))
                        bank = pb[tc % 4]
                        for kc in range(8):
                            te.matmul(bank[:, 0:512], buf[:, kc * 128:(kc + 1) * 128],
                                      xn0(kc, tc * 512, 512), start=(kc == 0), stop=False)
                        te.matmul(bank[:, 0:512], ctx["biasrows"][0:1, cg * 128:(cg + 1) * 128],
                                  ctx["onerb"][0:1, 0:512], start=False, stop=True).then_inc(p0, 1)
                        n += 1
                else:
                    for tc in range(8):
                        for tt in range(4):
                            idx = (cg - 8) * 32 + tc * 4 + tt
                            if idx >= 2:
                                te.wait_ge(v0, v0b + (idx - 1))
                            t0 = tc * 512 + tt * 128
                            bank = pb[4 + idx % 2]
                            for kc in range(8):
                                te.matmul(bank[:, 0:128], xn0(kc, t0, 128),
                                          buf[:, kc * 128:(kc + 1) * 128],
                                          start=(kc == 0), stop=False)
                            te.matmul(bank[:, 0:128], ctx["onerb"][0:1, 0:128],
                                      ctx["biasrows"][0:1, cg * 128:(cg + 1) * 128],
                                      start=False, stop=True).then_inc(p0, 1)
                            n += 1
            cp0.inc(n)

        @blk.scalar
        def _(sc):
            for cg in range(8):
                for tc in range(8):
                    sc.wait_ge(p0, p0b + cg * 8 + tc + 1)
                    bank = pb[tc % 4]
                    if cg < 4:
                        sc.activation(out=qfv(cg, tc * 512, 512), in_=bank[:, 0:512],
                                      func=AF.Copy, scale=0.125).then_inc(s0, 1)
                    else:
                        sc.activation(out=kfv(cg - 4, tc * 512, 512), in_=bank[:, 0:512],
                                      func=AF.Copy).then_inc(s0, 1)
            cs0.inc(64)

        @blk.vector
        def _(v):
            for cg in range(8, 12):
                for tc in range(8):
                    for tt in range(4):
                        idx = (cg - 8) * 32 + tc * 4 + tt
                        v.wait_ge(p0, p0b + 64 + idx + 1)
                        bank = pb[4 + idx % 2]
                        tile = tc * 4 + tt
                        hh = (cg - 8) * 2
                        v.tensor_copy(out=vaug(tile * 520 + hh * 65, 64), in_=bank[:, 0:64])
                        v.tensor_copy(out=vaug(tile * 520 + (hh + 1) * 65, 64), in_=bank[:, 64:128]).then_inc(v0, 1)
            cv0.inc(128)


def ph_kquant(ctx):
    nc = ctx["nc"]
    ar = ctx["ar"]
    pb = ctx["pb"]
    p0, cp0 = _sem(ctx, "p0")
    v0, cv0 = _sem(ctx, "v0")
    s0, cs0 = _sem(ctx, "s0")
    d2, cd2 = _sem(ctx, "d2")
    sv, srow = ctx["sv"], ctx["srow"]
    QL = S - 32
    kfv = lambda c, a, n: ar.bf("kf", c * S + a, n)

    with (
        nc.sbuf_tensor("kst", [128, 8], F32) as kst,
        nc.sbuf_tensor("kstt", [8, 128], F32) as kstt,
        nc.sbuf_tensor("st8", [8, 4], F32) as st8,
        nc.sbuf_tensor("svt", [2, 12], F32) as svt,
        nc.Block() as blk,
    ):
        @blk.vector
        def _(v):
            v.wait_ge(s0, cs0.v)
            for c in range(4):
                v.tensor_reduce(out=kst[:, c:c + 1], in_=kfv(c, 0, QL), axis=AX.X, op=ALU.min)
                mmv = v.tensor_reduce(out=kst[:, 4 + c:5 + c], in_=kfv(c, 0, QL),
                                      axis=AX.X, op=ALU.max)
            mmv.then_inc(v0, 1)
            cv0.inc()

        @blk.tensor
        def _(te):
            te.wait_ge(v0, cv0.v)
            te.transpose(pb[0][0:8, 0:128], kst[:, 0:8], ctx["cst"][:, 0:128]).then_inc(p0, 1)
            cp0.inc()

        @blk.vector
        def _(v):
            v.wait_ge(p0, cp0.v)
            v.tensor_copy(out=kstt[:], in_=pb[0][0:8, 0:128])
            v.drain()
            k3 = kstt[0:8, :].rearrange("p (h c) -> p h c", c=64)
            v.tensor_reduce(out=st8[0:8, 0:2], in_=k3, axis=AX.X, op=ALU.min)
            v.tensor_reduce(out=st8[0:8, 2:4], in_=k3, axis=AX.X, op=ALU.max).then_inc(v0, 1)
            cv0.inc()

        @blk.gpsimd
        def _(gp):
            gp.wait_ge(v0, cv0.v)
            gp.dma_start(out=srow[0:1, 8:16], in_=st8[0:4, 0:2]).then_inc(d2, 16)
            cd2.inc(16)
            gp.dma_start(out=srow[0:1, 16:24], in_=st8[4:8, 2:4]).then_inc(d2, 16)
            cd2.inc(16)

        @blk.vector
        def _(v):
            v.wait_ge(d2, cd2.v)
            v.tensor_scalar(out=srow[0:1, 8:24], in0=srow[0:1, 8:24],
                            scalar1=-1e4, scalar2=1e4, op0=ALU.max, op1=ALU.min)
            v.drain()
            v.tensor_tensor(out=srow[0:1, 24:32], in0=srow[0:1, 16:24], in1=srow[0:1, 8:16],
                            op=ALU.subtract)
            v.drain()
            v.tensor_scalar(out=srow[0:1, 24:32], in0=srow[0:1, 24:32],
                            scalar1=1.0 / 3.0, scalar2=1e-8, op0=ALU.mult, op1=ALU.max)
            v.drain()
            v.reciprocal(out=srow[0:1, 32:40], in_=srow[0:1, 24:32])
            v.drain()
            v.tensor_copy(out=srow[0:1, 16:24], in_=srow[0:1, 8:16]).then_inc(v0, 1)
            cv0.inc()

        @blk.gpsimd
        def _(gp):
            gp.wait_ge(v0, cv0.v)
            srci = srow[0:1, 16:40].rearrange("p (a h) -> p a h", h=2)
            with nc.allow_non_contiguous_dma(reason="24-element stat shuffle"):
                gp.dma_start(out=svt[0:1, :].unsqueeze(2),
                             in_=srci[:, :, 0:1]).then_inc(d2, 16)
                cd2.inc(16)
                gp.dma_start(out=svt[1:2, :].unsqueeze(2),
                             in_=srci[:, :, 1:2]).then_inc(d2, 16)
                cd2.inc(16)

        @blk.tensor
        def _(te):
            te.wait_ge(d2, cd2.v)
            te.matmul(pb[1][:, 0:12], ctx["hselt"][:], svt[:], start=True, stop=True).then_inc(p0, 1)
            cp0.inc()

        @blk.vector
        def _(v):
            v.wait_ge(p0, cp0.v)
            v.tensor_copy(out=sv[:, 11:23], in_=pb[1][:, 0:12]).then_inc(v0, 1)
            cv0.inc()

        @blk.vector
        def _(v):
            kqt = ar.f32("tmpA", 0, 512)
            kqr = ar.f32("tmpA", 512, 512)
            kqi = ar.i32("tmpA", 1024, 512)
            for c in range(4):
                mn, scl, rsc = sv[:, 11 + c:12 + c], sv[:, 15 + c:16 + c], sv[:, 19 + c:20 + c]
                for bi in range(8):
                    lo = bi * 512
                    w = min(512, QL - lo)
                    v.tensor_scalar(out=kqt[:, 0:w], in0=kfv(c, lo, w),
                                    scalar1=-1e4, scalar2=1e4, op0=ALU.max, op1=ALU.min)
                    v.drain()
                    v.tensor_tensor(out=kqt[:, 0:w], in0=kqt[:, 0:w],
                                    in1=mn.broadcast_to([128, w]), op=ALU.subtract)
                    v.drain()
                    v.tensor_tensor(out=kqt[:, 0:w], in0=kqt[:, 0:w],
                                    in1=rsc.broadcast_to([128, w]), op=ALU.mult)
                    v.drain()
                    v.tensor_copy(out=kqi[:, 0:w], in_=kqt[:, 0:w])
                    v.drain()
                    v.tensor_copy(out=kqr[:, 0:w], in_=kqi[:, 0:w])
                    v.drain()
                    v.tensor_tensor(out=kqt[:, 0:w], in0=kqt[:, 0:w],
                                    in1=kqr[:, 0:w], op=ALU.subtract)
                    v.drain()
                    v.tensor_scalar(out=kqt[:, 0:w], in0=kqt[:, 0:w],
                                    scalar1=0.5, scalar2=None, op0=ALU.is_ge)
                    v.drain()
                    v.tensor_tensor(out=kqr[:, 0:w], in0=kqr[:, 0:w],
                                    in1=kqt[:, 0:w], op=ALU.add)
                    v.drain()
                    v.tensor_tensor(out=kqr[:, 0:w], in0=kqr[:, 0:w],
                                    in1=scl.broadcast_to([128, w]), op=ALU.mult)
                    v.drain()
                    v.tensor_tensor(out=kfv(c, lo, w), in0=kqr[:, 0:w],
                                    in1=mn.broadcast_to([128, w]), op=ALU.add)
                    v.drain()
            v.drain(fusable=False).then_inc(v0, 1); cv0.inc()


def ph_vquant(ctx):
    nc = ctx["nc"]
    ar = ctx["ar"]
    v0, cv0 = _sem(ctx, "v0")
    vaug = lambda a, n: ar.bf("vaug", a, n)

    ar.alloc("vqtmp", (32 + 512 * 3) * 4)
    vst = ar.f32("vqtmp", 0, 32)
    vtmp = ar.f32("vqtmp", 32, 512)
    vtr = ar.f32("vqtmp", 544, 512)
    vti = ar.i32("vqtmp", 1056, 512)
    with (
        nc.Block() as blk,
    ):
        @blk.vector
        def _(v):
            for tile in range(32):
                nr = 128 if tile < 31 else 96
                sl3 = vaug(tile * 520, 520)[0:nr, :].rearrange(
                    "p (h c) -> p h c", c=65)[:, :, 0:64]
                vt3 = vtmp[0:nr, :].rearrange("p (h c) -> p h c", c=64)
                vi3 = vti[0:nr, :].rearrange("p (h c) -> p h c", c=64)
                mn = vst[0:nr, 0:8]
                scl = vst[0:nr, 16:24]
                rsc = vst[0:nr, 24:32]
                bc = lambda t: t.unsqueeze(2).broadcast_to([nr, 8, 64])
                v.drain()
                v.tensor_scalar(out=vt3, in0=sl3, scalar1=-1e4, scalar2=1e4,
                                op0=ALU.max, op1=ALU.min)
                v.drain()
                v.tensor_reduce(out=mn, in_=vt3, axis=AX.X, op=ALU.min)
                v.tensor_reduce(out=vst[0:nr, 8:16], in_=vt3, axis=AX.X, op=ALU.max)
                v.drain()
                v.tensor_tensor(out=scl, in0=vst[0:nr, 8:16], in1=mn, op=ALU.subtract)
                v.drain()
                v.tensor_scalar(out=scl, in0=scl, scalar1=1.0 / 3.0, scalar2=1e-8,
                                op0=ALU.mult, op1=ALU.max)
                v.drain()
                v.reciprocal(out=rsc, in_=scl)
                v.drain()
                v.tensor_tensor(out=vt3, in0=vt3, in1=bc(mn), op=ALU.subtract)
                v.drain()
                v.tensor_tensor(out=vt3, in0=vt3, in1=bc(rsc), op=ALU.mult)
                v.drain()
                v.tensor_copy(out=vti[0:nr, :], in_=vtmp[0:nr, :])
                v.drain()
                v.tensor_copy(out=vtr[0:nr, :], in_=vti[0:nr, :])
                v.drain()
                v.tensor_tensor(out=vtmp[0:nr, :], in0=vtmp[0:nr, :],
                                in1=vtr[0:nr, :], op=ALU.subtract)
                v.drain()
                v.tensor_scalar(out=vtmp[0:nr, :], in0=vtmp[0:nr, :],
                                scalar1=0.5, scalar2=None, op0=ALU.is_ge)
                v.drain()
                v.tensor_tensor(out=vtr[0:nr, :], in0=vtr[0:nr, :],
                                in1=vtmp[0:nr, :], op=ALU.add)
                v.drain()
                vr3 = vtr[0:nr, :].rearrange("p (h c) -> p h c", c=64)
                v.tensor_tensor(out=vr3, in0=vr3, in1=bc(scl), op=ALU.mult)
                v.drain()
                mmv = v.tensor_tensor(out=sl3, in0=vr3, in1=bc(mn), op=ALU.add)
            mmv.then_inc(v0, 1)
            cv0.inc()
    ar.release("vqtmp")


def ph_mhat(ctx):
    nc = ctx["nc"]
    ar = ctx["ar"]
    pb = ctx["pb"]
    p0, cp0 = _sem(ctx, "p0")
    v0, cv0 = _sem(ctx, "v0")
    s0, cs0 = _sem(ctx, "s0")
    sv, srow = ctx["sv"], ctx["srow"]
    sq = lambda a, n: ar.bf("tmpA", a, n)

    p0b, v0b = cp0.v, cv0.v
    with nc.Block() as blk:
        @blk.vector
        def _(v):
            v.wait_ge(s0, cs0.v)
            for c in range(4):
                if c >= 1:
                    v.wait_ge(p0, p0b + c)
                buf = ar.bf("qf", c * S, S)
                v.tensor_tensor(out=sq(0, S), in0=buf, in1=buf, op=ALU.mult).then_inc(v0, 1)
            v.wait_ge(p0, p0b + 4)
            for tc in range(8):
                v.tensor_reduce(out=srow[0:1, 48 + tc:49 + tc], in_=pb[tc][0:1, 0:512],
                                axis=AX.X, op=ALU.max)
            v.drain()
            v.tensor_reduce(out=srow[0:1, 40:41], in_=srow[0:1, 48:56], axis=AX.X, op=ALU.max).then_inc(v0, 1)
            for c in range(4):
                v.wait_ge(p0, p0b + 4 + c)
                buf = ar.bf("kf", c * S, S)
                v.tensor_tensor(out=sq(0, S), in0=buf, in1=buf, op=ALU.mult).then_inc(v0, 1)
            v.wait_ge(p0, p0b + 8)
            for tc in range(8):
                v.tensor_reduce(out=srow[0:1, 48 + tc:49 + tc], in_=pb[tc][0:1, 0:512],
                                axis=AX.X, op=ALU.max)
            v.drain()
            v.tensor_reduce(out=srow[0:1, 41:42], in_=srow[0:1, 48:56], axis=AX.X, op=ALU.max)
            v.drain()
            v.tensor_tensor(out=srow[0:1, 42:43], in0=srow[0:1, 40:41], in1=srow[0:1, 41:42],
                            op=ALU.mult).then_inc(v0, 1)
            cv0.inc(10)

        @blk.tensor
        def _(te):
            for half in range(2):
                for c in range(4):
                    te.wait_ge(v0, v0b + half * 5 + c + 1)
                    for tc in range(8):
                        mm = te.matmul(pb[tc][0:1, 0:512], ctx["onecb"][:], sq(tc * 512, 512),
                                       start=(c == 0), stop=(c == 3))
                        if tc == 7:
                            mm.then_inc(p0, 1)
            cp0.inc(8)

        @blk.scalar
        def _(sc):
            sc.wait_ge(v0, cv0.v)
            sc.activation(out=srow[0:1, 43:44], in_=srow[0:1, 42:43], func=AF.Sqrt).then_inc(s0, 1)
            cs0.inc()

        @blk.vector
        def _(v):
            v.wait_ge(s0, cs0.v)
            v.tensor_scalar(out=srow[0:1, 44:45], in0=srow[0:1, 43:44], scalar1=-1.0,
                            scalar2=None, op0=ALU.mult).then_inc(v0, 1)
            cv0.inc()

        @blk.tensor
        def _(te):
            te.wait_ge(v0, cv0.v)
            te.matmul(pb[1][:, 16:17], ctx["onerf"][:], srow[0:1, 44:45],
                      start=True, stop=True).then_inc(p0, 1)
            cp0.inc()

        @blk.vector
        def _(v):
            v.wait_ge(p0, cp0.v)
            v.tensor_copy(out=sv[:, 23:24], in_=pb[1][:, 16:17]).then_inc(v0, 1)
            cv0.inc()


def _attn_seq():
    seq = []
    for h in range(8):
        for j in range(8):
            seq.append((h, j, 4 * (j + 1)))
    return seq


def ph_attn(ctx):
    nc = ctx["nc"]
    ar = ctx["ar"]
    pb = ctx["pb"]
    p0, cp0 = _sem(ctx, "p0")
    v0, cv0 = _sem(ctx, "v0")
    s0, cs0 = _sem(ctx, "s0")
    d3, cd3 = _sem(ctx, "d3")
    sv = ctx["sv"]
    ering = ctx["eringb"]
    qfb = lambda c: ar.bf("qf", c * S, S)
    kfb = lambda c: ar.bf("kf", c * S, S)
    vaug = lambda a, n: ar.bf("vaug", a, n)
    seq = _attn_seq()

    p0b, v0b, s0b, d3b = cp0.v, cv0.v, cs0.v, cd3.v

    def replay():
        pe, act, dve = 0, 0, 0
        ev = []
        for (h, j, nk) in seq:
            for i in range(nk):
                pe += 1; ev.append(("qk", h, j, i, pe, act, dve))
                act += 1; ev.append(("exp", h, j, i, pe, act, dve))
                if i >= 4 * j:
                    dve += 1; ev.append(("mask", h, j, i, pe, act, dve))
                if i >= 1:
                    pe += 1; ev.append(("av", h, j, i - 1, pe, act, dve))
            pe += 1; ev.append(("av", h, j, nk - 1, pe, act, dve))
            dve += 1; ev.append(("recip", h, j, -1, pe, act, dve))
            pe += 1; ev.append(("bcast", h, j, -1, pe, act, dve))
            dve += 1; ev.append(("ymul", h, j, -1, pe, act, dve))
            ev.append(("ydma", h, j, -1, pe, act, dve))
        return ev

    ev = replay()
    exp_after_count = {}
    av_after = {}
    bcast_after = {}
    ymul_after = {}
    recip_after = {}
    qk_after = {}
    mask_after = {}
    for (kind, h, j, i, pe, act, dve) in ev:
        if kind == "exp":
            exp_after_count[(h, j, i)] = act
        elif kind == "av":
            av_after[(h, j, i)] = pe
        elif kind == "qk":
            qk_after[(h, j, i)] = pe
        elif kind == "mask":
            mask_after[(h, j, i)] = dve
        elif kind == "bcast":
            bcast_after[(h, j)] = pe
        elif kind == "ymul":
            ymul_after[(h, j)] = dve
        elif kind == "recip":
            recip_after[(h, j)] = dve

    with (
        nc.sbuf_tensor("zrec", [1, 512], F32) as zrec,
        nc.sbuf_tensor("zbc", [64, 512], F32) as zbc,
        nc.sbuf_tensor("ytile", [64, 512], BF16) as ytile,
        nc.Block() as blk,
    ):
        @blk.tensor
        def _(te):
            te.wait_ge(v0, v0b)
            te.wait_ge(s0, s0b)
            for si, (h, j, nk) in enumerate(seq):
                ch, half = h // 2, h % 2
                rows = slice(half * 64, (half + 1) * 64)
                av = pb[4 + si % 2]

                def emit_av(i):
                    if i == 0 and si >= 2:
                        h2, j2 = seq[si - 2][0], seq[si - 2][1]
                        te.wait_ge(v0, v0b + ymul_after[(h2, j2)])
                    if i >= 4 * j:
                        te.wait_ge(v0, v0b + mask_after[(h, j, i)])
                    else:
                        te.wait_ge(s0, s0b + exp_after_count[(h, j, i)])
                    te.matmul(av[0:65, 0:512], vaug(i * 520 + h * 65, 65),
                              ering[:, (i % 2) * 512:(i % 2 + 1) * 512],
                              start=(i == 0), stop=(i == nk - 1)).then_inc(p0, 1)

                for i in range(nk):
                    if i >= 2:
                        te.wait_ge(s0, s0b + exp_after_count[(h, j, i - 2)])
                    elif si > 0:
                        ph_, pj_, pn_ = seq[si - 1]
                        te.wait_ge(s0, s0b + exp_after_count[(ph_, pj_, pn_ - 2 + i)])
                    te.matmul(pb[i % 2][:, 0:512],
                              kfb(ch)[rows, i * 128:(i + 1) * 128],
                              qfb(ch)[rows, j * 512:(j + 1) * 512], start=True, stop=True).then_inc(p0, 1)
                    if i >= 1:
                        emit_av(i - 1)
                emit_av(nk - 1)
                te.wait_ge(v0, v0b + recip_after[(h, j)])
                if si >= 1:
                    h1, j1 = seq[si - 1][0], seq[si - 1][1]
                    te.wait_ge(v0, v0b + ymul_after[(h1, j1)])
                te.matmul(pb[6][0:64, 0:512], ctx["onerf"][0:1, 0:64], zrec[0:1, 0:512],
                          start=True, stop=True).then_inc(p0, 1)
            cp0.inc(sum(2 * nk + 1 for (_, _, nk) in seq))

        @blk.scalar
        def _(sc):
            sc.wait_ge(v0, v0b)
            for (h, j, nk) in seq:
                for i in range(nk):
                    sc.wait_ge(p0, p0b + qk_after[(h, j, i)])
                    sc.activation(out=ering[:, (i % 2) * 512:(i % 2 + 1) * 512],
                                  in_=pb[i % 2][:, 0:512], func=AF.Exp,
                                  bias=sv[:, 23:24], scale=1.0).then_inc(s0, 1)
            cs0.inc(sum(nk for (_, _, nk) in seq))

        @blk.vector
        def _(v):
            ndma = 0
            for si, (h, j, nk) in enumerate(seq):
                av = pb[4 + si % 2]
                for i in range(nk):
                    if i >= 4 * j:
                        v.wait_ge(s0, s0b + exp_after_count[(h, j, i)])
                        r = i - 4 * j
                        v.tensor_tensor(out=ering[:, (i % 2) * 512:(i % 2 + 1) * 512],
                                        in0=ering[:, (i % 2) * 512:(i % 2 + 1) * 512],
                                        in1=ar.bf("maskt", r * 512, 512), op=ALU.mult).then_inc(v0, 1)
                v.wait_ge(p0, p0b + av_after[(h, j, nk - 1)])
                v.reciprocal(out=zrec[0:1, 0:512], in_=av[64:65, 0:512]).then_inc(v0, 1)
                v.wait_ge(p0, p0b + bcast_after[(h, j)])
                if si >= 1:
                    v.wait_ge(d3, d3b + 16 * si)
                v.tensor_copy(out=zbc[:], in_=pb[6][0:64, 0:512])
                v.drain()
                v.tensor_tensor(out=ytile[:, 0:512], in0=av[0:64, 0:512],
                                in1=zbc[:], op=ALU.mult).then_inc(v0, 1)
            cv0.inc(max(d[1] for d in [(0, ymul_after[seq[-1][0], seq[-1][1]])]))

        @blk.gpsimd
        def _(gp):
            for si, (h, j, nk) in enumerate(seq):
                gp.wait_ge(v0, v0b + ymul_after[(h, j)])
                gp.dma_start(out=ctx["y0_in"].ap()[h * 64:(h + 1) * 64, j * 512:(j + 1) * 512],
                             in_=ytile[:, 0:512]).then_inc(d3, 16)
                cd3.inc(16)
            gp.wait_ge(d3, cd3.v)


def ph_l0tail(ctx):
    nc = ctx["nc"]
    ar = ctx["ar"]
    pb = ctx["pb"]
    d1, cd1 = _sem(ctx, "d1")
    d2, cd2 = _sem(ctx, "d2")
    d3, cd3 = _sem(ctx, "d3")
    d4, cd4 = _sem(ctx, "d4")
    d5, cd5 = _sem(ctx, "d5")
    p0, cp0 = _sem(ctx, "p0")
    v0, cv0 = _sem(ctx, "v0")
    s0, cs0 = _sem(ctx, "s0")
    g0, cg0 = _sem(ctx, "g0")
    cc, ccc = _sem(ctx, "cc")

    with nc.Block() as blk:
        @blk.gpsimd
        def _(gp):
            with allow_ici_all_to_all():
                gp.collective_compute(
                    "AllGather", ALU.bypass,
                    replica_groups=[[0, 1], [2, 3], [4, 5], [6, 7]],
                    ins=[ctx["y0_in"].ap().opt()],
                    outs=[ctx["y0_out"].ap().opt()],
                ).then_inc(cc, 1)
            ccc.inc()

    ar.release("xn0full", "qf", "kf", "vaug", "tmpA", "maskt")
    ar.alloc("x1", 8 * TOK * 4)
    ar.alloc("xn0e", 8 * EXT * 2)
    ar.alloc("mixr", 8 * TOK * 2)
    ar.alloc("mixk", 8 * TOK * 2)
    x1 = lambda f, a, n: ar.f32("x1", f * TOK + a, n)
    xn0e = lambda kc, a, n: ar.bf("xn0e", kc * EXT + a, n)
    xrv = lambda kc, a, n: ar.bf("mixr", kc * TOK + a, n)
    xkv = lambda kc, a, n: ar.bf("mixk", kc * TOK + a, n)

    ar.alloc("xtt", 2 * E * 4)
    xtt = ar.f32("xtt", 0, 2 * E)
    with (
        nc.Block() as blk,
    ):
        d1b, p0b, v0b = cd1.v, cp0.v, cv0.v
        d4b, d5b = cd4.v, cd5.v

        @blk.sync
        def _(sy):
            for f in range(8):
                sy.dma_start_transpose(xn0e(f, 0, EXT),
                                       ctx["xn0_own_tm"].ap()[:, f * 128:(f + 1) * 128]
                                       ).then_inc(d1, 16)
                cd1.inc(16)

        @blk.gpsimd
        def _(gp):
            for i in range(16):
                if i >= 2:
                    gp.wait_ge(p0, p0b + 8 * (i - 1))
                gp.dma_start(out=xtt[:, (i % 2) * E:(i % 2 + 1) * E],
                             in_=ctx["x_own"][i * 128:(i + 1) * 128, :]
                             ).then_inc(d4 if i % 2 == 0 else d5, 16)
                (cd4 if i % 2 == 0 else cd5).inc(16)

        @blk.tensor
        def _(te):
            for i in range(16):
                te.wait_ge(d4 if i % 2 == 0 else d5,
                           (d4b if i % 2 == 0 else d5b) + 16 * (i // 2 + 1))
                for f in range(8):
                    idx = i * 8 + f
                    if idx >= 4:
                        te.wait_ge(v0, v0b + idx - 3)
                    te.transpose(pb[idx % 4][:, 0:128],
                                 xtt[:, (i % 2) * E + f * 128:(i % 2) * E + (f + 1) * 128],
                                 ctx["cst"][:, 0:128]).then_inc(p0, 1)
            cp0.inc(128)

        @blk.vector
        def _(v):
            for i in range(16):
                for f in range(8):
                    idx = i * 8 + f
                    v.wait_ge(p0, p0b + idx + 1)
                    v.tensor_copy(out=x1(f, i * 128, 128), in_=pb[idx % 4][:, 0:128]).then_inc(v0, 1)
            cv0.inc(128)

        @blk.scalar
        def _(sc):
            sc.wait_ge(d1, cd1.v)

    ar.release("xtt")
    with nc.Block() as blk:
        @blk.vector
        def _(v):
            v.wait_ge(d1, cd1.v)
            for kc in range(8):
                for half in range(2):
                    a = half * 1024
                    xn = xn0e(kc, 1 + a, 1024)
                    xp = xn0e(kc, a, 1024)
                    d_r = xrv(kc, a, 1024)
                    d_k = xkv(kc, a, 1024)
                    v.tensor_tensor(out=d_k, in0=xn, in1=xp, op=ALU.subtract)
                    v.drain()
                    v.tensor_tensor(out=d_r, in0=d_k,
                                    in1=ctx["mixmu"][:, kc:kc + 1].broadcast_to([128, 1024]),
                                    op=ALU.mult)
                    v.drain()
                    v.tensor_tensor(out=d_k, in0=d_k,
                                    in1=ctx["mixmu"][:, 8 + kc:9 + kc].broadcast_to([128, 1024]),
                                    op=ALU.mult)
                    v.drain()
                    v.tensor_tensor(out=d_r, in0=d_r, in1=xp, op=ALU.add)
                    mmv = v.tensor_tensor(out=d_k, in0=d_k, in1=xp, op=ALU.add)
            mmv.then_inc(v0, 1)
            cv0.inc()

    ar.release("xn0e")
    ar.alloc("kact", 8 * TOK * 2)
    ar.alloc("wfull", 8 * 1024 * 2)
    wfull = lambda kc, dc: ar.bf("wfull", kc * 1024 + dc * 128, 128)
    kact = lambda kc, a, n: ar.bf("kact", kc * TOK + a, n)

    def stream_wfull(gp, wname):
        for kc in range(8):
            gp.dma_start(out=ar.bf("wfull", kc * 1024, 1024),
                         in_=bass.AP(ctx[wname + "_h"], kc * 128 * E, [[E, 128], [1, E]])
                         ).then_inc(d2, 16)
            cd2.inc(16)

    with nc.Block() as blk:
        d2b, p0b, v0b = cd2.v, cp0.v, cv0.v

        @blk.gpsimd
        def _(gp):
            stream_wfull(gp, "wkT")

        @blk.tensor
        def _(te):
            te.wait_ge(d2, d2b + 128)
            n = 0
            for dc in range(8):
                for tc in range(4):
                    idx = dc * 4 + tc
                    if idx >= 4:
                        te.wait_ge(v0, v0b + 2 * (idx - 3))
                    bank = pb[idx % 4]
                    for kc in range(8):
                        mm = te.matmul(bank[:, 0:512], wfull(kc, dc), xkv(kc, tc * 512, 512),
                                       start=(kc == 0), stop=(kc == 7))
                    mm.then_inc(p0, 1)
                    n += 1
            cp0.inc(n)

        @blk.vector
        def _(v):
            for dc in range(8):
                for tc in range(4):
                    idx = dc * 4 + tc
                    v.wait_ge(p0, p0b + idx + 1)
                    bank = pb[idx % 4]
                    v.tensor_scalar(out=kact(dc, tc * 512, 512), in0=bank[:, 0:512],
                                    scalar1=0.0, scalar2=None, op0=ALU.max)
                    v.drain()
                    v.tensor_tensor(out=kact(dc, tc * 512, 512), in0=kact(dc, tc * 512, 512),
                                    in1=kact(dc, tc * 512, 512), op=ALU.mult).then_inc(v0, 2)
            cv0.inc(64)

    ar.release("mixk")
    ar.alloc("wfull2", 8 * 1024 * 2)
    ar.alloc("kvslab", 8 * 512 * 2)
    ar.alloc("sgslab", 8 * 512 * 2)
    wfull2 = lambda kc, dc: ar.bf("wfull2", kc * 1024 + dc * 128, 128)
    kvs = lambda dc: ar.bf("kvslab", dc * 512, 512)
    sgs = lambda dc: ar.bf("sgslab", dc * 512, 512)

    with nc.Block() as blk:
        d2b, p0b, v0b, s0b = cd2.v, cp0.v, cv0.v, cs0.v

        @blk.gpsimd
        def _(gp):
            stream_wfull(gp, "wvT")
            for kc in range(8):
                gp.dma_start(out=ar.bf("wfull2", kc * 1024, 1024),
                             in_=bass.AP(ctx["wrT_h"], kc * 128 * E, [[E, 128], [1, E]])
                             ).then_inc(d2, 16)
                cd2.inc(16)

        @blk.tensor
        def _(te):
            te.wait_ge(d2, d2b + 256)
            n = 0
            for tc in range(4):
                for dc in range(8):
                    bank = pb[dc % 4]
                    if dc >= 4 or tc > 0:
                        te.wait_ge(v0, v0b + _l0_vr_vwait(tc, dc))
                    for kc in range(8):
                        mm = te.matmul(bank[:, 0:512], wfull(kc, dc), kact(kc, tc * 512, 512),
                                       start=(kc == 0), stop=(kc == 7))
                    mm.then_inc(p0, 1)
                    n += 1
                for dc in range(8):
                    bank = pb[4 + dc % 3]
                    if tc > 0 or dc >= 3:
                        te.wait_ge(s0, s0b + _l0_r_swait(tc, dc))
                    for kc in range(8):
                        mm = te.matmul(bank[:, 0:512], wfull2(kc, dc), xrv(kc, tc * 512, 512),
                                       start=(kc == 0), stop=(kc == 7))
                    mm.then_inc(p0, 1)
                    n += 1
            cp0.inc(n)

        @blk.vector
        def _(v):
            for tc in range(4):
                for dc in range(8):
                    v.wait_ge(p0, p0b + tc * 16 + dc + 1)
                    v.tensor_copy(out=kvs(dc), in_=pb[dc % 4][:, 0:512]).then_inc(v0, 1)
                for dc in range(8):
                    v.wait_ge(s0, s0b + tc * 8 + dc + 1)
                    v.tensor_tensor(out=sgs(dc), in0=sgs(dc), in1=kvs(dc), op=ALU.mult)
                    v.drain()
                    v.tensor_tensor(out=x1(dc, tc * 512, 512), in0=x1(dc, tc * 512, 512),
                                    in1=sgs(dc), op=ALU.add).then_inc(v0, 1)
            cv0.inc(64)

        @blk.scalar
        def _(sc):
            for tc in range(4):
                for dc in range(8):
                    sc.wait_ge(p0, p0b + tc * 16 + 8 + dc + 1)
                    if tc > 0:
                        sc.wait_ge(v0, v0b + 16 * (tc - 1) + 8 + dc + 1)
                    sc.activation(out=sgs(dc), in_=pb[4 + dc % 3][:, 0:512], func=AF.Sigmoid).then_inc(s0, 1)
            cs0.inc(32)

    ar.release("mixr", "kact", "kvslab", "sgslab", "wfull2")
    ar.alloc("yall", 8 * TOK * 2)
    yall = lambda kc, a, n: ar.bf("yall", kc * TOK + a, n)

    with nc.Block() as blk:
        d2b, d3b, p0b, v0b = cd2.v, cd3.v, cp0.v, cv0.v

        @blk.gpsimd
        def _(gp):
            stream_wfull(gp, "wproj")
            gp.wait_ge(cc, ccc.v)
            rgC = gp.alloc_register()
            gp.reg_load(rgC, ctx["offs"][0:1, 1:2])
            gp.dma_start(out=yall(0, 0, 8 * TOK).rearrange("p (kc t) -> p kc t", t=TOK),
                         in_=bass.AP(ctx["y0_out"], rgC,
                                     [[S, 128], [128 * S, 8], [1, TOK]])
                         ).then_inc(d3, 16)
            cd3.inc(16)

        @blk.tensor
        def _(te):
            te.wait_ge(d2, d2b + 128)
            te.wait_ge(d3, d3b + 16)
            n = 0
            for dc in range(8):
                for tc in range(4):
                    idx = dc * 4 + tc
                    if idx >= 4:
                        te.wait_ge(v0, v0b + idx - 3)
                    bank = pb[idx % 4]
                    for kc in range(8):
                        te.matmul(bank[:, 0:512], wfull(kc, dc), yall(kc, tc * 512, 512),
                                  start=(kc == 0), stop=False)
                    te.matmul(bank[:, 0:512], ctx["biasrows"][0:1, 1536 + dc * 128:1536 + (dc + 1) * 128],
                              ctx["onerb"][0:1, 0:512], start=False, stop=True).then_inc(p0, 1)
                    n += 1
            cp0.inc(n)

        @blk.vector
        def _(v):
            for dc in range(8):
                for tc in range(4):
                    idx = dc * 4 + tc
                    v.wait_ge(p0, p0b + idx + 1)
                    v.tensor_tensor(out=x1(dc, tc * 512, 512), in0=x1(dc, tc * 512, 512),
                                    in1=pb[idx % 4][:, 0:512], op=ALU.add).then_inc(v0, 1)
            cv0.inc(32)

    ar.release("yall", "wfull")
    ar.alloc("xn1", 8 * TOK * 2)
    _emit_ln_fm(ctx, src_f32=lambda f, a, n: ar.f32("x1", f * TOK + a, n),
                dst_bf=lambda f, a, n: ar.bf("xn1", f * TOK + a, n),
                n_tok=TOK)


def _l0_vr_vwait(tc, dc):
    if dc >= 4:
        return 16 * tc + (dc - 4) + 1
    return 16 * (tc - 1) + (dc + 4) + 1 if tc > 0 else 0


def _l0_r_swait(tc, dc):
    if dc >= 3:
        return tc * 8 + dc - 2
    return tc * 8


def _emit_ln_fm(ctx, src_f32, dst_bf, n_tok, gamma_fn=None, beta_fn=None,
                 res_dst=None, x1_acc=None):
    nc = ctx["nc"]
    pb = ctx["pb"]
    p0, cp0 = _sem(ctx, "p0")
    v0, cv0 = _sem(ctx, "v0")
    s0, cs0 = _sem(ctx, "s0")
    ntc = n_tok // 512

    def pe_sx(tc, kc):
        return tc * 16 + kc + 1

    def pe_bc(tc, kc):
        return tc * 16 + 8 + kc + 1

    ar = ctx["ar"]
    ar.alloc("lntmp", 4 * 512 * 4 + 2 * 512 * 2 + 2 * 512 * 4)
    lnr = ar.f32("lntmp", 0, 4 * 512)[0:1, :]
    lnrb = ar.bf("lntmp", 4 * 512 * 2, 2 * 512)[0:1, :]
    sqf = ar.f32("lntmp", 4 * 512 + 512, 2 * 512)
    with (
        nc.Block() as blk,
    ):
        p0b, v0b, s0b = cp0.v, cv0.v, cs0.v

        @blk.scalar
        def _(sc):
            sc.wait_ge(v0, v0b)
            for tc in range(ntc):
                for kc in range(8):
                    idx = tc * 8 + kc
                    if idx >= 2:
                        sc.wait_ge(p0, p0b + pe_sx((idx - 2) // 8, (idx - 2) % 8))
                    sc.activation(out=sqf[:, (kc % 2) * 512:(kc % 2 + 1) * 512],
                                  in_=src_f32(kc, tc * 512, 512), func=AF.Square).then_inc(s0, 1)
                sc.wait_ge(v0, v0b + tc * 10 + 1)
                sc.activation(out=lnr[0:1, 512:1024], in_=lnr[0:1, 1024:1536], func=AF.Sqrt).then_inc(s0, 1)
            cs0.inc(ntc * 9)

        @blk.tensor
        def _(te):
            te.wait_ge(v0, v0b)
            for tc in range(ntc):
                if tc >= 1:
                    te.wait_ge(v0, v0b + tc * 10)
                for kc in range(8):
                    te.matmul(pb[6][0:1, 0:512], ctx["onec"][:], src_f32(kc, tc * 512, 512),
                              start=(kc == 0), stop=(kc == 7))
                    te.wait_ge(s0, s0b + tc * 9 + kc + 1)
                    te.matmul(pb[7][0:1, 0:512], ctx["onec"][:],
                              sqf[:, (kc % 2) * 512:(kc % 2 + 1) * 512],
                              start=(kc == 0), stop=(kc == 7)).then_inc(p0, 1)
                for kc in range(8):
                    te.wait_ge(v0, v0b + tc * 10 + 2 + kc)
                    ga = ctx["onerb"][0:1, 0:128] if gamma_fn is None else gamma_fn(kc)
                    te.matmul(pb[6][:, 0:512], ga, lnrb[0:1, 0:512], start=True, stop=True)
                    if beta_fn is None:
                        te.matmul(pb[7][:, 0:512], ga, lnrb[0:1, 512:1024],
                                  start=True, stop=True).then_inc(p0, 1)
                    else:
                        te.matmul(pb[7][:, 0:512], ga, lnrb[0:1, 512:1024], start=True, stop=False)
                        te.matmul(pb[7][:, 0:512], beta_fn(kc), ctx["onerb"][0:1, 0:512],
                                  start=False, stop=True).then_inc(p0, 1)
            cp0.inc(ntc * 16)

        @blk.vector
        def _(v):
            for tc in range(ntc):
                v.wait_ge(p0, p0b + pe_sx(tc, 7))
                v.tensor_scalar(out=lnr[0:1, 0:512], in0=pb[6][0:1, 0:512],
                                scalar1=1.0 / E, scalar2=None, op0=ALU.mult)
                v.tensor_scalar(out=lnr[0:1, 1024:1536], in0=pb[7][0:1, 0:512],
                                scalar1=1.0 / E, scalar2=None, op0=ALU.mult)
                v.drain()
                v.tensor_tensor(out=lnr[0:1, 1536:2048], in0=lnr[0:1, 0:512],
                                in1=lnr[0:1, 0:512], op=ALU.mult)
                v.drain()
                v.tensor_tensor(out=lnr[0:1, 1024:1536], in0=lnr[0:1, 1024:1536],
                                in1=lnr[0:1, 1536:2048], op=ALU.subtract)
                v.drain()
                v.tensor_scalar(out=lnr[0:1, 1024:1536], in0=lnr[0:1, 1024:1536],
                                scalar1=1e-5, scalar2=None, op0=ALU.add).then_inc(v0, 1)
                v.wait_ge(s0, s0b + tc * 9 + 9)
                v.reciprocal(out=lnr[0:1, 1024:1536], in_=lnr[0:1, 512:1024])
                v.drain()
                v.tensor_tensor(out=lnr[0:1, 1536:2048], in0=lnr[0:1, 0:512],
                                in1=lnr[0:1, 1024:1536], op=ALU.mult)
                v.drain()
                v.tensor_scalar(out=lnr[0:1, 1536:2048], in0=lnr[0:1, 1536:2048],
                                scalar1=-1.0, scalar2=None, op0=ALU.mult)
                v.drain()
                v.tensor_copy(out=lnrb[0:1, 0:512], in_=lnr[0:1, 1024:1536])
                v.tensor_copy(out=lnrb[0:1, 512:1024], in_=lnr[0:1, 1536:2048]).then_inc(v0, 1)
                for kc in range(8):
                    v.wait_ge(p0, p0b + pe_bc(tc, kc))
                    if res_dst is not None:
                        v.tensor_tensor(out=res_dst(kc, tc * 512, 512),
                                        in0=src_f32(kc, tc * 512, 512),
                                        in1=pb[6][:, 0:512], op=ALU.mult)
                        v.drain()
                        if x1_acc is None:
                            v.tensor_tensor(out=res_dst(kc, tc * 512, 512),
                                            in0=res_dst(kc, tc * 512, 512),
                                            in1=pb[7][:, 0:512], op=ALU.add).then_inc(v0, 1)
                        else:
                            v.tensor_tensor(out=res_dst(kc, tc * 512, 512),
                                            in0=res_dst(kc, tc * 512, 512),
                                            in1=pb[7][:, 0:512], op=ALU.add)
                            v.drain()
                            v.tensor_tensor(out=x1_acc(kc, tc * 512, 512),
                                            in0=x1_acc(kc, tc * 512, 512),
                                            in1=res_dst(kc, tc * 512, 512),
                                            op=ALU.add).then_inc(v0, 1)
                    else:
                        v.tensor_tensor(out=dst_bf(kc, tc * 512, 512),
                                        in0=src_f32(kc, tc * 512, 512),
                                        in1=pb[6][:, 0:512], op=ALU.mult)
                        v.drain()
                        v.tensor_tensor(out=dst_bf(kc, tc * 512, 512),
                                        in0=dst_bf(kc, tc * 512, 512),
                                        in1=pb[7][:, 0:512], op=ALU.add).then_inc(v0, 1)
            cv0.inc(ntc * 10)
    ar.release("lntmp")


def ph_l1head(ctx):
    nc = ctx["nc"]
    ar = ctx["ar"]
    pb = ctx["pb"]
    d2, cd2 = _sem(ctx, "d2")
    d3, cd3 = _sem(ctx, "d3")
    p0, cp0 = _sem(ctx, "p0")
    v0, cv0 = _sem(ctx, "v0")
    cc, ccc = _sem(ctx, "cc")
    xn1 = lambda kc, a, n: ar.bf("xn1", kc * TOK + a, n)
    ar.alloc("uown", 16 * 256 * 2)
    uown = lambda tt: ar.bf("uown", tt * 256, 256)

    with nc.Block() as blk:
        d2b, p0b, v0b = cd2.v, cp0.v, cv0.v

        @blk.gpsimd
        def _(gp):
            gp.dma_start(out=ctx["wsa"][:, 0:1024],
                         in_=bass.AP(ctx["wiT_h"], 0, [[256, 128], [128 * 256, 4], [1, 256]])
                         ).then_inc(d2, 16)
            cd2.inc(16)
            gp.dma_start(out=ctx["wsb"][:, 0:1024],
                         in_=bass.AP(ctx["wiT_h"], 4 * 128 * 256, [[256, 128], [128 * 256, 4], [1, 256]])
                         ).then_inc(d2, 16)
            cd2.inc(16)

        @blk.tensor
        def _(te):
            te.wait_ge(d2, d2b + 32)
            n = 0
            for tt in range(16):
                bank = pb[tt % 4]
                if tt >= 4:
                    te.wait_ge(v0, v0b + tt - 3)
                for kc in range(8):
                    wbuf = ctx["wsa"] if kc < 4 else ctx["wsb"]
                    te.matmul(bank[:, 0:256], xn1(kc, tt * 128, 128),
                              wbuf[:, (kc % 4) * 256:(kc % 4 + 1) * 256],
                              start=(kc == 0), stop=False)
                te.matmul(bank[:, 0:256], ctx["onerb"][0:1, 0:128],
                          ctx["biasrows"][0:1, 3584:3840], start=False, stop=True).then_inc(p0, 1)
                n += 1
            cp0.inc(n)

        @blk.vector
        def _(v):
            for tt in range(16):
                v.wait_ge(p0, p0b + tt + 1)
                v.tensor_copy(out=uown(tt), in_=pb[tt % 4][:, 0:256]).then_inc(v0, 1)
            cv0.inc(16)

        @blk.gpsimd
        def _(gp):
            for tt in range(16):
                gp.wait_ge(v0, v0b + tt + 1)
                gp.dma_start(out=ctx["u_in"].ap()[tt * 128:(tt + 1) * 128, :],
                             in_=uown(tt)).then_inc(d3, 16)
                cd3.inc(16)
            gp.wait_ge(d3, cd3.v)
            with allow_ici_all_to_all():
                gp.collective_compute(
                    "AllGather", ALU.bypass,
                    replica_groups=[[0, 1], [2, 3], [4, 5], [6, 7]],
                    ins=[ctx["u_in"].ap().opt()],
                    outs=[ctx["u_out"].ap().opt()],
                ).then_inc(cc, 1)
            ccc.inc()

    ar.release("uown")


def ph_scan(ctx):
    nc = ctx["nc"]
    ar = ctx["ar"]
    pb = ctx["pb"]
    d2, cd2 = _sem(ctx, "d2")
    d3, cd3 = _sem(ctx, "d3")
    p0, cp0 = _sem(ctx, "p0")
    v0, cv0 = _sem(ctx, "v0")
    cc, ccc = _sem(ctx, "cc")
    T = SCAN_T
    ar.alloc("ustrip", 2 * 32 * 128 * 2)
    ar.alloc("yfm", S * 2)
    ustrip = lambda k: ar.bf("ustrip", (k % 2) * 32 * 128, 32 * 128)
    yfm = lambda a, n: ar.bf("yfm", a, n)
    ss = [ctx["ss0"], ctx["ss1"]]
    pv = lambda t: pb[t % 3][:, 0:128]
    py = lambda t: pb[3 + (t // 512) % 2][:, (t % 512):(t % 512) + 1]

    p0b, v0b, d2b = cp0.v, cv0.v, cd2.v
    NSTRIP = (T + 31) // 32
    d3b = cd3.v + 16

    with nc.Block() as blk:
        @blk.gpsimd
        def _(gp):
            gp.wait_ge(cc, ccc.v)
            rgD = gp.alloc_register()
            gp.reg_load(rgD, ctx["offs"][0:1, 2:3])
            gp.dma_start(out=ctx["u_g"].ap(),
                         in_=bass.AP(ctx["u_out"], rgD, [[256, S], [1, 128]])
                         ).then_inc(d3, 16)
            cd3.inc(16)
            gp.wait_ge(d3, cd3.v)
            for k in range(NSTRIP):
                if k >= 2:
                    gp.wait_ge(v0, v0b + 32 * (k - 1))
                gp.dma_start(out=ustrip(k)[0:1, :],
                             in_=ctx["u_g"].ap()[k * 32:(k + 1) * 32, :]
                             ).then_inc(d2 if k % 2 == 0 else d3, 16)
                (cd2 if k % 2 == 0 else cd3).inc(16)

        @blk.tensor
        def _(te):
            te.wait_ge(d2, d2b + 16)
            te.matmul(pv(0), ctx["brow"][:], ustrip(0)[0:1, 0:128], start=True, stop=False)
            te.matmul(pv(1), ctx["brow"][:], ustrip(0)[0:1, 128:256], start=True, stop=False)
            n = 0
            for t in range(T):
                if (t + 2) % 32 == 0 and (t + 2) < T:
                    k = (t + 2) // 32
                    te.wait_ge(d2 if k % 2 == 0 else d3,
                               (d2b if k % 2 == 0 else d3b) + 16 * (k // 2 + 1))
                te.wait_ge(v0, v0b + t)
                te.matmul(pv(t), ctx["adT"][:], ss[(t + 1) % 2][:], start=False, stop=True).then_inc(p0, 1)
                n += 1
                if t >= 1:
                    te.matmul(py(t - 1), ss[(t - 1) % 2][:], ctx["ccol"][:], start=True, stop=True)
                if t + 2 < T:
                    k = (t + 2) // 32
                    te.matmul(pv(t + 2), ctx["brow"][:],
                              ustrip(k)[0:1, ((t + 2) % 32) * 128:((t + 2) % 32 + 1) * 128],
                              start=True, stop=False)
            te.wait_ge(v0, v0b + T)
            te.matmul(py(T - 1), ss[(T - 1) % 2][:], ctx["ccol"][:], start=True, stop=True).then_inc(p0, 1)
            n += 1
            cp0.inc(n)

        @blk.vector
        def _(v):
            for t in range(T):
                v.wait_ge(p0, p0b + t + 1)
                v.tensor_scalar(out=ss[t % 2][:], in0=pv(t), scalar1=-10.0, scalar2=10.0,
                                op0=ALU.max, op1=ALU.min).then_inc(v0, 1)
                if t % 512 == 2 and t > 512:
                    kk = t // 512 - 1
                    v.tensor_copy(out=yfm(kk * 512, 512), in_=pb[3 + kk % 2][:, 0:512])
            v.wait_ge(p0, p0b + T + 1)
            kk = (T - 1) // 512
            w = (T - 1) % 512 + 1
            v.tensor_copy(out=yfm(kk * 512, w), in_=pb[3 + kk % 2][:, 0:w]).then_inc(v0, 1)
            cv0.inc(T + 1)

        @blk.gpsimd
        def _(gp):
            gp.wait_ge(v0, v0b + T + 1)
            gp.dma_start(out=ctx["ys_in"].ap(), in_=yfm(0, S)[:, 0:S]).then_inc(d2, 16)
            cd2.inc(16)
            gp.wait_ge(d2, cd2.v)
            with allow_ici_all_to_all():
                gp.collective_compute(
                    "AllGather", ALU.bypass,
                    replica_groups=[[0, 1], [2, 3], [4, 5], [6, 7]],
                    ins=[ctx["ys_in"].ap().opt()],
                    outs=[ctx["ys_out"].ap().opt()],
                ).then_inc(cc, 1)
            ccc.inc()

    ar.release("ustrip")


def ph_ffn1(ctx):
    nc = ctx["nc"]
    ar = ctx["ar"]
    pb = ctx["pb"]
    d2, cd2 = _sem(ctx, "d2")
    d3, cd3 = _sem(ctx, "d3")
    d4, cd4 = _sem(ctx, "d4")
    d5, cd5 = _sem(ctx, "d5")
    p0, cp0 = _sem(ctx, "p0")
    v0, cv0 = _sem(ctx, "v0")
    s0, cs0 = _sem(ctx, "s0")
    xn1 = lambda kc, a, n: ar.bf("xn1", kc * TOK + a, n)
    x1 = lambda f, a, n: ar.f32("x1", f * TOK + a, n)
    ffn_cast_d5 = ctx["ctrs"]["d5"].v
    ar.alloc("hslab", 24 * 512 * 2)
    ar.alloc("ftmp", 2 * 512 * 4)
    ar.alloc("wgub", 4 * 1024 * 2)
    ar.alloc("wsc", 2 * 24 * 128 * 2)
    hs = lambda hc: ar.bf("hslab", hc * 512, 512)
    wgb = lambda cg: ar.bf("wgub", (cg % 2) * 1024, 1024)
    wub = lambda cg: ar.bf("wgub", 2048 + (cg % 2) * 1024, 1024)
    wdb = lambda dc: ar.bf("wsc", (dc % 2) * 3072, 3072)
    SC2 = 1.0 / (127.0 * 127.0)
    sems4 = [d2, d3, d4, d5]
    ctr4 = {0: cd2, 1: cd3, 2: cd4, 3: cd5}
    wpair = {}; wdl = {}
    for _tc in range(4):
        for _cg in range(24):
            si = _cg % 4
            ctr4[si].inc(32)
            wpair[(_tc, _cg)] = (si, ctr4[si].v)
        for _dc in range(8):
            si = 2 + _dc % 2
            ctr4[si].inc(16)
            wdl[(_tc, _dc)] = (si, ctr4[si].v)

    for tc in range(4):
        with nc.Block() as blk:
            p0b, v0b, s0b = cp0.v, cv0.v, cs0.v
            d2b, d3b, d4b, d5b = cd2.v, cd3.v, cd4.v, cd5.v

            @blk.gpsimd
            def _(gp, tc=tc):
                if tc == 0:
                    gp.wait_ge(d5, ffn_cast_d5)
                for cg in range(24):
                    if cg >= 2:
                        gp.wait_ge(p0, p0b + 2 * cg - 3)
                    elif tc > 0:
                        gp.wait_ge(p0, p0b)
                    si = cg % 4
                    gp.dma_start(out=wgb(cg)[:, 0:1024],
                                 in_=bass.AP(ctx["wg_bf"], cg * 128,
                                             [[HID, 128], [128 * HID, 8], [1, 128]])
                                 ).then_inc(sems4[si], 16)
                    if cg >= 2:
                        gp.wait_ge(p0, p0b + 2 * cg - 2)
                    gp.dma_start(out=wub(cg)[:, 0:1024],
                                 in_=bass.AP(ctx["wu_bf"], cg * 128,
                                             [[HID, 128], [128 * HID, 8], [1, 128]])
                                 ).then_inc(sems4[si], 16)
                for dc in range(8):
                    if dc >= 2:
                        gp.wait_ge(p0, p0b + 48 + dc - 1)
                    else:
                        gp.wait_ge(p0, p0b + 48)
                    gp.dma_start(out=wdb(dc)[:, 0:3072],
                                 in_=bass.AP(ctx["wd_bf"], dc * 128,
                                             [[E, 128], [128 * E, 24], [1, 128]])
                                 ).then_inc(sems4[2 + dc % 2], 16)

            @blk.tensor
            def _(te, tc=tc):
                n = 0
                for cg in range(24):
                    si, cnt = wpair[(tc, cg)]
                    te.wait_ge(sems4[si], cnt)
                    bg = pb[cg % 2]
                    bu = pb[2 + cg % 2]
                    if cg >= 2:
                        te.wait_ge(v0, v0b + cg - 1)
                    for kc in range(8):
                        mm = te.matmul(bg[:, 0:512], wgb(cg)[:, kc * 128:(kc + 1) * 128],
                                       xn1(kc, tc * 512, 512), start=(kc == 0), stop=(kc == 7))
                    mm.then_inc(p0, 1)
                    for kc in range(8):
                        mm = te.matmul(bu[:, 0:512], wub(cg)[:, kc * 128:(kc + 1) * 128],
                                       xn1(kc, tc * 512, 512), start=(kc == 0), stop=(kc == 7))
                    mm.then_inc(p0, 1)
                    n += 2
                for dc in range(8):
                    si, cnt = wdl[(tc, dc)]
                    te.wait_ge(sems4[si], cnt)
                    bank = pb[4 + dc % 2]
                    if dc == 0:
                        te.wait_ge(v0, v0b + 24)
                    if dc >= 2:
                        te.wait_ge(s0, s0b + 23 + dc)
                    for hc in range(24):
                        mm = te.matmul(bank[:, 0:512], wdb(dc)[:, hc * 128:(hc + 1) * 128],
                                       hs(hc), start=(hc == 0), stop=(hc == 23))
                    mm.then_inc(p0, 1)
                    n += 1
                cp0.inc(n)

            @blk.scalar
            def _(sc, tc=tc):
                for cg in range(24):
                    sc.wait_ge(p0, p0b + 2 * cg + 2)
                    sc.activation(out=hs(cg), in_=pb[cg % 2][:, 0:512], func=AF.Silu,
                                  scale=1.0 / 127.0).then_inc(s0, 1)
                for dc in range(8):
                    sc.wait_ge(p0, p0b + 48 + dc + 1)
                    if dc >= 2:
                        sc.wait_ge(v0, v0b + 23 + dc)
                    sc.activation(out=ar.f32("ftmp", (dc % 2) * 512, 512),
                                  in_=pb[4 + dc % 2][:, 0:512], func=AF.Copy, scale=SC2).then_inc(s0, 1)
                cs0.inc(32)

            @blk.vector
            def _(v, tc=tc):
                for cg in range(24):
                    v.wait_ge(s0, s0b + cg + 1)
                    v.tensor_tensor(out=hs(cg), in0=hs(cg), in1=pb[2 + cg % 2][:, 0:512],
                                    op=ALU.mult).then_inc(v0, 1)
                for dc in range(8):
                    v.wait_ge(s0, s0b + 24 + dc + 1)
                    v.tensor_tensor(out=x1(dc, tc * 512, 512), in0=x1(dc, tc * 512, 512),
                                    in1=ar.f32("ftmp", (dc % 2) * 512, 512), op=ALU.add).then_inc(v0, 1)
                cv0.inc(32)

    ar.release("hslab", "ftmp", "wgub", "wsc")


def ph_tail(ctx):
    nc = ctx["nc"]
    ar = ctx["ar"]
    pb = ctx["pb"]
    d2, cd2 = _sem(ctx, "d2")
    d3, cd3 = _sem(ctx, "d3")
    d4, cd4 = _sem(ctx, "d4")
    p0, cp0 = _sem(ctx, "p0")
    v0, cv0 = _sem(ctx, "v0")
    cc, ccc = _sem(ctx, "cc")
    xn1 = lambda kc, a, n: ar.bf("xn1", kc * TOK + a, n)
    x1 = lambda f, a, n: ar.f32("x1", f * TOK + a, n)
    ar.release("yfm")
    ar.alloc("res", 8 * TOK * 4)
    ar.alloc("ysl", 2 * TOK * 2)
    res = lambda f, a, n: ar.f32("res", f * TOK + a, n)
    ysl = lambda kc: ar.bf("ysl", kc * TOK, TOK)

    with nc.Block() as blk:
        d2b, p0b, v0b = cd2.v, cp0.v, cv0.v

        @blk.gpsimd
        def _(gp):
            gp.wait_ge(cc, ccc.v)
            rgE = gp.alloc_register()
            gp.reg_load(rgE, ctx["offs"][0:1, 1:2])
            gp.dma_start(out=ysl(0).rearrange("p t -> p 1 t").broadcast_to([128, 2, TOK])
                         if False else
                         ar.bf("ysl", 0, 2 * TOK).rearrange("p (kc t) -> p kc t", t=TOK),
                         in_=bass.AP(ctx["ys_out"], rgE,
                                     [[S, 128], [128 * S, 2], [1, TOK]])
                         ).then_inc(d2, 16)
            cd2.inc(16)
            gp.dma_start(out=ctx["wsa"][:, 0:1024],
                         in_=bass.AP(ctx["woT_h"], 0, [[E, 128], [1, E]])).then_inc(d2, 16)
            cd2.inc(16)
            gp.dma_start(out=ctx["wsb"][:, 0:1024],
                         in_=bass.AP(ctx["woT_h"], 128 * E, [[E, 128], [1, E]])).then_inc(d2, 16)
            cd2.inc(16)

        @blk.tensor
        def _(te):
            te.wait_ge(d2, d2b + 48)
            n = 0
            for dc in range(8):
                for tcc in range(4):
                    idx = dc * 4 + tcc
                    if idx >= 4:
                        te.wait_ge(v0, v0b + idx - 3)
                    bank = pb[idx % 4]
                    te.matmul(bank[:, 0:512], ctx["wsa"][:, dc * 128:(dc + 1) * 128],
                              ysl(0)[:, tcc * 512:(tcc + 1) * 512], start=True, stop=False)
                    te.matmul(bank[:, 0:512], ctx["wsb"][:, dc * 128:(dc + 1) * 128],
                              ysl(1)[:, tcc * 512:(tcc + 1) * 512], start=False, stop=False)
                    te.matmul(bank[:, 0:512],
                              ctx["biasrows"][0:1, 2560 + dc * 128:2560 + (dc + 1) * 128],
                              ctx["onerb"][0:1, 0:512], start=False, stop=True).then_inc(p0, 1)
                    n += 1
            cp0.inc(n)

        @blk.vector
        def _(v):
            for dc in range(8):
                for tcc in range(4):
                    idx = dc * 4 + tcc
                    v.wait_ge(p0, p0b + idx + 1)
                    v.tensor_tensor(out=res(dc, tcc * 512, 512), in0=pb[idx % 4][:, 0:512],
                                    in1=xn1(dc, tcc * 512, 512), op=ALU.add).then_inc(v0, 1)
            cv0.inc(32)

    ar.release("xn1", "ysl")
    gam = ctx["gbrow"]
    _emit_ln_fm(ctx, src_f32=res, dst_bf=None, n_tok=TOK,
                gamma_fn=lambda kc: gam[0:1, kc * 128:(kc + 1) * 128],
                beta_fn=lambda kc: gam[0:1, 1024 + kc * 128:1024 + (kc + 1) * 128],
                res_dst=lambda f, a, n: ar.f32("res", f * TOK + a, n),
                x1_acc=x1)

    ar.release("res")
    ar.alloc("otile", 2 * E * 4)
    otile = ar.f32("otile", 0, 2 * E)
    with (
        nc.Block() as blk,
    ):
        p0b, v0b, d3b, d4b = cp0.v, cv0.v, cd3.v, cd4.v

        @blk.tensor
        def _(te):
            for tt in range(16):
                for f in range(8):
                    idx = tt * 8 + f
                    if idx >= 4:
                        te.wait_ge(v0, v0b + idx - 3)
                    te.transpose(pb[idx % 4][:, 0:128], x1(f, tt * 128, 128),
                                 ctx["cst"][:, 0:128]).then_inc(p0, 1)
            cp0.inc(128)

        @blk.vector
        def _(v):
            for tt in range(16):
                if tt >= 2:
                    v.wait_ge(d3 if tt % 2 == 0 else d4,
                              (d3b if tt % 2 == 0 else d4b) + 16 * (tt // 2))
                for f in range(8):
                    idx = tt * 8 + f
                    v.wait_ge(p0, p0b + idx + 1)
                    v.tensor_copy(out=otile[:, (tt % 2) * E + f * 128:(tt % 2) * E + (f + 1) * 128],
                                  in_=pb[idx % 4][:, 0:128]).then_inc(v0, 1)
            cv0.inc(128)

        @blk.gpsimd
        def _(gp):
            for tt in range(16):
                gp.wait_ge(v0, v0b + (tt + 1) * 8)
                gp.dma_start(out=ctx["out_x2"][tt * 128:(tt + 1) * 128, :],
                             in_=otile[:, (tt % 2) * E:(tt % 2 + 1) * E]
                             ).then_inc(d3 if tt % 2 == 0 else d4, 16)
                (cd3 if tt % 2 == 0 else cd4).inc(16)
            gp.wait_ge(d3, cd3.v)
            gp.wait_ge(d4, cd4.v)


def ph_debug(ctx):
    nc = ctx["nc"]
    ar = ctx["ar"]
    d2, cd2 = _sem(ctx, "d2")
    with nc.Block() as blk:
        @blk.gpsimd
        def _(gp):
            srcs = {
                "xn0": lambda: ctx["xn0_tm"].ap()[1:4097, :],
                "qf": lambda: ar.bf("qf", 0, 4 * S),
                "kf": lambda: ar.bf("kf", 0, 4 * S),
                "vaug": lambda: ar.bf("vaug", 0, 32 * 520),
                "negm": lambda: ctx["sv"][:, 23:24],
                "st0dbg": lambda: ctx["ss0"][:],
                "svdbg": lambda: ctx["sv"][:, 8:24],
                "srowdbg": lambda: ctx["srow"][0:1, 8:45],
                "xt32dbg": lambda: ctx["arena_t"][:, 38912:43008].bitcast(F32),
                "tmpAdbg": lambda: ctx["arena_t"][:, 34816:38912].bitcast(F32),
                "adT": lambda: ctx["adT"][:],
                "brow": lambda: ctx["brow"][:],
                "y0": lambda: ctx["y0_in"].ap(),
                "x1": lambda: ar.f32("x1", 0, 8 * TOK),
                "xn1": lambda: ar.bf("xn1", 0, 8 * TOK),
                "uown": lambda: ctx["u_in"].ap().rearrange("a b -> (a b)").rearrange(
                    "(p f) -> p f", p=128),
                "yfm": lambda: ar.bf("yfm", 0, S),
                "res": lambda: ar.f32("res", 0, 8 * TOK),
                "xn0e": lambda: ar.bf("xn0e", 0, 8 * EXT),
            }
            for nm, ap in ctx["dbg"].items():
                try:
                    src = srcs[nm]()
                except (KeyError, RuntimeError):
                    continue
                gp.dma_start(out=ap, in_=src).then_inc(d2, 16)
                cd2.inc(16)
            gp.wait_ge(d2, cd2.v)
    if N_PHASES < 12:
        with nc.Block() as blk:
            @blk.gpsimd
            def _(gp):
                gp.dma_start(out=ctx["out_x2"].rearrange("a b -> (a b)").rearrange(
                    "(p f) -> p f", p=128),
                    in_=ctx["arena_t"][:, 0:2 * E * TOK // 128].bitcast(F32)
                    ).then_inc(d2, 16)
                cd2.inc(16)
                gp.wait_ge(d2, cd2.v)


_NC_CACHE = {}


def _host_prepare(inputs):
    x = np.ascontiguousarray(inputs["L0_attn_x"])
    Wa = inputs["L0_attn_W_attn"]; ba = inputs["L0_attn_b_attn"]
    id32 = np.eye(128, dtype=np.float32)
    eye2 = (2 * np.eye(128)).astype(np.float32)
    masks = np.zeros((128, 4 * 512), bnp16)
    for r in range(4):
        k_idx = np.arange(128)[:, None] + r * 128
        q_idx = np.arange(512)[None, :]
        masks[:, r * 512:(r + 1) * 512] = (k_idx <= q_idx).astype(bnp16)
    shared = dict(
        wproj=np.ascontiguousarray(inputs["L0_attn_W_proj"]),
        bproj_row=np.ascontiguousarray(inputs["L0_attn_b_proj"][None, :].astype(bnp16)),
        wrT=np.ascontiguousarray(inputs["L0_ffn_W_r"].T),
        wkT=np.ascontiguousarray(inputs["L0_ffn_W_k"].T),
        wvT=np.ascontiguousarray(inputs["L0_ffn_W_v"].T),
        mur=np.ascontiguousarray(inputs["L0_ffn_mu_r"].reshape(8, 128).T.astype(np.float32)),
        muk=np.ascontiguousarray(inputs["L0_ffn_mu_k"].reshape(8, 128).T.astype(np.float32)),
        wiT=np.ascontiguousarray(inputs["L1_attn_Wi"].T),
        bi_row=np.ascontiguousarray(inputs["L1_attn_bi"][None, :].astype(bnp16)),
        woT=np.ascontiguousarray(inputs["L1_attn_Wo"].T),
        bo_row=np.ascontiguousarray(inputs["L1_attn_bo"][None, :].astype(bnp16)),
        gam_row=np.ascontiguousarray(inputs["L1_attn_gamma"][None, :].astype(bnp16)),
        bet_row=np.ascontiguousarray(inputs["L1_attn_beta"][None, :].astype(bnp16)),
        wgT=np.ascontiguousarray(inputs["L1_ffn_w_gate"].T),
        wuT=np.ascontiguousarray(inputs["L1_ffn_w_up"].T),
        wdT=np.ascontiguousarray(inputs["L1_ffn_w_down"].T),
        a_hip=np.ascontiguousarray(inputs["L1_attn_A_hippo"]),
        id32=id32, eye2=eye2,
        ones_col=np.ones((128, 1), np.float32),
        ones_colb=np.ones((128, 1), bnp16),
        ones_rowb=np.ones((1, 512), bnp16),
        masks=masks,
        hsel=np.concatenate([np.repeat([[1, 0]], 64, 1).astype(np.float32).reshape(1, 128),
                             np.repeat([[0, 1]], 64, 1).astype(np.float32).reshape(1, 128)]),
        ones_rowf=np.ones((1, 128), np.float32),
    )
    maps = []
    for c in range(8):
        b, gg = c // 2, c % 2
        hsl = slice(gg * 512, (gg + 1) * 512)
        m = dict(shared)
        m.update(
            x_b=np.ascontiguousarray(x[b]),
            x_own=np.ascontiguousarray(x[b, gg * TOK:(gg + 1) * TOK]),
            x_bh=np.ascontiguousarray(x[b].astype(bnp16)),
            wattn=np.ascontiguousarray(np.concatenate(
                [Wa[:, hsl], Wa[:, 1024 + gg * 512:1024 + (gg + 1) * 512],
                 Wa[:, 2048 + gg * 512:2048 + (gg + 1) * 512]], axis=1)),
            battn_row=np.ascontiguousarray(np.concatenate(
                [ba[hsl], ba[1024 + gg * 512:1024 + (gg + 1) * 512],
                 ba[2048 + gg * 512:2048 + (gg + 1) * 512]])[None, :].astype(bnp16)),
            bp_g=np.ascontiguousarray(inputs["L1_attn_Bp"][gg]),
            cv_g=np.ascontiguousarray(inputs["L1_attn_Cp"][gg, 0][:, None]),
            ldt_g=np.array([[inputs["L1_attn_log_dt"][gg]]], np.float32),
            sc_g=np.array([[inputs["L1_attn_scale"][gg]]], np.float32),
            off_xel=np.array([[gg * TOK * E]], np.int32),
            off_y0=np.array([[gg * TOK]], np.int32),
            off_uel=np.array([[gg * 128]], np.int32),
        )
        maps.append(m)
    return maps


def kernel(**inputs):
    maps = _host_prepare(inputs)
    key = (N_PHASES, tuple(DBG), SCAN_T, AR_KB)
    if key not in _NC_CACHE:
        _NC_CACHE[key] = build_nc()
    nc = _NC_CACHE[key]
    res = run_bass_kernel_spmd(nc, maps, core_ids=list(range(8)))
    out = np.empty((B, S, E), np.float32)
    for c in range(8):
        b, gg = c // 2, c % 2
        out[b, gg * TOK:(gg + 1) * TOK, :] = res.results[c]["out_x2"]
    globals()["_LAST_RES"] = res
    return out
